# version 09
# baseline (speedup 1.0000x reference)
import os
import numpy as np
import ml_dtypes
from contextlib import ExitStack

import concourse.bass as bass
import concourse.mybir as mybir
from concourse.bass_utils import run_bass_kernel_spmd
from concourse.bass_core_rust import IndirectOffsetOnAxis

F32 = mybir.dt.float32
BF16 = mybir.dt.bfloat16
I32 = mybir.dt.int32
AF = mybir.ActivationFunctionType
ALU = mybir.AluOpType
AX = mybir.AxisListType

T = 2048
D = 1024
CT = 8
TCn = 4
H = 16
TK = 683
TKP = 768
JTn = 6
G4 = 4

UPTO = int(os.environ.get("MK_UPTO", "4"))
FFNSTAGE = int(os.environ.get("MK_FFNSTAGE", "3"))


class Sem:
    def __init__(self, h):
        self.h = h
        self.n = 0

    def inc(self, instr, k=1):
        instr.then_inc(self.h, k)
        self.n += k
        return instr

    def dma(self, instr):
        return self.inc(instr, 16)


def blk_state(jt, tc):
    jmin, jmax = 128 * jt, min(128 * jt + 127, TK - 1)
    tmin, tmax = 512 * tc, 512 * tc + 511
    if 3 * jmin > tmax:
        return "masked"
    if 3 * jmax <= tmin:
        return "full"
    return "partial"


PARTIALS = [(jt, tc) for jt in range(JTn) for tc in range(TCn)
            if blk_state(jt, tc) == "partial"]
NPART = len(PARTIALS)


def build_program(nc, upto=4):
    es = ExitStack()

    d_x = nc.dram_tensor("xT", [D, T], F32, kind="ExternalInput").ap()
    d_out = nc.dram_tensor("outT", [D, T], F32, kind="ExternalOutput").ap()
    d_wq = nc.dram_tensor("wqT", [D, D], F32, kind="ExternalInput").ap()
    d_wk = nc.dram_tensor("wkT", [D, D], F32, kind="ExternalInput").ap()
    d_wv = nc.dram_tensor("wvT", [D, D], F32, kind="ExternalInput").ap()
    d_wo = nc.dram_tensor("woT", [D, D], F32, kind="ExternalInput").ap()
    d_bo = nc.dram_tensor("bo128", [128, CT], F32, kind="ExternalInput").ap()
    d_wconv = nc.dram_tensor("wconvT", [3, D, D], F32, kind="ExternalInput").ap()
    d_wqkvw = nc.dram_tensor("wqkvw", [D, 4 * D], F32, kind="ExternalInput").ap()
    d_wout = nc.dram_tensor("woutS", [D, D], F32, kind="ExternalInput").ap()
    d_gw = nc.dram_tensor("gw128", [128, 8 * CT], F32, kind="ExternalInput").ap()
    d_gb = nc.dram_tensor("gb", [1, 8], F32, kind="ExternalInput").ap()
    d_ew = nc.dram_tensor("ewT", [8 * D, D], F32, kind="ExternalInput").ap()
    d_eb = nc.dram_tensor("ebP", [8 * 128, CT], F32, kind="ExternalInput").ap()
    d_mask = nc.dram_tensor("maskP", [max(1, NPART) * 128, 512], BF16, kind="ExternalInput").ap()
    d_bidx = nc.dram_tensor("bandidx", [128, 32], I32, kind="ExternalInput").ap()

    d_odr = nc.dram_tensor("o_bounce", [D, T], BF16).ap()
    d_stage = nc.dram_tensor("qkvw_stage", [G4, 8 * 128, T], BF16).ap()
    d_cin = nc.dram_tensor("cc_in", [G4 * D, 512], BF16).ap()
    d_cout = nc.dram_tensor("cc_out", [8 * G4 * D, 512], BF16).ap()

    X = es.enter_context(nc.sbuf_tensor("X", [128, CT * T], F32))
    WB = [es.enter_context(nc.sbuf_tensor(f"WB{i}", [128, CT * D], BF16)) for i in range(2)]
    ONESB = es.enter_context(nc.sbuf_tensor("ONESB", [128, 16], BF16))
    ONE128 = es.enter_context(nc.sbuf_tensor("ONE128", [128, 128], BF16))
    ONESR = es.enter_context(nc.sbuf_tensor("ONESR", [1, 128], F32))
    IOTA8 = es.enter_context(nc.sbuf_tensor("IOTA8", [1, 8], F32))
    IOTAP = es.enter_context(nc.sbuf_tensor("IOTAP", [128, 1], F32))
    GW = es.enter_context(nc.sbuf_tensor("GW", [128, 8 * CT], F32))
    GBr = es.enter_context(nc.sbuf_tensor("GBr", [1, 8], F32))
    SCR = es.enter_context(nc.sbuf_tensor("SCR", [1, 64], F32))
    EPS1 = es.enter_context(nc.sbuf_tensor("EPS1", [128, 1], F32))
    SCB = es.enter_context(nc.sbuf_tensor("SCB", [128, 16], F32))
    IDXS = es.enter_context(nc.sbuf_tensor("IDXS", [128, 18], I32))
    BG = es.enter_context(nc.sbuf_tensor("BGg", [128, 2 * CT], F32))
    CIOTA = es.enter_context(nc.sbuf_tensor("CIOTA", [128, 8], F32))
    IXS = es.enter_context(nc.sbuf_tensor("IXS", [128, 16], F32))

    sS = Sem(es.enter_context(nc.semaphore("sS")))
    sG = Sem(es.enter_context(nc.semaphore("sG")))
    sMM = Sem(es.enter_context(nc.semaphore("sMM")))
    sV = Sem(es.enter_context(nc.semaphore("sV")))
    sA = Sem(es.enter_context(nc.semaphore("sA")))
    sP = Sem(es.enter_context(nc.semaphore("sP")))
    sC = Sem(es.enter_context(nc.semaphore("sC")))
    sX = Sem(es.enter_context(nc.semaphore("sX")))
    sXg = Sem(es.enter_context(nc.semaphore("sXg")))
    sR = Sem(es.enter_context(nc.semaphore("sR")))
    sRb = Sem(es.enter_context(nc.semaphore("sRb")))
    ALL_SEMS = [sS, sG, sMM, sV, sA, sP, sC, sX, sXg, sR, sRb]

    block = es.enter_context(nc.Block())

    def vsync(ve, instr):
        sV.inc(instr)
        ve.wait_ge(sV.h, sV.n)

    def asylc(sc, instr):
        sA.inc(instr)
        sc.wait_ge(sA.h, sA.n)

    wb_free = [0, 0]
    wb_loaded = [0, 0]

    def load_weight(slot, dram_ap, cols=D, nct=CT):
        @block.gpsimd
        def _(gp, slot=slot, dram_ap=dram_ap, cols=cols, nct=nct):
            gp.wait_ge(sMM.h, wb_free[slot])
            sG.dma(gp.dma_start(
                out=WB[slot].rearrange("p (ct i) -> p ct i", ct=CT)[:, 0:nct, 0:cols],
                in_=dram_ap.rearrange("(ct p) i -> p ct i", p=128)))
        wb_loaded[slot] = sG.n

    XT = X.rearrange("p (ct t) -> p ct t", ct=CT)
    DXT = d_x.rearrange("(ct p) t -> p ct t", p=128)

    @block.sync
    def _(sy):
        sX.dma(sy.dma_start(out=XT[:, 0:3], in_=DXT[:, 0:3]))
        sS.dma(sy.dma_start(out=GW[:], in_=d_gw[:]))
        sS.dma(sy.dma_start(out=GBr[:], in_=d_gb[:]))

    s_init = sS.n

    @block.scalar
    def _(sc):
        sX.dma(sc.dma_start(out=XT[:, 3:6], in_=DXT[:, 3:6]))

    @block.gpsimd
    def _(gp):
        sXg.dma(gp.dma_start(out=XT[:, 6:8], in_=DXT[:, 6:8]))
        gp.memset(ONESB[:], 1.0)
        gp.memset(ONE128[:], 1.0)
        gp.memset(ONESR[:], 1.0)
        gp.memset(EPS1[:], 1e-5)
        nc.gpsimd.iota(IOTA8[:], pattern=[[1, 8]], base=0, channel_multiplier=0,
                       allow_small_or_imprecise_dtypes=True)
        nc.gpsimd.iota(CIOTA[:], pattern=[[128, 8]], base=0, channel_multiplier=0,
                       allow_small_or_imprecise_dtypes=True)
        sP.inc(nc.gpsimd.iota(IOTAP[:], pattern=[[0, 1]], base=0, channel_multiplier=1,
                              allow_small_or_imprecise_dtypes=True))
    p_init = sP.n

    ln_ctr = [0]

    def emit_ln(XLN):
        k = ln_ctr[0]
        ln_ctr[0] += 1
        with ExitStack() as ph:
            SQ = ph.enter_context(nc.sbuf_tensor(f"lnSQ{k}", [128, CT * T], BF16))
            VAR = SQ[:].bitcast(F32)[:, 0:T]
            MB = SQ[:, 2 * T:3 * T]
            RB = SQ[:, 3 * T:4 * T]
            M2 = RB

            a_sq = []
            v_cp = []

            @block.scalar
            def _(sc):
                for s in ALL_SEMS:
                    if s.n:
                        sc.wait_ge(s.h, s.n)
                for ct in range(CT):
                    sA.inc(nc.scalar.activation(SQ[:, ct * T:(ct + 1) * T],
                                                X[:, ct * T:(ct + 1) * T], AF.Square))
                    a_sq.append(sA.n)
            a_prep = sA.n

            @block.vector
            def _(ve):
                for s in ALL_SEMS:
                    if s.n:
                        ve.wait_ge(s.h, s.n)
                for ct in range(CT):
                    sV.inc(nc.vector.tensor_copy(XLN[:, ct * T:(ct + 1) * T],
                                                 X[:, ct * T:(ct + 1) * T]))
                    v_cp.append(sV.n)
            v_prep = sV.n

            with ExitStack() as pp:
                p_s = [pp.enter_context(nc.psum_tensor(f"lnp{k}_{i}", [128, 512], F32))
                       for i in range(8)]

                @block.tensor
                def _(te):
                    for ct in range(CT):
                        te.wait_ge(sV.h, v_cp[ct])
                        te.wait_ge(sA.h, a_sq[ct])
                        for tcn in range(TCn):
                            sl = slice(ct * T + tcn * 512, ct * T + tcn * 512 + 512)
                            nc.tensor.matmul(p_s[tcn][:], ONE128[:], XLN[:, sl],
                                             start=(ct == 0), stop=(ct == CT - 1))
                            i = nc.tensor.matmul(p_s[4 + tcn][:], ONE128[:], SQ[:, sl],
                                                 start=(ct == 0), stop=(ct == CT - 1))
                    sMM.inc(i)
                mm_sums = sMM.n

                @block.vector
                def _(ve):
                    ve.wait_ge(sMM.h, mm_sums)
                    for tcn in range(TCn):
                        sl = slice(tcn * 512, tcn * 512 + 512)
                        nc.vector.tensor_scalar_mul(MB[:, sl], p_s[tcn][:], 1.0 / D)
                        nc.vector.tensor_scalar_mul(VAR[:, sl], p_s[4 + tcn][:], 1.0 / D)
                    for tcn in range(TCn):
                        sl = slice(tcn * 512, tcn * 512 + 512)
                        nc.vector.tensor_mul(M2[:, sl], MB[:, sl], MB[:, sl])
                    for tcn in range(TCn):
                        sl = slice(tcn * 512, tcn * 512 + 512)
                        i = nc.vector.tensor_sub(VAR[:, sl], VAR[:, sl], M2[:, sl])
                    sV.inc(i)
                v_var = sV.n

            @block.scalar
            def _(sc):
                sc.wait_ge(sV.h, v_var)
                sc.wait_ge(sP.h, p_init)
                asylc(sc, nc.scalar.activation(VAR[:], VAR[:], AF.Ln, bias=EPS1[:]))
                sA.inc(nc.scalar.activation(RB[:], VAR[:], AF.Exp, scale=-0.5))
            a_rstd = sA.n

            norm = []

            @block.vector
            def _(ve):
                ve.wait_ge(sA.h, a_rstd)
                for ct in range(CT):
                    sl = slice(ct * T, (ct + 1) * T)
                    nc.vector.tensor_sub(XLN[:, sl], XLN[:, sl], MB[:])
                    sV.inc(nc.vector.tensor_mul(XLN[:, sl], XLN[:, sl], RB[:]))
                    norm.append(sV.n)
            return norm

    proj_ctr = [0]

    def emit_proj(slot, rhs_fn, o_tiles, epilogue, ep_sem, tensor_wait=None, n_k=CT,
                  epi_batch=None, k_waits=None):
        k = proj_ctr[0]
        proj_ctr[0] += 1
        items = [(ot, tcn) for ot in o_tiles for tcn in range(TCn)]
        with ExitStack() as pp:
            pss = [pp.enter_context(nc.psum_tensor(f"pj{k}_{i}", [128, 512], F32))
                   for i in range(min(8, len(items)))]
            np_ = len(pss)
            epi_at = {}
            pending = []
            for i, (ot, tcn) in enumerate(items):
                ps = pss[i % np_]

                @block.tensor
                def _(te, i=i, ot=ot, tcn=tcn, ps=ps):
                    if i == 0:
                        if k_waits is None:
                            te.wait_ge(sG.h, wb_loaded[slot])
                        if tensor_wait is not None:
                            tensor_wait(te)
                    if i >= np_:
                        te.wait_ge(ep_sem.h, epi_at[i - np_])
                    for c in range(n_k):
                        if i == 0 and k_waits is not None:
                            te.wait_ge(sG.h, k_waits[c])
                        j = nc.tensor.matmul(
                            ps[:],
                            WB[slot][:, c * D + ot * 128: c * D + ot * 128 + 128],
                            rhs_fn(c, tcn),
                            start=(c == 0), stop=(c == n_k - 1))
                    sMM.inc(j)
                mm_i = sMM.n

                if epi_batch is None:
                    epilogue(ot, tcn, ps, mm_i)
                    epi_at[i] = ep_sem.n
                else:
                    pending.append((ot, tcn, ps, mm_i, i))
                    if len(pending) == epi_batch or i == len(items) - 1:
                        base = ep_sem.n
                        epilogue(pending)
                        assert ep_sem.n == base + len(pending)
                        for kk2, (_, _, _, _, i2) in enumerate(pending):
                            epi_at[i2] = base + kk2 + 1
                        pending = []
            wb_free[slot] = sMM.n

    def emit_attn_a(XLN, norm):
        with ExitStack() as ph:
            KT = ph.enter_context(nc.sbuf_tensor("aKT", [128, CT * TKP], BF16))
            VA = ph.enter_context(nc.sbuf_tensor("aVA", [128, JTn * 1040], BF16))
            QHP = [ph.enter_context(nc.sbuf_tensor(f"aQ{i}", [128, T], BF16)) for i in range(2)]
            PTs = [ph.enter_context(nc.sbuf_tensor(f"aPT{i}", [128, JTn * 512], BF16))
                   for i in range(2)]
            RZ2 = [ph.enter_context(nc.sbuf_tensor(f"aRZ{i}", [1, 512], BF16))
                   for i in range(2)]
            RZB = ph.enter_context(nc.sbuf_tensor("aRZB", [64, 512], F32))
            RZROW = RZB[0:1, :]
            OB = [ph.enter_context(nc.sbuf_tensor(f"aOB{i}", [128, T], BF16)) for i in range(2)]
            MASKS = ph.enter_context(nc.sbuf_tensor("aMSK", [128, max(1, NPART) * 512], BF16))
            ph_k = ExitStack()
            KTMP = ph_k.enter_context(nc.sbuf_tensor("aKTMP", [128, CT * TKP], BF16))

            if NPART:
                @block.sync
                def _(sy):
                    sS.dma(sy.dma_start(out=MASKS.rearrange("p (n w) -> p n w", w=512),
                                        in_=d_mask.rearrange("(n p) w -> p n w", p=128)))
            s_mask = sS.n

            @block.gpsimd
            def _(gp, w=[(s, s.n) for s in ALL_SEMS]):
                for s, n in w:
                    if n:
                        gp.wait_ge(s.h, n)
                gp.memset(KTMP[:], 0.0)
                gp.memset(KT[:], 0.0)
                sP.inc(gp.memset(VA[:], 0.0))
            p_zero = sP.n

            ln_done = sV.n
            with ExitStack() as pp:
                pcs = [pp.enter_context(nc.psum_tensor(f"cv{i}", [128, 341], F32))
                       for i in range(8)]
                cv_evac = [0]
                for og in range(2):
                    ots = [og * 4 + i for i in range(4)]
                    for tau in range(3):
                        slot = (og * 3 + tau) % 2
                        load_weight(slot, d_wconv[tau])
                        g_w = sG.n

                        @block.tensor
                        def _(te, og=og, tau=tau, slot=slot, g_w=g_w):
                            te.wait_ge(sG.h, g_w)
                            if og == 0 and tau == 0:
                                te.wait_ge(sP.h, p_zero)
                            if og == 1 and tau == 0:
                                te.wait_ge(sV.h, cv_evac[0])
                            last = None
                            for oi, ot in enumerate(ots):
                                for jc in range(2):
                                    ps = pcs[oi * 2 + jc]
                                    j0 = jc * 341
                                    for c in range(CT):
                                        if og == 0 and tau == 0 and oi == 0 and jc == 0:
                                            te.wait_ge(sV.h, norm[c])
                                        last = nc.tensor.matmul(
                                            ps[:],
                                            WB[slot][:, c * D + ot * 128: c * D + ot * 128 + 128],
                                            XLN[:, c * T + 3 * j0 + tau: c * T + 3 * j0 + tau + 3 * 341: 3],
                                            start=(tau == 0 and c == 0),
                                            stop=(tau == 2 and c == CT - 1))
                            sMM.inc(last)
                        wb_free[slot] = sMM.n
                    mm_cv = sMM.n

                    @block.vector
                    def _(ve, ots=ots, mm_cv=mm_cv):
                        ve.wait_ge(sMM.h, mm_cv)
                        for oi, ot in enumerate(ots):
                            for jc in range(2):
                                nc.vector.tensor_copy(
                                    KTMP[:, ot * TKP + 1 + jc * 341: ot * TKP + 1 + jc * 341 + 341],
                                    pcs[oi * 2 + jc][:])
                            i = nc.vector.tensor_copy(KTMP[:, ot * TKP: ot * TKP + 1],
                                                      XLN[:, ot * T: ot * T + 1])
                        sV.inc(i)
                        cv_evac[0] = sV.n
            v_ktmp = sV.n

            load_weight(0, d_wk)
            g_wk = sG.n
            with ExitStack() as pp:
                pks = [pp.enter_context(nc.psum_tensor(f"kp{i}", [128, 512], F32))
                       for i in range(8)]
                jchunks = [(0, 512), (512, 171)]
                items = [(ot, jci) for ot in range(CT) for jci in range(2)]
                epi_at = {}
                for i, (ot, jci) in enumerate(items):
                    j0, n = jchunks[jci]
                    ps = pks[i % 8]

                    @block.tensor
                    def _(te, i=i, ot=ot, j0=j0, n=n, ps=ps):
                        if i == 0:
                            te.wait_ge(sG.h, g_wk)
                            te.wait_ge(sV.h, v_ktmp)
                        if i >= 8:
                            te.wait_ge(sV.h, epi_at[i - 8])
                        for c in range(CT):
                            j = nc.tensor.matmul(
                                ps[:, 0:n],
                                WB[0][:, c * D + ot * 128: c * D + ot * 128 + 128],
                                KTMP[:, c * TKP + j0: c * TKP + j0 + n],
                                start=(c == 0), stop=(c == CT - 1))
                        sMM.inc(j)
                    mm_i = sMM.n

                    @block.vector
                    def _(ve, ot=ot, j0=j0, n=n, ps=ps, mm_i=mm_i):
                        ve.wait_ge(sMM.h, mm_i)
                        sV.inc(nc.vector.tensor_copy(KT[:, ot * TKP + j0: ot * TKP + j0 + n],
                                                     ps[:, 0:n]))
                    epi_at[i] = sV.n
                wb_free[0] = sMM.n

            load_weight(1, d_wv)
            g_wv = sG.n
            v_kdone = sV.n
            with ExitStack() as pp:
                pvs = [pp.enter_context(nc.psum_tensor(f"vp{i}", [128, 512], F32))
                       for i in range(8)]
                items = [(jt, ic) for jt in range(JTn) for ic in range(2)]
                epi_at = {}
                for i, (jt, ic) in enumerate(items):
                    ps = pvs[i % 8]

                    @block.tensor
                    def _(te, i=i, jt=jt, ic=ic, ps=ps):
                        if i == 0:
                            te.wait_ge(sG.h, g_wv)
                            te.wait_ge(sV.h, v_kdone)
                        if i >= 8:
                            te.wait_ge(sV.h, epi_at[i - 8])
                        for c in range(CT):
                            j = nc.tensor.matmul(
                                ps[:],
                                KTMP[:, c * TKP + jt * 128: c * TKP + jt * 128 + 128],
                                WB[1][:, c * D + ic * 512: c * D + ic * 512 + 512],
                                start=(c == 0), stop=(c == CT - 1))
                        sMM.inc(j)
                    mm_i = sMM.n

                    @block.vector
                    def _(ve, jt=jt, ic=ic, ps=ps, mm_i=mm_i):
                        ve.wait_ge(sMM.h, mm_i)
                        sV.inc(nc.vector.tensor_copy(
                            VA[:, jt * 1040 + (8 * ic) * 65: jt * 1040 + (8 * ic + 8) * 65]
                            .rearrange("p (h c) -> p h c", h=8)[:, :, 0:64],
                            ps[:].rearrange("p (h c) -> p h c", h=8)))
                    epi_at[i] = sV.n
                wb_free[1] = sMM.n

                @block.vector
                def _(ve):
                    sV.inc(nc.vector.tensor_copy(
                        VA.rearrange("p (jt x) -> p jt x", jt=JTn)[:, :, 64:1040:65],
                        ONESB[:, 0:1].to_broadcast([128, JTn, 16])))
            v_va = sV.n

            ph_k.close()
            load_weight(0, d_wq)
            g_wq = sG.n
            with ExitStack() as pp:
                psc = [pp.enter_context(nc.psum_tensor(f"sc{i}", [128, 512], F32))
                       for i in range(4)]
                pso = [pp.enter_context(nc.psum_tensor(f"so{i}", [65, 512], F32))
                       for i in range(2)]
                psz = [pp.enter_context(nc.psum_tensor(f"szz{i}", [64, 512], F32))
                       for i in range(2)]
                sc_use = [0]
                sc_free = {}
                q_epi = {}
                ob_store = {}
                o_free = {}
                z_free = {}
                v_q = {}
                mm_o = {}
                mm_zd = {}
                it_state = {}
                iters = [(hp, hh, tcn) for hp in range(8) for hh in range(2)
                         for tcn in range(TCn)]
                n_it = len(iters)

                def emit_qproj(hp):
                    qb = QHP[hp % 2]
                    for tcn in range(TCn):
                        u = sc_use[0]
                        ps = psc[u % 4]
                        sc_use[0] += 1

                        @block.tensor
                        def _(te, hp=hp, tcn=tcn, ps=ps, u=u):
                            if hp == 0 and tcn == 0:
                                te.wait_ge(sG.h, g_wq)
                                te.wait_ge(sV.h, v_va)
                            if u >= 4:
                                fr = sc_free[u - 4]
                                te.wait_ge(fr[0].h, fr[1])
                            if hp >= 2 and tcn == 0:
                                te.wait_ge(sA.h, q_epi[hp - 2])
                            for c in range(CT):
                                j = nc.tensor.matmul(
                                    ps[:],
                                    WB[0][:, c * D + hp * 128: c * D + hp * 128 + 128],
                                    XLN[:, c * T + tcn * 512: c * T + tcn * 512 + 512],
                                    start=(c == 0), stop=(c == CT - 1))
                            sMM.inc(j)
                        mm_i = sMM.n

                        @block.vector
                        def _(ve, qb=qb, tcn=tcn, ps=ps, mm_i=mm_i):
                            ve.wait_ge(sMM.h, mm_i)
                            sV.inc(nc.vector.tensor_copy(qb[:, tcn * 512:(tcn + 1) * 512], ps[:]))
                        sc_free[u] = (sV, sV.n)
                    v_q[hp] = sV.n

                def emit_lagged(n):
                    po, pz, hp, h, tcn, a_rz_n = it_state[n]

                    @block.tensor
                    def _(te, n=n, pz=pz, a_rz_n=a_rz_n):
                        if n >= 2:
                            fr = z_free[n - 2]
                            te.wait_ge(fr[0].h, fr[1])
                        te.wait_ge(sA.h, a_rz_n)
                        sMM.inc(nc.tensor.matmul(pz[:], ONE128[0:1, 0:64], RZ2[n % 2][:],
                                                 start=True, stop=True))
                    mm_zd[n] = sMM.n

                    @block.vector
                    def _(ve, n=n, po=po, pz=pz, hp=hp, h=h, tcn=tcn):
                        ve.wait_ge(sMM.h, mm_zd[n])
                        if h % 2 == 0 and tcn == 0 and hp >= 2:
                            ve.wait_ge(sS.h, ob_store[hp - 2])
                        nc.vector.tensor_copy(RZB[:], pz[:])
                        sV.inc(nc.vector.tensor_mul(
                            OB[hp % 2][(h % 2) * 64:(h % 2) * 64 + 64, tcn * 512:(tcn + 1) * 512],
                            po[0:64, :], RZB[:]))
                    o_free[n] = (sV, sV.n)
                    z_free[n] = (sV, sV.n)
                    if h % 2 == 1 and tcn == TCn - 1:
                        v_ob = sV.n

                        @block.sync
                        def _(sy, hp=hp, v_ob=v_ob):
                            if hp >= 2:
                                sy.wait_ge(sS.h, ob_store[hp - 2])
                            sy.wait_ge(sV.h, v_ob)
                            sS.dma(sy.dma_start(out=d_odr[hp * 128:(hp + 1) * 128, :],
                                                in_=OB[hp % 2][:]))
                        ob_store[hp] = sS.n

                def emit_sc_z(n):
                    po = it_state[n][0]

                    @block.scalar
                    def _(sc, n=n, po=po):
                        sc.wait_ge(sMM.h, mm_o[n])
                        if n >= 2:
                            sc.wait_ge(sMM.h, mm_zd[n - 2])
                        nc.scalar.activation(RZROW[:], po[64:65, :], AF.Ln)
                        sA.inc(nc.scalar.activation(RZ2[n % 2][:], RZROW[:], AF.Exp,
                                                    scale=-1.0))

                for n, (hp, hh, tcn) in enumerate(iters):
                    if hh == 0 and tcn == 0:
                        emit_qproj(hp)
                    qb = QHP[hp % 2]
                    h = 2 * hp + hh
                    kbase = (h // 2) * TKP
                    krow = (h % 2) * 64
                    PTb = PTs[n % 2]
                    live = [jt for jt in range(JTn) if blk_state(jt, tcn) != "masked"]
                    ntcn = iters[n + 1][2] if n + 1 < n_it else None
                    nlive = (len([jt for jt in range(JTn) if blk_state(jt, ntcn) != "masked"])
                             if ntcn is not None else 0)
                    S_n = sA.n
                    a_exp_at = {jt: S_n + ji + 1 for ji, jt in enumerate(live)}
                    a_last = S_n + len(live)
                    a_rz_n = S_n + len(live) + (1 if n >= 1 else 0) + nlive + 1
                    mm_sc_at = {}
                    psc_map = []
                    for ji, jt in enumerate(live):
                        u = sc_use[0]
                        ps = psc[u % 4]
                        sc_use[0] += 1
                        psc_map.append(ps)

                        @block.tensor
                        def _(te, jt=jt, ps=ps, u=u, vq=v_q[hp], kbase=kbase,
                              krow=krow, qb=qb, tcn=tcn):
                            if u >= 4:
                                fr = sc_free[u - 4]
                                te.wait_ge(fr[0].h, fr[1])
                            te.wait_ge(sV.h, vq)
                            sMM.inc(nc.tensor.matmul(
                                ps[:],
                                KT[krow:krow + 64, kbase + jt * 128: kbase + jt * 128 + 128],
                                qb[krow:krow + 64, tcn * 512:(tcn + 1) * 512],
                                start=True, stop=True))
                        mm_sc_at[jt] = sMM.n
                        sc_free[u] = (sA, a_exp_at[jt])
                    v_mask = sV.n
                    parts = [jt for jt in live if blk_state(jt, tcn) == "partial"]
                    if parts:
                        @block.vector
                        def _(ve, parts=parts, tcn=tcn, a_exp_at=dict(a_exp_at), PTb=PTb):
                            ve.wait_ge(sS.h, s_mask)
                            for jt in parts:
                                pi = PARTIALS.index((jt, tcn))
                                ve.wait_ge(sA.h, a_exp_at[jt])
                                i = nc.vector.tensor_mul(
                                    PTb[:, jt * 512:(jt + 1) * 512],
                                    PTb[:, jt * 512:(jt + 1) * 512],
                                    MASKS[:, pi * 512:(pi + 1) * 512])
                            sV.inc(i)
                        v_mask = sV.n
                    po = pso[n % 2]
                    pz = psz[n % 2]

                    @block.tensor
                    def _(te, n=n, h=h, live=live, po=po, v_mask=v_mask,
                          a_last=a_last, PTb=PTb):
                        if n >= 2:
                            fr = o_free[n - 2]
                            te.wait_ge(fr[0].h, fr[1])
                        te.wait_ge(sA.h, a_last)
                        te.wait_ge(sV.h, v_mask)
                        for ci, jt in enumerate(live):
                            j = nc.tensor.matmul(
                                po[:],
                                VA[:, jt * 1040 + 65 * h: jt * 1040 + 65 * h + 65],
                                PTb[:, jt * 512:(jt + 1) * 512],
                                start=(ci == 0), stop=(ci == len(live) - 1))
                        sMM.inc(j)
                    mm_o[n] = sMM.n
                    it_state[n] = (po, pz, hp, h, tcn, a_rz_n)
                    if n >= 1:
                        emit_lagged(n - 1)

                    @block.scalar
                    def _(sc, n=n, live=live, mm_sc_at=dict(mm_sc_at),
                          psc_map=list(psc_map), PTb=PTb):
                        if n >= 2:
                            sc.wait_ge(sMM.h, mm_o[n - 2])
                        for ji, jt in enumerate(live):
                            sc.wait_ge(sMM.h, mm_sc_at[jt])
                            sA.inc(nc.scalar.activation(
                                PTb[:, jt * 512:(jt + 1) * 512], psc_map[ji][:],
                                AF.Exp, scale=0.125))
                    assert sA.n == a_last, (sA.n, a_last)
                    if n >= 1:
                        emit_sc_z(n - 1)
                        assert sA.n == it_state[n - 1][5], (sA.n, it_state[n - 1][5])
                    if h % 2 == 1 and tcn == TCn - 1:
                        q_epi[hp] = sA.n
                emit_lagged(n_it - 1)
                emit_sc_z(n_it - 1)
                assert sA.n == it_state[n_it - 1][5], (sA.n, it_state[n_it - 1][5])
                wb_free[0] = sMM.n
            s_oall = sS.n

        if True:
            load_weight(1, d_wo)
            with ExitStack() as op:
                OCH = op.enter_context(nc.sbuf_tensor("aOCH", [128, CT * T], BF16))
                BO = op.enter_context(nc.sbuf_tensor("aBO2", [128, CT], F32))

                @block.sync
                def _(sy):
                    sS.dma(sy.dma_start(out=BO[:], in_=d_bo[:]))
                s_bo = sS.n

                @block.sync
                def _(sy):
                    sy.wait_ge(sS.h, s_oall)
                    sS.dma(sy.dma_start(out=OCH.rearrange("p (ct t) -> p ct t", ct=CT),
                                        in_=d_odr.rearrange("(ct p) t -> p ct t", p=128)))
                s_och = sS.n

                def epi(ot, tcn, ps, mm_i):
                    @block.vector
                    def _(ve, ot=ot, tcn=tcn, ps=ps, mm_i=mm_i):
                        ve.wait_ge(sMM.h, mm_i)
                        ve.wait_ge(sS.h, s_bo)
                        sV.inc(nc.vector.scalar_tensor_tensor(
                            X[:, ot * T + tcn * 512: ot * T + tcn * 512 + 512],
                            ps[:], BO[:, ot:ot + 1],
                            X[:, ot * T + tcn * 512: ot * T + tcn * 512 + 512],
                            ALU.add, ALU.add))

                emit_proj(1, lambda c, tcn: OCH[:, c * T + tcn * 512: c * T + tcn * 512 + 512],
                          list(range(CT)), epi, sV,
                          tensor_wait=lambda te: te.wait_ge(sS.h, s_och))

    ffn_ctr = [0]

    def emit_ffn(XLN, out_dma=False):
        k = ffn_ctr[0]
        ffn_ctr[0] += 1
        with ExitStack() as ph:
            POOL = ph.enter_context(nc.sbuf_tensor(f"fPO{k}", [128, CT], F32))
            LOGI = ph.enter_context(nc.sbuf_tensor(f"fLG{k}", [1, 8], F32))
            TOP8 = ph.enter_context(nc.sbuf_tensor(f"fT8{k}", [1, 8], F32))
            TMP8 = ph.enter_context(nc.sbuf_tensor(f"fTM{k}", [1, 24], F32))
            G1B = ph.enter_context(nc.sbuf_tensor(f"fG1{k}", [128, CT * T], BF16))
            G2B = ph.enter_context(nc.sbuf_tensor(f"fG2{k}", [128, 8 * 512], BF16))

            @block.vector
            def _(ve):
                for ct in range(CT):
                    i = nc.vector.tensor_reduce(POOL[:, ct:ct + 1], XLN[:, ct * T:(ct + 1) * T],
                                                AX.X, ALU.add)
                vsync(ve, i)
                sV.inc(nc.vector.tensor_scalar_mul(POOL[:], POOL[:], 1.0 / T))
            v_pool = sV.n

            with ExitStack() as pp:
                pl = pp.enter_context(nc.psum_tensor(f"fpl{k}", [1, 8], F32))
                pb = pp.enter_context(nc.psum_tensor(f"fpb{k}", [128, 8], F32))

                @block.tensor
                def _(te):
                    te.wait_ge(sV.h, v_pool)
                    te.wait_ge(sS.h, s_init)
                    for c in range(CT):
                        j = nc.tensor.matmul(pl[:], POOL[:, c:c + 1], GW[:, c * 8:(c + 1) * 8],
                                             start=(c == 0), stop=(c == CT - 1))
                    sMM.inc(j)
                mm_lg = sMM.n

                @block.vector
                def _(ve):
                    ve.wait_ge(sMM.h, mm_lg)
                    vsync(ve, nc.vector.tensor_add(LOGI[:], pl[:], GBr[:]))
                    vsync(ve, nc.vector.max(out=TOP8[:], in_=LOGI[:]))
                    vsync(ve, nc.vector.tensor_single_scalar(TMP8[:, 0:8], LOGI[:], TOP8[:, 1:2], ALU.is_ge))
                    vsync(ve, nc.vector.tensor_single_scalar(TMP8[:, 8:16], LOGI[:], TOP8[:, 0:1], ALU.is_ge))
                v_keep = sV.n

                @block.scalar
                def _(sc):
                    sc.wait_ge(sV.h, v_keep)
                    sA.inc(nc.scalar.activation(TMP8[:, 16:24], LOGI[:], AF.Exp))
                a_exp = sA.n

                @block.vector
                def _(ve):
                    ve.wait_ge(sA.h, a_exp)
                    vsync(ve, nc.vector.tensor_mul(TMP8[:, 16:24], TMP8[:, 16:24], TMP8[:, 0:8]))
                    vsync(ve, nc.vector.tensor_reduce(SCR[:, 0:1], TMP8[:, 16:24], AX.X, ALU.add))
                    vsync(ve, nc.vector.reciprocal(SCR[:, 1:2], SCR[:, 0:1]))
                    vsync(ve, nc.vector.tensor_single_scalar(TMP8[:, 16:24], TMP8[:, 16:24], SCR[:, 1:2], ALU.mult))
                    vsync(ve, nc.vector.tensor_mul(SCR[:, 8:16], IOTA8[:], TMP8[:, 8:16]))
                    vsync(ve, nc.vector.tensor_reduce(SCR[:, 2:3], SCR[:, 8:16], AX.X, ALU.add))
                    vsync(ve, nc.vector.tensor_mul(SCR[:, 8:16], IOTA8[:], TMP8[:, 0:8]))
                    vsync(ve, nc.vector.tensor_reduce(SCR[:, 3:4], SCR[:, 8:16], AX.X, ALU.add))
                    vsync(ve, nc.vector.tensor_sub(SCR[:, 3:4], SCR[:, 3:4], SCR[:, 2:3]))
                    vsync(ve, nc.vector.tensor_mul(SCR[:, 8:16], TMP8[:, 16:24], TMP8[:, 8:16]))
                    vsync(ve, nc.vector.tensor_reduce(SCR[:, 4:5], SCR[:, 8:16], AX.X, ALU.add))
                    vsync(ve, nc.vector.tensor_scalar(SCR[:, 5:6], SCR[:, 4:5], -1.0, 1.0, ALU.mult, ALU.add))
                    nc.vector.tensor_scalar_mul(SCR[:, 16:17], SCR[:, 2:3], 1024.0)
                    nc.vector.tensor_scalar_mul(SCR[:, 17:18], SCR[:, 3:4], 1024.0)
                    nc.vector.tensor_scalar_mul(SCR[:, 18:19], SCR[:, 2:3], 128.0)
                    nc.vector.tensor_scalar_mul(SCR[:, 19:20], SCR[:, 3:4], 128.0)
                    nc.vector.tensor_copy(SCR[:, 20:21], SCR[:, 4:5])
                    vsync(ve, nc.vector.tensor_copy(SCR[:, 21:22], SCR[:, 5:6]))
                v_gate = sV.n

                @block.tensor
                def _(te):
                    te.wait_ge(sV.h, v_gate)
                    sMM.inc(nc.tensor.matmul(pb[:, 0:6], ONESR[:], SCR[0:1, 16:22],
                                             start=True, stop=True))
                mm_bc = sMM.n

                @block.vector
                def _(ve):
                    ve.wait_ge(sMM.h, mm_bc)
                    ve.wait_ge(sP.h, p_init)
                    vsync(ve, nc.vector.tensor_copy(SCB[:, 0:6], pb[:, 0:6]))
                    nc.vector.tensor_add(SCB[:, 6:7], SCB[:, 0:1], IOTAP[:])
                    nc.vector.tensor_add(SCB[:, 7:8], SCB[:, 1:2], IOTAP[:])
                    nc.vector.tensor_add(SCB[:, 8:9], SCB[:, 2:3], IOTAP[:])
                    vsync(ve, nc.vector.tensor_add(SCB[:, 9:10], SCB[:, 3:4], IOTAP[:]))
                    nc.vector.tensor_tensor(IXS[:, 0:8], SCB[:, 6:7].to_broadcast([128, 8]),
                                            CIOTA[:], ALU.add)
                    vsync(ve, nc.vector.tensor_tensor(IXS[:, 8:16], SCB[:, 7:8].to_broadcast([128, 8]),
                                                      CIOTA[:], ALU.add))
                    vsync(ve, nc.vector.tensor_copy(IDXS[:, 0:16], IXS[:]))
                    nc.vector.tensor_copy(IDXS[:, 16:17], SCB[:, 8:9])
                    sV.inc(nc.vector.tensor_copy(IDXS[:, 17:18], SCB[:, 9:10]))
                v_idx = sV.n

            if FFNSTAGE < 2:
                return
            gmarks = {0: [], 1: []}
            for s_ in range(2):
                @block.gpsimd
                def _(gp, s_=s_):
                    gp.wait_ge(sV.h, v_idx)
                    gp.wait_ge(sMM.h, wb_free[s_])
                    for c in range(CT):
                        sG.dma(nc.gpsimd.indirect_dma_start(
                            out=WB[s_][:, c * D:(c + 1) * D],
                            out_offset=None,
                            in_=d_ew[:],
                            in_offset=IndirectOffsetOnAxis(IDXS[:, s_ * 8 + c:s_ * 8 + c + 1], 0)))
                        gmarks[s_].append(sG.n)
                    sG.dma(nc.gpsimd.indirect_dma_start(
                        out=BG[:, s_ * CT:(s_ + 1) * CT],
                        out_offset=None,
                        in_=d_eb[:],
                        in_offset=IndirectOffsetOnAxis(IDXS[:, 16 + s_:17 + s_], 0)))
                wb_loaded[s_] = sG.n

            if FFNSTAGE < 3:
                return
            def epi1(batch):
                @block.scalar
                def _(sc, batch=list(batch)):
                    sc.wait_ge(sG.h, wb_loaded[0])
                    for (ot, tcn, ps, mm_i, _) in batch:
                        sc.wait_ge(sMM.h, mm_i)
                        sA.inc(nc.scalar.activation(
                            G1B[:, ot * T + tcn * 512: ot * T + tcn * 512 + 512],
                            ps[:], AF.Gelu, bias=BG[:, ot:ot + 1]))

            emit_proj(0, lambda c, tcn: XLN[:, c * T + tcn * 512: c * T + tcn * 512 + 512],
                      list(range(CT)), epi1, sA, epi_batch=8)

            def epi2(batch):
                a_base = sA.n

                @block.scalar
                def _(sc, batch=list(batch), vprev=cmb_done[0]):
                    sc.wait_ge(sG.h, wb_loaded[1])
                    sc.wait_ge(sV.h, vprev)
                    for sl_i, (ot, tcn, ps, mm_i, _) in enumerate(batch):
                        sc.wait_ge(sMM.h, mm_i)
                        sA.inc(nc.scalar.activation(G2B[:, sl_i * 512:(sl_i + 1) * 512],
                                                    ps[:], AF.Gelu,
                                                    bias=BG[:, CT + ot:CT + ot + 1]))

                @block.vector
                def _(ve, batch=list(batch), a_base=a_base):
                    for sl_i, (ot, tcn, ps, mm_i, _) in enumerate(batch):
                        ve.wait_ge(sA.h, a_base + sl_i + 1)
                        sl = slice(ot * T + tcn * 512, ot * T + tcn * 512 + 512)
                        nc.vector.scalar_tensor_tensor(X[:, sl], G1B[:, sl], SCB[:, 4:5],
                                                       X[:, sl], ALU.mult, ALU.add)
                        sV.inc(nc.vector.scalar_tensor_tensor(X[:, sl], G2B[:, sl_i * 512:(sl_i + 1) * 512],
                                                              SCB[:, 5:6],
                                                              X[:, sl], ALU.mult, ALU.add))
                cmb_done[0] = sV.n

                if out_dma:
                    @block.sync
                    def _(sy, batch=list(batch), vmark=cmb_done[0]):
                        sy.wait_ge(sV.h, vmark)
                        for (ot, tcn, ps, mm_i, _) in batch:
                            sS.dma(sy.dma_start(
                                out=d_out.rearrange("(ct p) t -> p ct t", p=128)
                                [:, ot, tcn * 512:(tcn + 1) * 512],
                                in_=X[:, ot * T + tcn * 512: ot * T + tcn * 512 + 512]))

            cmb_done = [sV.n]
            emit_proj(1, lambda c, tcn: XLN[:, c * T + tcn * 512: c * T + tcn * 512 + 512],
                      list(range(CT)), epi2, sV, epi_batch=8)

    def emit_attn_b(XLN, xs, ln_done, Q13, K13):
        s_stage_g = []
        pqz = [0]
        with ExitStack() as ph:
            BNC = [ph.enter_context(nc.sbuf_tensor(f"bBN{i}", [128, 512], BF16)) for i in range(4)]
            bnc_free = {}
            bnc_use = [0]
            for g in range(G4):
                load_weight(g % 2, d_wqkvw[:, g * D:(g + 1) * D])
                if g == 0:
                    @block.gpsimd
                    def _(gp, w=[(s, s.n) for s in ALL_SEMS]):
                        for s, n in w:
                            if n:
                                gp.wait_ge(s.h, n)
                        gp.memset(Q13[:], 0.0)
                        sP.inc(gp.memset(K13[:], 0.0))
                    pqz[0] = sP.n

                def epi(ot, tcn, ps, mm_i, g=g):
                    u = bnc_use[0]
                    bnc_use[0] += 1
                    bn = BNC[u % 4]

                    @block.vector
                    def _(ve, ot=ot, tcn=tcn, ps=ps, mm_i=mm_i, bn=bn, u=u):
                        ve.wait_ge(sMM.h, mm_i)
                        if u >= 4:
                            ve.wait_ge(sS.h, bnc_free[u - 4])
                        sV.inc(nc.vector.tensor_copy(bn[:], ps[:]))
                    v_i = sV.n

                    @block.sync
                    def _(sy, ot=ot, tcn=tcn, bn=bn, v_i=v_i, g=g):
                        sy.wait_ge(sV.h, v_i)
                        sS.dma(sy.dma_start(
                            out=d_stage[g, ot * 128:(ot + 1) * 128, tcn * 512:(tcn + 1) * 512],
                            in_=bn[:]))
                    bnc_free[bnc_use[0] - 1] = sS.n

                def twait(te, g=g):
                    te.wait_ge(sV.h, ln_done)
                    if g == 0:
                        te.wait_ge(sA.h, sA.n)
                emit_proj(g % 2, lambda c, tcn: XLN[:, c * T + tcn * 512: c * T + tcn * 512 + 512],
                          list(range(CT)), epi, sV, tensor_wait=twait)
                s_stage_g.append(sS.n)
            s_stage_all = sS.n
        xs.close()
        b1_end = sMM.n
        b1_v_end = sV.n
        p_qk_zero = pqz[0]

        load_weight(0, d_wout)

        with ExitStack() as ph:
            V17 = ph.enter_context(nc.sbuf_tensor("bV17", [128, 8192], BF16))
            W18 = ph.enter_context(nc.sbuf_tensor("bW18", [128, 8192], BF16))
            ESB = ph.enter_context(nc.sbuf_tensor("bESB", [128, 2048], BF16))
            ZS = ph.enter_context(nc.sbuf_tensor("bZS", [128, 128], F32))
            RZS = ph.enter_context(nc.sbuf_tensor("bRZS", [128, 128], F32))
            PACK = ph.enter_context(nc.sbuf_tensor("bPACK", [128, 4096], BF16))
            PTB = ph.enter_context(nc.sbuf_tensor("bPTB", [128, 4096], BF16))
            ZG = ph.enter_context(nc.sbuf_tensor("bZG", [128, 8192], BF16))
            PCS = [ph.enter_context(nc.sbuf_tensor(f"bPC{i}", [128, 2048], BF16))
                   for i in range(2)]

            st4s = [d_stage[g].rearrange("(cc hd) (h p) -> cc hd h p", hd=16, p=128)
                    for g in range(G4)]
            r_qk = {}
            r_vw = {}
            mm_p1_done = {}
            a_exp_last = {}
            rdv_last = {}
            ev_gp_last = {}
            zdma_at = []
            g_mm_done = {}
            g_evac_done = {}
            pvz = [0]

            def emit_wave_a(g):
                @block.scalar
                def _(sc, g=g):
                    sc.wait_ge(sP.h, p_qk_zero)
                    sc.wait_ge(sS.h, s_stage_g[g])
                    if g >= 1:
                        sc.wait_ge(sMM.h, mm_p1_done[g - 1])
                    st4 = st4s[g]
                    for j in range(4):
                        sR.dma(sc.dma_start(
                            out=Q13[32 * j:32 * j + 16, :].rearrange("hd (m h p) -> hd m h p", m=4, h=16),
                            in_=st4[j:16:4].transpose([1, 0, 2, 3])))
                        sR.dma(sc.dma_start(
                            out=K13[32 * j:32 * j + 16, :].rearrange("hd (m h p) -> hd m h p", m=4, h=16),
                            in_=st4[16 + j:32:4].transpose([1, 0, 2, 3])))
                r_qk[g] = sR.n

            def emit_wave_b(g):
                @block.gpsimd
                def _(gp, g=g):
                    gp.wait_ge(sP.h, pvz[0])
                    gp.wait_ge(sS.h, s_stage_g[g])
                    if g >= 1:
                        gp.wait_ge(sMM.h, g_mm_done[g - 1])
                        gp.wait_ge(sP.h, g_evac_done[g - 1])
                    st4 = st4s[g]
                    for j in range(4):
                        sRb.dma(gp.dma_start(
                            out=W18[32 * j:32 * j + 16, :].rearrange("hd (m h p) -> hd m h p", m=4, h=16),
                            in_=st4[48 + j:64:4].transpose([1, 0, 2, 3])))
                    for m in range(4):
                        sRb.dma(gp.dma_start(
                            out=V17[32 * m:32 * m + 16, :].rearrange("h (j hd p) -> h j hd p", j=4, hd=16),
                            in_=st4[32 + 4 * m:32 + 4 * m + 4].transpose([2, 0, 1, 3])))
                r_vw[g] = sRb.n

            emit_wave_a(0)

            @block.gpsimd
            def _(gp, w=[(s, s.n) for s in ALL_SEMS]):
                for s, n in w:
                    if n:
                        gp.wait_ge(s.h, n)
                gp.memset(V17[:], 0.0)
                gp.memset(W18[:], 0.0)
                sP.inc(gp.memset(PACK[:], 0.0))
            pvz[0] = sP.n
            p_vw_zero = pvz[0]

            emit_wave_b(0)

            for g in range(G4):

                with ExitStack() as pp:
                    P1 = [pp.enter_context(nc.psum_tensor(f"bP1{g}_{j}", [128, 512], F32))
                          for j in range(4)]
                    P2 = [pp.enter_context(nc.psum_tensor(f"bP2{g}_{m}", [128, 512], F32))
                          for m in range(4)]
                    r_done_v = {}
                    cp_at = {}
                    ev_gp = {}
                    r_exp = {}
                    r_ev = {}
                    mm_p2_at = {}
                    for r in range(4):
                        @block.tensor
                        def _(te, g=g, r=r):
                            if r == 0:
                                te.wait_ge(sR.h, r_qk[g])
                                if g == 0:
                                    te.wait_ge(sMM.h, b1_end)
                                    te.wait_ge(sV.h, b1_v_end)
                                else:
                                    te.wait_ge(sA.h, a_exp_last[g - 1])
                            if r >= 1:
                                te.wait_ge(sA.h, r_exp[r - 1])
                            for beta in range(32):
                                p = 32 * r + beta
                                for cc in range(16):
                                    j, m = cc % 4, cc // 4
                                    tp = (32 * j, 32 * m) if (j == 3 or m == 3) else None
                                    i = nc.tensor.matmul(
                                        P1[j][32 * m:32 * m + 16, 16 * beta:16 * beta + 16],
                                        Q13[32 * j:32 * j + 32,
                                            m * 2048 + p: m * 2048 + p + 15 * 128 + 1: 128],
                                        K13[32 * j:32 * j + 32,
                                            m * 2048 + p: m * 2048 + p + 15 * 128 + 1: 128],
                                        start=True, stop=True, tile_position=tp,
                                        skip_group_check=True)
                            sMM.inc(i)
                        mm_p1 = sMM.n
                        if r == 3:
                            mm_p1_done[g] = sMM.n

                        @block.scalar
                        def _(sc, g=g, r=r, mm_p1=mm_p1):
                            sc.wait_ge(sMM.h, mm_p1)
                            if r >= 1:
                                sc.wait_ge(sV.h, r_done_v[r - 1])
                            if r == 0 and g >= 1:
                                sc.wait_ge(sV.h, rdv_last[g - 1])
                            if r == 0 and g == 0:
                                sc.wait_ge(sS.h, s_stage_all)
                            for j in range(4):
                                i = nc.scalar.activation(ESB[:, j * 512:(j + 1) * 512],
                                                         P1[j][:], AF.Exp, scale=0.25)
                            sA.inc(i)
                        r_exp[r] = sA.n

                        @block.vector
                        def _(ve, g=g, r=r, a_e=sA.n):
                            ve.wait_ge(sA.h, a_e)
                            if r >= 1:
                                ve.wait_ge(sMM.h, mm_p2_at[r - 1])
                            if r == 0 and g >= 1:
                                ve.wait_ge(sMM.h, g_mm_done[g - 1])
                            if r == 0 and g == 0:
                                ve.wait_ge(sP.h, p_vw_zero)
                                ve.wait_ge(sS.h, s_stage_all)
                            for j in range(4):
                                i = nc.vector.tensor_reduce(
                                    ZS[:, j * 32:(j + 1) * 32],
                                    ESB[:, j * 512:(j + 1) * 512].rearrange("q (b h) -> q b h", h=16),
                                    AX.X, ALU.add)
                            vsync(ve, i)
                            vsync(ve, nc.vector.reciprocal(RZS[:], ZS[:]))
                            for j in range(4):
                                i = nc.vector.tensor_tensor(
                                    PACK.rearrange("q (bq bl x) -> q bq bl x", bq=8, bl=4)[:, :, :, 32 * j:32 * j + 16],
                                    ESB[:, j * 512:(j + 1) * 512].rearrange("q (bq bl h) -> q bq bl h", bq=8, bl=4),
                                    RZS[:, j * 32:(j + 1) * 32].rearrange("q (bq bl) -> q bq bl", bq=8)
                                    .unsqueeze(3).to_broadcast([128, 8, 4, 16]),
                                    ALU.mult)
                            vsync(ve, i)
                            for pk in range(8):
                                i = nc.vector.transpose(PTB[:, pk * 512:(pk + 1) * 512],
                                                        PACK[:, pk * 512:(pk + 1) * 512])
                            sV.inc(i)
                        r_done_v[r] = sV.n
                        v_pt = sV.n

                        @block.tensor
                        def _(te, g=g, r=r, v_pt=v_pt):
                            te.wait_ge(sV.h, v_pt)
                            if r == 0:
                                te.wait_ge(sRb.h, r_vw[g])
                            if r >= 1:
                                te.wait_ge(sA.h, cp_at[r - 1])
                            for beta in range(32):
                                p = 32 * r + beta
                                pk, bl = beta // 4, beta % 4
                                for cc in range(16):
                                    j, m = cc % 4, cc // 4
                                    tp = (32 * m, 32 * j) if (j == 3 or m == 3) else None
                                    i = nc.tensor.matmul(
                                        P2[m][32 * j:32 * j + 16, 16 * beta:16 * beta + 16],
                                        V17[32 * m:32 * m + 32,
                                            j * 2048 + p: j * 2048 + p + 15 * 128 + 1: 128],
                                        PTB[32 * m:32 * m + 32,
                                            pk * 512 + 128 * bl + 32 * j: pk * 512 + 128 * bl + 32 * j + 16],
                                        start=True, stop=True, tile_position=tp,
                                        skip_group_check=True)
                            sMM.inc(i)
                        mm_p2_at[r] = sMM.n
                        mm_p2 = sMM.n

                        @block.scalar
                        def _(sc, g=g, r=r, mm_p2=mm_p2):
                            sc.wait_ge(sMM.h, mm_p2)
                            if r >= 2:
                                sc.wait_ge(sP.h, ev_gp[r - 2])
                            if r < 2 and g >= 1:
                                sc.wait_ge(sP.h, ev_gp_last[g - 1])
                            for m in range(4):
                                i = nc.scalar.activation(PCS[r % 2][:, m * 512:(m + 1) * 512],
                                                         P2[m][:], AF.Copy)
                            sA.inc(i)
                        cp_at[r] = sA.n

                        @block.gpsimd
                        def _(gp, g=g, r=r, cpa=sA.n):
                            gp.wait_ge(sA.h, cpa)
                            if r == 0 and g >= 1:
                                gp.wait_ge(sS.h, zdma_at[g - 1])
                            for m in range(4):
                                outv = (ZG.rearrange("q (bl sm h w) -> q bl sm h w",
                                                     bl=4, sm=4, h=16)[:, :, m, :, 8 * r:8 * r + 8]
                                        .rearrange("q bl h bq -> q bq bl h"))
                                in0 = PCS[r % 2][:, m * 512:(m + 1) * 512].rearrange(
                                    "q (bq bl h) -> q bq bl h", bq=8, bl=4)
                                in1 = (W18.rearrange("q (mm h p) -> q mm h p", mm=4, h=16)
                                       [:, m, :, 32 * r:32 * r + 32]
                                       .rearrange("q h (bq bl) -> q bq bl h", bq=8))
                                i = nc.gpsimd.tensor_tensor(outv, in0, in1, ALU.mult)
                            sP.inc(i)
                        ev_gp[r] = sP.n
                    g_mm_done[g] = sMM.n
                    g_evac_done[g] = sP.n
                    a_exp_last[g] = r_exp[3]
                    rdv_last[g] = r_done_v[3]
                    ev_gp_last[g] = ev_gp[3]
                if g + 1 < G4:
                    emit_wave_a(g + 1)
                    emit_wave_b(g + 1)
                p_ev = sP.n
                zhalf = []

                @block.sync
                def _(sy, g=g, p_ev=p_ev):
                    sy.wait_ge(sP.h, p_ev)
                    for sl in range(16):
                        for j in range(4):
                            sS.dma(sy.dma_start(
                                out=d_cin[g * D + sl * 64 + j * 16: g * D + sl * 64 + j * 16 + 16, :],
                                in_=ZG[32 * j:32 * j + 16, sl * 512:(sl + 1) * 512]))
                        if sl == 7:
                            zhalf.append(sS.n)
                zdma_at.append(sS.n)

                for hf in range(2):
                    @block.gpsimd
                    def _(gp, g=g, hf=hf, zat=(zhalf[0] if hf == 0 else zdma_at[g])):
                        gp.wait_ge(sS.h, zat)
                        sC.inc(nc.gpsimd.collective_compute(
                            "AllGather", ALU.bypass,
                            replica_groups=[list(range(8))],
                            ins=[d_cin[g * D + hf * 512: g * D + hf * 512 + 512, :].opt()],
                            outs=[d_cout[g * 8 * D + hf * 4096: g * 8 * D + hf * 4096 + 4096, :].opt()]))
        c_done = sC.n

        with ExitStack() as ph:
            BIX = ph.enter_context(nc.sbuf_tensor("bBIX", [128, 32], I32))
            BNDs = [ph.enter_context(nc.sbuf_tensor(f"bBND{i}", [128, 8 * 512], BF16))
                    for i in range(2)]

            @block.sync
            def _(sy):
                sS.dma(sy.dma_start(out=BIX[:], in_=d_bidx[:]))
            s_bix = sS.n

            band_loaded = {}
            band_mm = {}
            wo_epi = {}
            with ExitStack() as pp:
                pws = [pp.enter_context(nc.psum_tensor(f"bwo{i}", [128, 512], F32))
                       for i in range(8)]
                for jj in range(4):
                    @block.gpsimd
                    def _(gp, jj=jj):
                        gp.wait_ge(sC.h, c_done)
                        gp.wait_ge(sS.h, s_bix)
                        if jj >= 2:
                            gp.wait_ge(sMM.h, band_mm[jj - 2])
                        for kk in range(8):
                            sG.dma(nc.gpsimd.indirect_dma_start(
                                out=BNDs[jj % 2][:, kk * 512:(kk + 1) * 512],
                                out_offset=None,
                                in_=d_cout[:],
                                in_offset=IndirectOffsetOnAxis(BIX[:, jj * 8 + kk:jj * 8 + kk + 1], 0)))
                    band_loaded[jj] = sG.n

                    for dt in range(CT):
                        ps = pws[dt]

                        @block.tensor
                        def _(te, jj=jj, dt=dt, ps=ps):
                            if dt == 0:
                                te.wait_ge(sG.h, band_loaded[jj])
                                te.wait_ge(sG.h, wb_loaded[0])
                            if jj >= 1 and dt == 0:
                                te.wait_ge(sV.h, wo_epi[jj - 1])
                            for kk in range(8):
                                i = nc.tensor.matmul(
                                    ps[:],
                                    WB[0][:, kk * D + dt * 128: kk * D + dt * 128 + 128],
                                    BNDs[jj % 2][:, kk * 512:(kk + 1) * 512],
                                    start=(kk == 0), stop=(kk == 7))
                            sMM.inc(i)
                        mm_i = sMM.n

                        @block.vector
                        def _(ve, jj=jj, dt=dt, ps=ps, mm_i=mm_i):
                            ve.wait_ge(sMM.h, mm_i)
                            sl = slice(dt * T + jj * 512, dt * T + jj * 512 + 512)
                            sV.inc(nc.vector.tensor_tensor(X[:, sl], X[:, sl], ps[:], ALU.add))
                    wo_epi[jj] = sV.n
                    band_mm[jj] = sMM.n
            wb_free[0] = sMM.n

    with nc.sbuf_tensor("XLN1", [128, CT * T], BF16) as XLN:
        norm1 = emit_ln(XLN)
        emit_attn_a(XLN, norm1)
    if upto >= 2:
        with nc.sbuf_tensor("XLN2", [128, CT * T], BF16) as XLN:
            emit_ln(XLN)
            emit_ffn(XLN)
    if upto >= 3:
        qk = ExitStack()
        Q13 = qk.enter_context(nc.sbuf_tensor("bQ13", [128, 8192], BF16))
        K13 = qk.enter_context(nc.sbuf_tensor("bK13", [128, 8192], BF16))
        xs = ExitStack()
        XLN = xs.enter_context(nc.sbuf_tensor("XLN3", [128, CT * T], BF16))
        emit_ln(XLN)
        emit_attn_b(XLN, xs, sV.n, Q13, K13)
        qk.close()
    if upto >= 4:
        with nc.sbuf_tensor("XLN4", [128, CT * T], BF16) as XLN:
            emit_ln(XLN)
            emit_ffn(XLN, out_dma=True)

    @block.sync
    def _(sy):
        for s in ALL_SEMS:
            if s.n:
                sy.wait_ge(s.h, s.n)
        if upto < 4:
            sS.dma(sy.dma_start(out=d_out.rearrange("(ct p) t -> p ct t", p=128),
                                in_=X.rearrange("p (ct t) -> p ct t", ct=CT)))
        sy.wait_ge(sS.h, sS.n)

    es.close()
    return nc


def _mask_tiles():
    tiles = np.zeros((max(1, NPART), 128, 512), dtype=np.float32)
    for i, (jt, tc) in enumerate(PARTIALS):
        j = 128 * jt + np.arange(128)[:, None]
        t = 512 * tc + np.arange(512)[None, :]
        tiles[i] = (3.0 * j <= t).astype(np.float32)
    return tiles.reshape(-1, 512).astype(ml_dtypes.bfloat16)


def _wout_perm():
    perm = np.zeros(D, dtype=np.int64)
    for pm4 in range(4):
        for m in range(4):
            for j in range(4):
                for hd in range(16):
                    row = (4 * pm4 + m) * 64 + j * 16 + hd
                    cc = 4 * m + j
                    perm[row] = 256 * pm4 + 16 * cc + hd
    return perm


def _band_idx(core):
    idx = np.zeros((32, 128), dtype=np.int32)
    for jj in range(4):
        s_ = 4 * (core % 2) + jj
        g = core // 2
        for kk in range(8):
            row = kk * 128 + np.arange(128)
            idx[jj * 8 + kk] = g * (8 * D) + (row // 512) * 4096 + s_ * 512 + row % 512
    return np.ascontiguousarray(idx.T)


def kernel(L0_attn_a_x, L0_attn_a_Wq, L0_attn_a_Wk, L0_attn_a_Wv, L0_attn_a_Wo,
           L0_attn_a_bo, L0_attn_a_Wconv, L0_attn_b_w_qkvw, L0_attn_b_w_out,
           L0_ffn_gate_w, L0_ffn_gate_b, L0_ffn_expert_w, L0_ffn_expert_b):
    f32 = np.float32
    x = np.asarray(L0_attn_a_x, f32)
    com = {
        "wqT": np.ascontiguousarray(np.asarray(L0_attn_a_Wq, f32).T),
        "wkT": np.ascontiguousarray(np.asarray(L0_attn_a_Wk, f32).T),
        "wvT": np.ascontiguousarray(np.asarray(L0_attn_a_Wv, f32).T),
        "woT": np.ascontiguousarray(np.asarray(L0_attn_a_Wo, f32).T),
        "bo128": np.ascontiguousarray(np.asarray(L0_attn_a_bo, f32).reshape(CT, 128).T),
        "wconvT": np.ascontiguousarray(np.asarray(L0_attn_a_Wconv, f32).transpose(2, 1, 0)),
        "wqkvw": np.ascontiguousarray(np.asarray(L0_attn_b_w_qkvw, f32)),
        "woutS": np.ascontiguousarray(np.asarray(L0_attn_b_w_out, f32)[_wout_perm()]),
        "gw128": np.ascontiguousarray(
            np.asarray(L0_ffn_gate_w, f32).T.reshape(CT, 128, 8).transpose(1, 0, 2).reshape(128, 8 * CT)),
        "gb": np.asarray(L0_ffn_gate_b, f32).reshape(1, 8),
        "ewT": np.ascontiguousarray(
            np.asarray(L0_ffn_expert_w, f32).transpose(0, 2, 1).reshape(8 * D, D)),
        "ebP": np.ascontiguousarray(
            np.asarray(L0_ffn_expert_b, f32).reshape(8, CT, 128).transpose(0, 2, 1).reshape(8 * 128, CT)),
        "maskP": _mask_tiles(),
    }
    nc = bass.Bass("TRN2", target_bir_lowering=False, debug=False)
    build_program(nc, upto=UPTO)
    in_maps = []
    for c in range(8):
        m = dict(com)
        m["xT"] = np.ascontiguousarray(x[c].T)
        m["bandidx"] = _band_idx(c)
        in_maps.append(m)
    res = run_bass_kernel_spmd(nc, in_maps, core_ids=list(range(8)))
    out = np.stack([r["outT"] for r in res.results])
    return np.ascontiguousarray(out.transpose(0, 2, 1))


# version 10
# speedup vs baseline: 1.0095x; 1.0095x over previous
import os
import numpy as np
import ml_dtypes
from contextlib import ExitStack

import concourse.bass as bass
import concourse.mybir as mybir
from concourse.bass_utils import run_bass_kernel_spmd
from concourse.bass_core_rust import IndirectOffsetOnAxis

F32 = mybir.dt.float32
BF16 = mybir.dt.bfloat16
I32 = mybir.dt.int32
AF = mybir.ActivationFunctionType
ALU = mybir.AluOpType
AX = mybir.AxisListType

T = 2048
D = 1024
CT = 8
TCn = 4
H = 16
TK = 683
TKP = 768
JTn = 6
G4 = 4

UPTO = int(os.environ.get("MK_UPTO", "4"))
FFNSTAGE = int(os.environ.get("MK_FFNSTAGE", "3"))


class Sem:
    def __init__(self, h):
        self.h = h
        self.n = 0

    def inc(self, instr, k=1):
        instr.then_inc(self.h, k)
        self.n += k
        return instr

    def dma(self, instr):
        return self.inc(instr, 16)


def blk_state(jt, tc):
    jmin, jmax = 128 * jt, min(128 * jt + 127, TK - 1)
    tmin, tmax = 512 * tc, 512 * tc + 511
    if 3 * jmin > tmax:
        return "masked"
    if 3 * jmax <= tmin:
        return "full"
    return "partial"


PARTIALS = [(jt, tc) for jt in range(JTn) for tc in range(TCn)
            if blk_state(jt, tc) == "partial"]
NPART = len(PARTIALS)


def build_program(nc, upto=4):
    es = ExitStack()

    d_x = nc.dram_tensor("xT", [D, T], F32, kind="ExternalInput").ap()
    d_out = nc.dram_tensor("outT", [D, T], F32, kind="ExternalOutput").ap()
    d_wq = nc.dram_tensor("wqT", [D, D], F32, kind="ExternalInput").ap()
    d_wk = nc.dram_tensor("wkT", [D, D], F32, kind="ExternalInput").ap()
    d_wv = nc.dram_tensor("wvT", [D, D], F32, kind="ExternalInput").ap()
    d_wo = nc.dram_tensor("woT", [D, D], F32, kind="ExternalInput").ap()
    d_bo = nc.dram_tensor("bo128", [128, CT], F32, kind="ExternalInput").ap()
    d_wconv = nc.dram_tensor("wconvT", [3, D, D], F32, kind="ExternalInput").ap()
    d_wqkvw = nc.dram_tensor("wqkvw", [D, 4 * D], F32, kind="ExternalInput").ap()
    d_wout = nc.dram_tensor("woutS", [D, D], F32, kind="ExternalInput").ap()
    d_gw = nc.dram_tensor("gw128", [128, 8 * CT], F32, kind="ExternalInput").ap()
    d_gb = nc.dram_tensor("gb", [1, 8], F32, kind="ExternalInput").ap()
    d_ew = nc.dram_tensor("ewT", [8 * D, D], F32, kind="ExternalInput").ap()
    d_eb = nc.dram_tensor("ebP", [8 * 128, CT], F32, kind="ExternalInput").ap()
    d_mask = nc.dram_tensor("maskP", [max(1, NPART) * 128, 512], BF16, kind="ExternalInput").ap()
    d_bidx = nc.dram_tensor("bandidx", [128, 32], I32, kind="ExternalInput").ap()

    d_odr = nc.dram_tensor("o_bounce", [D, T], BF16).ap()
    d_stage = nc.dram_tensor("qkvw_stage", [G4, 8 * 128, T], BF16).ap()
    d_cin = nc.dram_tensor("cc_in", [G4 * D, 512], BF16).ap()
    d_cout = nc.dram_tensor("cc_out", [8 * G4 * D, 512], BF16).ap()

    X = es.enter_context(nc.sbuf_tensor("X", [128, CT * T], F32))
    WB = [es.enter_context(nc.sbuf_tensor(f"WB{i}", [128, CT * D], BF16)) for i in range(2)]
    ONESB = es.enter_context(nc.sbuf_tensor("ONESB", [128, 16], BF16))
    ONE128 = es.enter_context(nc.sbuf_tensor("ONE128", [128, 128], BF16))
    ONESR = es.enter_context(nc.sbuf_tensor("ONESR", [1, 128], F32))
    IOTA8 = es.enter_context(nc.sbuf_tensor("IOTA8", [1, 8], F32))
    IOTAP = es.enter_context(nc.sbuf_tensor("IOTAP", [128, 1], F32))
    GW = es.enter_context(nc.sbuf_tensor("GW", [128, 8 * CT], F32))
    GBr = es.enter_context(nc.sbuf_tensor("GBr", [1, 8], F32))
    SCR = es.enter_context(nc.sbuf_tensor("SCR", [1, 64], F32))
    EPS1 = es.enter_context(nc.sbuf_tensor("EPS1", [128, 1], F32))
    SCB = es.enter_context(nc.sbuf_tensor("SCB", [128, 16], F32))
    IDXS = es.enter_context(nc.sbuf_tensor("IDXS", [128, 18], I32))
    BG = es.enter_context(nc.sbuf_tensor("BGg", [128, 2 * CT], F32))
    CIOTA = es.enter_context(nc.sbuf_tensor("CIOTA", [128, 8], F32))
    IXS = es.enter_context(nc.sbuf_tensor("IXS", [128, 16], F32))

    sS = Sem(es.enter_context(nc.semaphore("sS")))
    sG = Sem(es.enter_context(nc.semaphore("sG")))
    sMM = Sem(es.enter_context(nc.semaphore("sMM")))
    sV = Sem(es.enter_context(nc.semaphore("sV")))
    sA = Sem(es.enter_context(nc.semaphore("sA")))
    sP = Sem(es.enter_context(nc.semaphore("sP")))
    sC = Sem(es.enter_context(nc.semaphore("sC")))
    sX = Sem(es.enter_context(nc.semaphore("sX")))
    sXg = Sem(es.enter_context(nc.semaphore("sXg")))
    sR = Sem(es.enter_context(nc.semaphore("sR")))
    sRb = Sem(es.enter_context(nc.semaphore("sRb")))
    ALL_SEMS = [sS, sG, sMM, sV, sA, sP, sC, sX, sXg, sR, sRb]

    block = es.enter_context(nc.Block())

    def vsync(ve, instr):
        sV.inc(instr)
        ve.wait_ge(sV.h, sV.n)

    def asylc(sc, instr):
        sA.inc(instr)
        sc.wait_ge(sA.h, sA.n)

    wb_free = [0, 0]
    wb_loaded = [0, 0]

    def load_weight(slot, dram_ap, cols=D, nct=CT):
        @block.gpsimd
        def _(gp, slot=slot, dram_ap=dram_ap, cols=cols, nct=nct):
            gp.wait_ge(sMM.h, wb_free[slot])
            sG.dma(gp.dma_start(
                out=WB[slot].rearrange("p (ct i) -> p ct i", ct=CT)[:, 0:nct, 0:cols],
                in_=dram_ap.rearrange("(ct p) i -> p ct i", p=128)))
        wb_loaded[slot] = sG.n

    XT = X.rearrange("p (ct t) -> p ct t", ct=CT)
    DXT = d_x.rearrange("(ct p) t -> p ct t", p=128)

    @block.sync
    def _(sy):
        sX.dma(sy.dma_start(out=XT[:, 0:3], in_=DXT[:, 0:3]))
        sS.dma(sy.dma_start(out=GW[:], in_=d_gw[:]))
        sS.dma(sy.dma_start(out=GBr[:], in_=d_gb[:]))

    s_init = sS.n

    @block.scalar
    def _(sc):
        sX.dma(sc.dma_start(out=XT[:, 3:6], in_=DXT[:, 3:6]))

    @block.gpsimd
    def _(gp):
        sXg.dma(gp.dma_start(out=XT[:, 6:8], in_=DXT[:, 6:8]))
        gp.memset(ONESB[:], 1.0)
        gp.memset(ONE128[:], 1.0)
        gp.memset(ONESR[:], 1.0)
        gp.memset(EPS1[:], 1e-5)
        nc.gpsimd.iota(IOTA8[:], pattern=[[1, 8]], base=0, channel_multiplier=0,
                       allow_small_or_imprecise_dtypes=True)
        nc.gpsimd.iota(CIOTA[:], pattern=[[128, 8]], base=0, channel_multiplier=0,
                       allow_small_or_imprecise_dtypes=True)
        sP.inc(nc.gpsimd.iota(IOTAP[:], pattern=[[0, 1]], base=0, channel_multiplier=1,
                              allow_small_or_imprecise_dtypes=True))
    p_init = sP.n

    ln_ctr = [0]

    def emit_ln(XLN):
        k = ln_ctr[0]
        ln_ctr[0] += 1
        with ExitStack() as ph:
            SQ = ph.enter_context(nc.sbuf_tensor(f"lnSQ{k}", [128, CT * T], BF16))
            VAR = SQ[:].bitcast(F32)[:, 0:T]
            MB = SQ[:, 2 * T:3 * T]
            RB = SQ[:, 3 * T:4 * T]
            M2 = RB

            a_sq = []
            v_cp = []

            @block.scalar
            def _(sc):
                for s in ALL_SEMS:
                    if s.n:
                        sc.wait_ge(s.h, s.n)
                for ct in range(CT):
                    sA.inc(nc.scalar.activation(SQ[:, ct * T:(ct + 1) * T],
                                                X[:, ct * T:(ct + 1) * T], AF.Square))
                    a_sq.append(sA.n)
            a_prep = sA.n

            @block.vector
            def _(ve):
                for s in ALL_SEMS:
                    if s.n:
                        ve.wait_ge(s.h, s.n)
                for ct in range(CT):
                    sV.inc(nc.vector.tensor_copy(XLN[:, ct * T:(ct + 1) * T],
                                                 X[:, ct * T:(ct + 1) * T]))
                    v_cp.append(sV.n)
            v_prep = sV.n

            with ExitStack() as pp:
                p_s = [pp.enter_context(nc.psum_tensor(f"lnp{k}_{i}", [128, 512], F32))
                       for i in range(8)]

                @block.tensor
                def _(te):
                    for ct in range(CT):
                        te.wait_ge(sV.h, v_cp[ct])
                        te.wait_ge(sA.h, a_sq[ct])
                        for tcn in range(TCn):
                            sl = slice(ct * T + tcn * 512, ct * T + tcn * 512 + 512)
                            nc.tensor.matmul(p_s[tcn][:], ONE128[:], XLN[:, sl],
                                             start=(ct == 0), stop=(ct == CT - 1))
                            i = nc.tensor.matmul(p_s[4 + tcn][:], ONE128[:], SQ[:, sl],
                                                 start=(ct == 0), stop=(ct == CT - 1))
                    sMM.inc(i)
                mm_sums = sMM.n

                @block.vector
                def _(ve):
                    ve.wait_ge(sMM.h, mm_sums)
                    for tcn in range(TCn):
                        sl = slice(tcn * 512, tcn * 512 + 512)
                        nc.vector.tensor_scalar_mul(MB[:, sl], p_s[tcn][:], 1.0 / D)
                        nc.vector.tensor_scalar_mul(VAR[:, sl], p_s[4 + tcn][:], 1.0 / D)
                    for tcn in range(TCn):
                        sl = slice(tcn * 512, tcn * 512 + 512)
                        nc.vector.tensor_mul(M2[:, sl], MB[:, sl], MB[:, sl])
                    for tcn in range(TCn):
                        sl = slice(tcn * 512, tcn * 512 + 512)
                        i = nc.vector.tensor_sub(VAR[:, sl], VAR[:, sl], M2[:, sl])
                    sV.inc(i)
                v_var = sV.n

            @block.scalar
            def _(sc):
                sc.wait_ge(sV.h, v_var)
                sc.wait_ge(sP.h, p_init)
                asylc(sc, nc.scalar.activation(VAR[:], VAR[:], AF.Ln, bias=EPS1[:]))
                sA.inc(nc.scalar.activation(RB[:], VAR[:], AF.Exp, scale=-0.5))
            a_rstd = sA.n

            norm = []

            @block.vector
            def _(ve):
                ve.wait_ge(sA.h, a_rstd)
                for ct in range(CT):
                    sl = slice(ct * T, (ct + 1) * T)
                    nc.vector.tensor_sub(XLN[:, sl], XLN[:, sl], MB[:])
                    sV.inc(nc.vector.tensor_mul(XLN[:, sl], XLN[:, sl], RB[:]))
                    norm.append(sV.n)
            return norm

    proj_ctr = [0]

    def emit_proj(slot, rhs_fn, o_tiles, epilogue, ep_sem, tensor_wait=None, n_k=CT,
                  epi_batch=None, k_waits=None):
        k = proj_ctr[0]
        proj_ctr[0] += 1
        items = [(ot, tcn) for ot in o_tiles for tcn in range(TCn)]
        with ExitStack() as pp:
            pss = [pp.enter_context(nc.psum_tensor(f"pj{k}_{i}", [128, 512], F32))
                   for i in range(min(8, len(items)))]
            np_ = len(pss)
            epi_at = {}
            pending = []
            for i, (ot, tcn) in enumerate(items):
                ps = pss[i % np_]

                @block.tensor
                def _(te, i=i, ot=ot, tcn=tcn, ps=ps):
                    if i == 0:
                        if k_waits is None:
                            te.wait_ge(sG.h, wb_loaded[slot])
                        if tensor_wait is not None:
                            tensor_wait(te)
                    if i >= np_:
                        te.wait_ge(ep_sem.h, epi_at[i - np_])
                    for c in range(n_k):
                        if i == 0 and k_waits is not None:
                            te.wait_ge(sG.h, k_waits[c])
                        j = nc.tensor.matmul(
                            ps[:],
                            WB[slot][:, c * D + ot * 128: c * D + ot * 128 + 128],
                            rhs_fn(c, tcn),
                            start=(c == 0), stop=(c == n_k - 1))
                    sMM.inc(j)
                mm_i = sMM.n

                if epi_batch is None:
                    epilogue(ot, tcn, ps, mm_i)
                    epi_at[i] = ep_sem.n
                else:
                    pending.append((ot, tcn, ps, mm_i, i))
                    if len(pending) == epi_batch or i == len(items) - 1:
                        base = ep_sem.n
                        epilogue(pending)
                        assert ep_sem.n == base + len(pending)
                        for kk2, (_, _, _, _, i2) in enumerate(pending):
                            epi_at[i2] = base + kk2 + 1
                        pending = []
            wb_free[slot] = sMM.n

    def emit_attn_a(XLN, norm):
        with ExitStack() as ph:
            KT = ph.enter_context(nc.sbuf_tensor("aKT", [128, CT * TKP], BF16))
            VA = ph.enter_context(nc.sbuf_tensor("aVA", [128, JTn * 1040], BF16))
            QHP = [ph.enter_context(nc.sbuf_tensor(f"aQ{i}", [128, T], BF16)) for i in range(2)]
            PTs = [ph.enter_context(nc.sbuf_tensor(f"aPT{i}", [128, JTn * 512], BF16))
                   for i in range(2)]
            RZ2 = [ph.enter_context(nc.sbuf_tensor(f"aRZ{i}", [1, 512], BF16))
                   for i in range(2)]
            RZB = ph.enter_context(nc.sbuf_tensor("aRZB", [64, 512], F32))
            RZROW = RZB[0:1, :]
            OB = [ph.enter_context(nc.sbuf_tensor(f"aOB{i}", [128, T], BF16)) for i in range(2)]
            MASKS = ph.enter_context(nc.sbuf_tensor("aMSK", [128, max(1, NPART) * 512], BF16))
            ph_k = ExitStack()
            KTMP = ph_k.enter_context(nc.sbuf_tensor("aKTMP", [128, CT * TKP], BF16))

            if NPART:
                @block.sync
                def _(sy):
                    sS.dma(sy.dma_start(out=MASKS.rearrange("p (n w) -> p n w", w=512),
                                        in_=d_mask.rearrange("(n p) w -> p n w", p=128)))
            s_mask = sS.n

            @block.gpsimd
            def _(gp, w=[(s, s.n) for s in ALL_SEMS]):
                for s, n in w:
                    if n:
                        gp.wait_ge(s.h, n)
                sP.inc(gp.memset(KTMP[:], 0.0))
            p_zero = sP.n

            @block.vector
            def _(ve, w=[(s, s.n) for s in ALL_SEMS]):
                for s, n in w:
                    if n:
                        ve.wait_ge(s.h, n)
                nc.vector.memset(KT[:], 0.0)
                sV.inc(nc.vector.memset(VA[:], 0.0))

            ln_done = sV.n
            with ExitStack() as pp:
                pcs = [pp.enter_context(nc.psum_tensor(f"cv{i}", [128, 341], F32))
                       for i in range(8)]
                cv_evac = [0]
                for og in range(2):
                    ots = [og * 4 + i for i in range(4)]
                    for tau in range(3):
                        slot = (og * 3 + tau) % 2
                        load_weight(slot, d_wconv[tau])
                        g_w = sG.n

                        @block.tensor
                        def _(te, og=og, tau=tau, slot=slot, g_w=g_w):
                            te.wait_ge(sG.h, g_w)
                            if og == 0 and tau == 0:
                                te.wait_ge(sP.h, p_zero)
                            if og == 1 and tau == 0:
                                te.wait_ge(sV.h, cv_evac[0])
                            last = None
                            for oi, ot in enumerate(ots):
                                for jc in range(2):
                                    ps = pcs[oi * 2 + jc]
                                    j0 = jc * 341
                                    for c in range(CT):
                                        if og == 0 and tau == 0 and oi == 0 and jc == 0:
                                            te.wait_ge(sV.h, norm[c])
                                        last = nc.tensor.matmul(
                                            ps[:],
                                            WB[slot][:, c * D + ot * 128: c * D + ot * 128 + 128],
                                            XLN[:, c * T + 3 * j0 + tau: c * T + 3 * j0 + tau + 3 * 341: 3],
                                            start=(tau == 0 and c == 0),
                                            stop=(tau == 2 and c == CT - 1))
                            sMM.inc(last)
                        wb_free[slot] = sMM.n
                    mm_cv = sMM.n

                    @block.vector
                    def _(ve, ots=ots, mm_cv=mm_cv):
                        ve.wait_ge(sMM.h, mm_cv)
                        for oi, ot in enumerate(ots):
                            for jc in range(2):
                                nc.vector.tensor_copy(
                                    KTMP[:, ot * TKP + 1 + jc * 341: ot * TKP + 1 + jc * 341 + 341],
                                    pcs[oi * 2 + jc][:])
                            i = nc.vector.tensor_copy(KTMP[:, ot * TKP: ot * TKP + 1],
                                                      XLN[:, ot * T: ot * T + 1])
                        sV.inc(i)
                        cv_evac[0] = sV.n
            v_ktmp = sV.n

            load_weight(0, d_wk)
            g_wk = sG.n
            with ExitStack() as pp:
                pks = [pp.enter_context(nc.psum_tensor(f"kp{i}", [128, 512], F32))
                       for i in range(8)]
                jchunks = [(0, 512), (512, 171)]
                items = [(ot, jci) for ot in range(CT) for jci in range(2)]
                epi_at = {}
                for i, (ot, jci) in enumerate(items):
                    j0, n = jchunks[jci]
                    ps = pks[i % 8]

                    @block.tensor
                    def _(te, i=i, ot=ot, j0=j0, n=n, ps=ps):
                        if i == 0:
                            te.wait_ge(sG.h, g_wk)
                            te.wait_ge(sV.h, v_ktmp)
                        if i >= 8:
                            te.wait_ge(sV.h, epi_at[i - 8])
                        for c in range(CT):
                            j = nc.tensor.matmul(
                                ps[:, 0:n],
                                WB[0][:, c * D + ot * 128: c * D + ot * 128 + 128],
                                KTMP[:, c * TKP + j0: c * TKP + j0 + n],
                                start=(c == 0), stop=(c == CT - 1))
                        sMM.inc(j)
                    mm_i = sMM.n

                    @block.vector
                    def _(ve, ot=ot, j0=j0, n=n, ps=ps, mm_i=mm_i):
                        ve.wait_ge(sMM.h, mm_i)
                        sV.inc(nc.vector.tensor_copy(KT[:, ot * TKP + j0: ot * TKP + j0 + n],
                                                     ps[:, 0:n]))
                    epi_at[i] = sV.n
                wb_free[0] = sMM.n

            load_weight(1, d_wv)
            g_wv = sG.n
            v_kdone = sV.n
            with ExitStack() as pp:
                pvs = [pp.enter_context(nc.psum_tensor(f"vp{i}", [128, 512], F32))
                       for i in range(8)]
                items = [(jt, ic) for jt in range(JTn) for ic in range(2)]
                epi_at = {}
                for i, (jt, ic) in enumerate(items):
                    ps = pvs[i % 8]

                    @block.tensor
                    def _(te, i=i, jt=jt, ic=ic, ps=ps):
                        if i == 0:
                            te.wait_ge(sG.h, g_wv)
                            te.wait_ge(sV.h, v_kdone)
                        if i >= 8:
                            te.wait_ge(sV.h, epi_at[i - 8])
                        for c in range(CT):
                            j = nc.tensor.matmul(
                                ps[:],
                                KTMP[:, c * TKP + jt * 128: c * TKP + jt * 128 + 128],
                                WB[1][:, c * D + ic * 512: c * D + ic * 512 + 512],
                                start=(c == 0), stop=(c == CT - 1))
                        sMM.inc(j)
                    mm_i = sMM.n

                    @block.vector
                    def _(ve, jt=jt, ic=ic, ps=ps, mm_i=mm_i):
                        ve.wait_ge(sMM.h, mm_i)
                        sV.inc(nc.vector.tensor_copy(
                            VA[:, jt * 1040 + (8 * ic) * 65: jt * 1040 + (8 * ic + 8) * 65]
                            .rearrange("p (h c) -> p h c", h=8)[:, :, 0:64],
                            ps[:].rearrange("p (h c) -> p h c", h=8)))
                    epi_at[i] = sV.n
                wb_free[1] = sMM.n

                @block.vector
                def _(ve):
                    sV.inc(nc.vector.tensor_copy(
                        VA.rearrange("p (jt x) -> p jt x", jt=JTn)[:, :, 64:1040:65],
                        ONESB[:, 0:1].to_broadcast([128, JTn, 16])))
            v_va = sV.n

            ph_k.close()
            load_weight(0, d_wq)
            g_wq = sG.n
            with ExitStack() as pp:
                psc = [pp.enter_context(nc.psum_tensor(f"sc{i}", [128, 512], F32))
                       for i in range(4)]
                pso = [pp.enter_context(nc.psum_tensor(f"so{i}", [65, 512], F32))
                       for i in range(2)]
                psz = [pp.enter_context(nc.psum_tensor(f"szz{i}", [64, 512], F32))
                       for i in range(2)]
                sc_use = [0]
                sc_free = {}
                q_epi = {}
                ob_store = {}
                o_free = {}
                z_free = {}
                v_q = {}
                mm_o = {}
                mm_zd = {}
                it_state = {}
                iters = [(hp, hh, tcn) for hp in range(8) for hh in range(2)
                         for tcn in range(TCn)]
                n_it = len(iters)

                def emit_qproj(hp):
                    qb = QHP[hp % 2]
                    for tcn in range(TCn):
                        u = sc_use[0]
                        ps = psc[u % 4]
                        sc_use[0] += 1

                        @block.tensor
                        def _(te, hp=hp, tcn=tcn, ps=ps, u=u):
                            if hp == 0 and tcn == 0:
                                te.wait_ge(sG.h, g_wq)
                                te.wait_ge(sV.h, v_va)
                            if u >= 4:
                                fr = sc_free[u - 4]
                                te.wait_ge(fr[0].h, fr[1])
                            if hp >= 2 and tcn == 0:
                                te.wait_ge(sA.h, q_epi[hp - 2])
                            for c in range(CT):
                                j = nc.tensor.matmul(
                                    ps[:],
                                    WB[0][:, c * D + hp * 128: c * D + hp * 128 + 128],
                                    XLN[:, c * T + tcn * 512: c * T + tcn * 512 + 512],
                                    start=(c == 0), stop=(c == CT - 1))
                            sMM.inc(j)
                        mm_i = sMM.n

                        @block.vector
                        def _(ve, qb=qb, tcn=tcn, ps=ps, mm_i=mm_i):
                            ve.wait_ge(sMM.h, mm_i)
                            sV.inc(nc.vector.tensor_copy(qb[:, tcn * 512:(tcn + 1) * 512], ps[:]))
                        sc_free[u] = (sV, sV.n)
                    v_q[hp] = sV.n

                def emit_lagged(n):
                    po, pz, hp, h, tcn, a_rz_n = it_state[n]

                    @block.tensor
                    def _(te, n=n, pz=pz, a_rz_n=a_rz_n):
                        if n >= 2:
                            fr = z_free[n - 2]
                            te.wait_ge(fr[0].h, fr[1])
                        te.wait_ge(sA.h, a_rz_n)
                        sMM.inc(nc.tensor.matmul(pz[:], ONE128[0:1, 0:64], RZ2[n % 2][:],
                                                 start=True, stop=True))
                    mm_zd[n] = sMM.n

                    @block.vector
                    def _(ve, n=n, po=po, pz=pz, hp=hp, h=h, tcn=tcn):
                        ve.wait_ge(sMM.h, mm_zd[n])
                        if h % 2 == 0 and tcn == 0 and hp >= 2:
                            ve.wait_ge(sS.h, ob_store[hp - 2])
                        nc.vector.tensor_copy(RZB[:], pz[:])
                        sV.inc(nc.vector.tensor_mul(
                            OB[hp % 2][(h % 2) * 64:(h % 2) * 64 + 64, tcn * 512:(tcn + 1) * 512],
                            po[0:64, :], RZB[:]))
                    o_free[n] = (sV, sV.n)
                    z_free[n] = (sV, sV.n)
                    if h % 2 == 1 and tcn == TCn - 1:
                        v_ob = sV.n

                        @block.sync
                        def _(sy, hp=hp, v_ob=v_ob):
                            if hp >= 2:
                                sy.wait_ge(sS.h, ob_store[hp - 2])
                            sy.wait_ge(sV.h, v_ob)
                            sS.dma(sy.dma_start(out=d_odr[hp * 128:(hp + 1) * 128, :],
                                                in_=OB[hp % 2][:]))
                        ob_store[hp] = sS.n

                def emit_sc_z(n):
                    po = it_state[n][0]

                    @block.scalar
                    def _(sc, n=n, po=po):
                        sc.wait_ge(sMM.h, mm_o[n])
                        if n >= 2:
                            sc.wait_ge(sMM.h, mm_zd[n - 2])
                        nc.scalar.activation(RZROW[:], po[64:65, :], AF.Ln)
                        sA.inc(nc.scalar.activation(RZ2[n % 2][:], RZROW[:], AF.Exp,
                                                    scale=-1.0))

                for n, (hp, hh, tcn) in enumerate(iters):
                    if hh == 0 and tcn == 0:
                        emit_qproj(hp)
                    qb = QHP[hp % 2]
                    h = 2 * hp + hh
                    kbase = (h // 2) * TKP
                    krow = (h % 2) * 64
                    PTb = PTs[n % 2]
                    live = [jt for jt in range(JTn) if blk_state(jt, tcn) != "masked"]
                    ntcn = iters[n + 1][2] if n + 1 < n_it else None
                    nlive = (len([jt for jt in range(JTn) if blk_state(jt, ntcn) != "masked"])
                             if ntcn is not None else 0)
                    S_n = sA.n
                    a_exp_at = {jt: S_n + ji + 1 for ji, jt in enumerate(live)}
                    a_last = S_n + len(live)
                    a_rz_n = S_n + len(live) + (1 if n >= 1 else 0) + nlive + 1
                    mm_sc_at = {}
                    psc_map = []
                    for ji, jt in enumerate(live):
                        u = sc_use[0]
                        ps = psc[u % 4]
                        sc_use[0] += 1
                        psc_map.append(ps)

                        @block.tensor
                        def _(te, jt=jt, ps=ps, u=u, vq=v_q[hp], kbase=kbase,
                              krow=krow, qb=qb, tcn=tcn):
                            if u >= 4:
                                fr = sc_free[u - 4]
                                te.wait_ge(fr[0].h, fr[1])
                            te.wait_ge(sV.h, vq)
                            sMM.inc(nc.tensor.matmul(
                                ps[:],
                                KT[krow:krow + 64, kbase + jt * 128: kbase + jt * 128 + 128],
                                qb[krow:krow + 64, tcn * 512:(tcn + 1) * 512],
                                start=True, stop=True))
                        mm_sc_at[jt] = sMM.n
                        sc_free[u] = (sA, a_exp_at[jt])
                    v_mask = sV.n
                    parts = [jt for jt in live if blk_state(jt, tcn) == "partial"]
                    if parts:
                        @block.vector
                        def _(ve, parts=parts, tcn=tcn, a_exp_at=dict(a_exp_at), PTb=PTb):
                            ve.wait_ge(sS.h, s_mask)
                            for jt in parts:
                                pi = PARTIALS.index((jt, tcn))
                                ve.wait_ge(sA.h, a_exp_at[jt])
                                i = nc.vector.tensor_mul(
                                    PTb[:, jt * 512:(jt + 1) * 512],
                                    PTb[:, jt * 512:(jt + 1) * 512],
                                    MASKS[:, pi * 512:(pi + 1) * 512])
                            sV.inc(i)
                        v_mask = sV.n
                    po = pso[n % 2]
                    pz = psz[n % 2]

                    @block.tensor
                    def _(te, n=n, h=h, live=live, po=po, v_mask=v_mask,
                          a_last=a_last, PTb=PTb):
                        if n >= 2:
                            fr = o_free[n - 2]
                            te.wait_ge(fr[0].h, fr[1])
                        te.wait_ge(sA.h, a_last)
                        te.wait_ge(sV.h, v_mask)
                        for ci, jt in enumerate(live):
                            j = nc.tensor.matmul(
                                po[:],
                                VA[:, jt * 1040 + 65 * h: jt * 1040 + 65 * h + 65],
                                PTb[:, jt * 512:(jt + 1) * 512],
                                start=(ci == 0), stop=(ci == len(live) - 1))
                        sMM.inc(j)
                    mm_o[n] = sMM.n
                    it_state[n] = (po, pz, hp, h, tcn, a_rz_n)
                    if n >= 1:
                        emit_lagged(n - 1)

                    @block.scalar
                    def _(sc, n=n, live=live, mm_sc_at=dict(mm_sc_at),
                          psc_map=list(psc_map), PTb=PTb):
                        if n >= 2:
                            sc.wait_ge(sMM.h, mm_o[n - 2])
                        for ji, jt in enumerate(live):
                            sc.wait_ge(sMM.h, mm_sc_at[jt])
                            sA.inc(nc.scalar.activation(
                                PTb[:, jt * 512:(jt + 1) * 512], psc_map[ji][:],
                                AF.Exp, scale=0.125))
                    assert sA.n == a_last, (sA.n, a_last)
                    if n >= 1:
                        emit_sc_z(n - 1)
                        assert sA.n == it_state[n - 1][5], (sA.n, it_state[n - 1][5])
                    if h % 2 == 1 and tcn == TCn - 1:
                        q_epi[hp] = sA.n
                emit_lagged(n_it - 1)
                emit_sc_z(n_it - 1)
                assert sA.n == it_state[n_it - 1][5], (sA.n, it_state[n_it - 1][5])
                wb_free[0] = sMM.n
            s_oall = sS.n

        if True:
            load_weight(1, d_wo)
            with ExitStack() as op:
                OCH = op.enter_context(nc.sbuf_tensor("aOCH", [128, CT * T], BF16))
                BO = op.enter_context(nc.sbuf_tensor("aBO2", [128, CT], F32))

                @block.sync
                def _(sy):
                    sS.dma(sy.dma_start(out=BO[:], in_=d_bo[:]))
                s_bo = sS.n

                @block.sync
                def _(sy):
                    sy.wait_ge(sS.h, s_oall)
                    sS.dma(sy.dma_start(out=OCH.rearrange("p (ct t) -> p ct t", ct=CT),
                                        in_=d_odr.rearrange("(ct p) t -> p ct t", p=128)))
                s_och = sS.n

                def epi(ot, tcn, ps, mm_i):
                    @block.vector
                    def _(ve, ot=ot, tcn=tcn, ps=ps, mm_i=mm_i):
                        ve.wait_ge(sMM.h, mm_i)
                        ve.wait_ge(sS.h, s_bo)
                        sV.inc(nc.vector.scalar_tensor_tensor(
                            X[:, ot * T + tcn * 512: ot * T + tcn * 512 + 512],
                            ps[:], BO[:, ot:ot + 1],
                            X[:, ot * T + tcn * 512: ot * T + tcn * 512 + 512],
                            ALU.add, ALU.add))

                emit_proj(1, lambda c, tcn: OCH[:, c * T + tcn * 512: c * T + tcn * 512 + 512],
                          list(range(CT)), epi, sV,
                          tensor_wait=lambda te: te.wait_ge(sS.h, s_och))

    ffn_ctr = [0]

    def emit_ffn(XLN, out_dma=False):
        k = ffn_ctr[0]
        ffn_ctr[0] += 1
        with ExitStack() as ph:
            POOL = ph.enter_context(nc.sbuf_tensor(f"fPO{k}", [128, CT], F32))
            LOGI = ph.enter_context(nc.sbuf_tensor(f"fLG{k}", [1, 8], F32))
            TOP8 = ph.enter_context(nc.sbuf_tensor(f"fT8{k}", [1, 8], F32))
            TMP8 = ph.enter_context(nc.sbuf_tensor(f"fTM{k}", [1, 24], F32))
            G1B = ph.enter_context(nc.sbuf_tensor(f"fG1{k}", [128, CT * T], BF16))
            G2B = ph.enter_context(nc.sbuf_tensor(f"fG2{k}", [128, 8 * 512], BF16))

            @block.vector
            def _(ve):
                for ct in range(CT):
                    i = nc.vector.tensor_reduce(POOL[:, ct:ct + 1], XLN[:, ct * T:(ct + 1) * T],
                                                AX.X, ALU.add)
                vsync(ve, i)
                sV.inc(nc.vector.tensor_scalar_mul(POOL[:], POOL[:], 1.0 / T))
            v_pool = sV.n

            with ExitStack() as pp:
                pl = pp.enter_context(nc.psum_tensor(f"fpl{k}", [1, 8], F32))
                pb = pp.enter_context(nc.psum_tensor(f"fpb{k}", [128, 8], F32))

                @block.tensor
                def _(te):
                    te.wait_ge(sV.h, v_pool)
                    te.wait_ge(sS.h, s_init)
                    for c in range(CT):
                        j = nc.tensor.matmul(pl[:], POOL[:, c:c + 1], GW[:, c * 8:(c + 1) * 8],
                                             start=(c == 0), stop=(c == CT - 1))
                    sMM.inc(j)
                mm_lg = sMM.n

                @block.vector
                def _(ve):
                    ve.wait_ge(sMM.h, mm_lg)
                    vsync(ve, nc.vector.tensor_add(LOGI[:], pl[:], GBr[:]))
                    vsync(ve, nc.vector.max(out=TOP8[:], in_=LOGI[:]))
                    vsync(ve, nc.vector.tensor_single_scalar(TMP8[:, 0:8], LOGI[:], TOP8[:, 1:2], ALU.is_ge))
                    vsync(ve, nc.vector.tensor_single_scalar(TMP8[:, 8:16], LOGI[:], TOP8[:, 0:1], ALU.is_ge))
                v_keep = sV.n

                @block.scalar
                def _(sc):
                    sc.wait_ge(sV.h, v_keep)
                    sA.inc(nc.scalar.activation(TMP8[:, 16:24], LOGI[:], AF.Exp))
                a_exp = sA.n

                @block.vector
                def _(ve):
                    ve.wait_ge(sA.h, a_exp)
                    vsync(ve, nc.vector.tensor_mul(TMP8[:, 16:24], TMP8[:, 16:24], TMP8[:, 0:8]))
                    vsync(ve, nc.vector.tensor_reduce(SCR[:, 0:1], TMP8[:, 16:24], AX.X, ALU.add))
                    vsync(ve, nc.vector.reciprocal(SCR[:, 1:2], SCR[:, 0:1]))
                    vsync(ve, nc.vector.tensor_single_scalar(TMP8[:, 16:24], TMP8[:, 16:24], SCR[:, 1:2], ALU.mult))
                    vsync(ve, nc.vector.tensor_mul(SCR[:, 8:16], IOTA8[:], TMP8[:, 8:16]))
                    vsync(ve, nc.vector.tensor_reduce(SCR[:, 2:3], SCR[:, 8:16], AX.X, ALU.add))
                    vsync(ve, nc.vector.tensor_mul(SCR[:, 8:16], IOTA8[:], TMP8[:, 0:8]))
                    vsync(ve, nc.vector.tensor_reduce(SCR[:, 3:4], SCR[:, 8:16], AX.X, ALU.add))
                    vsync(ve, nc.vector.tensor_sub(SCR[:, 3:4], SCR[:, 3:4], SCR[:, 2:3]))
                    vsync(ve, nc.vector.tensor_mul(SCR[:, 8:16], TMP8[:, 16:24], TMP8[:, 8:16]))
                    vsync(ve, nc.vector.tensor_reduce(SCR[:, 4:5], SCR[:, 8:16], AX.X, ALU.add))
                    vsync(ve, nc.vector.tensor_scalar(SCR[:, 5:6], SCR[:, 4:5], -1.0, 1.0, ALU.mult, ALU.add))
                    nc.vector.tensor_scalar_mul(SCR[:, 16:17], SCR[:, 2:3], 1024.0)
                    nc.vector.tensor_scalar_mul(SCR[:, 17:18], SCR[:, 3:4], 1024.0)
                    nc.vector.tensor_scalar_mul(SCR[:, 18:19], SCR[:, 2:3], 128.0)
                    nc.vector.tensor_scalar_mul(SCR[:, 19:20], SCR[:, 3:4], 128.0)
                    nc.vector.tensor_copy(SCR[:, 20:21], SCR[:, 4:5])
                    vsync(ve, nc.vector.tensor_copy(SCR[:, 21:22], SCR[:, 5:6]))
                v_gate = sV.n

                @block.tensor
                def _(te):
                    te.wait_ge(sV.h, v_gate)
                    sMM.inc(nc.tensor.matmul(pb[:, 0:6], ONESR[:], SCR[0:1, 16:22],
                                             start=True, stop=True))
                mm_bc = sMM.n

                @block.vector
                def _(ve):
                    ve.wait_ge(sMM.h, mm_bc)
                    ve.wait_ge(sP.h, p_init)
                    vsync(ve, nc.vector.tensor_copy(SCB[:, 0:6], pb[:, 0:6]))
                    nc.vector.tensor_add(SCB[:, 6:7], SCB[:, 0:1], IOTAP[:])
                    nc.vector.tensor_add(SCB[:, 7:8], SCB[:, 1:2], IOTAP[:])
                    nc.vector.tensor_add(SCB[:, 8:9], SCB[:, 2:3], IOTAP[:])
                    vsync(ve, nc.vector.tensor_add(SCB[:, 9:10], SCB[:, 3:4], IOTAP[:]))
                    nc.vector.tensor_tensor(IXS[:, 0:8], SCB[:, 6:7].to_broadcast([128, 8]),
                                            CIOTA[:], ALU.add)
                    vsync(ve, nc.vector.tensor_tensor(IXS[:, 8:16], SCB[:, 7:8].to_broadcast([128, 8]),
                                                      CIOTA[:], ALU.add))
                    vsync(ve, nc.vector.tensor_copy(IDXS[:, 0:16], IXS[:]))
                    nc.vector.tensor_copy(IDXS[:, 16:17], SCB[:, 8:9])
                    sV.inc(nc.vector.tensor_copy(IDXS[:, 17:18], SCB[:, 9:10]))
                v_idx = sV.n

            if FFNSTAGE < 2:
                return
            gmarks = {0: [], 1: []}
            for s_ in range(2):
                @block.gpsimd
                def _(gp, s_=s_):
                    gp.wait_ge(sV.h, v_idx)
                    gp.wait_ge(sMM.h, wb_free[s_])
                    for c in range(CT):
                        sG.dma(nc.gpsimd.indirect_dma_start(
                            out=WB[s_][:, c * D:(c + 1) * D],
                            out_offset=None,
                            in_=d_ew[:],
                            in_offset=IndirectOffsetOnAxis(IDXS[:, s_ * 8 + c:s_ * 8 + c + 1], 0)))
                        gmarks[s_].append(sG.n)
                    sG.dma(nc.gpsimd.indirect_dma_start(
                        out=BG[:, s_ * CT:(s_ + 1) * CT],
                        out_offset=None,
                        in_=d_eb[:],
                        in_offset=IndirectOffsetOnAxis(IDXS[:, 16 + s_:17 + s_], 0)))
                wb_loaded[s_] = sG.n

            if FFNSTAGE < 3:
                return
            def epi1(batch):
                @block.scalar
                def _(sc, batch=list(batch)):
                    sc.wait_ge(sG.h, wb_loaded[0])
                    for (ot, tcn, ps, mm_i, _) in batch:
                        sc.wait_ge(sMM.h, mm_i)
                        sA.inc(nc.scalar.activation(
                            G1B[:, ot * T + tcn * 512: ot * T + tcn * 512 + 512],
                            ps[:], AF.Gelu, bias=BG[:, ot:ot + 1]))

            emit_proj(0, lambda c, tcn: XLN[:, c * T + tcn * 512: c * T + tcn * 512 + 512],
                      list(range(CT)), epi1, sA, epi_batch=8)

            def epi2(batch):
                a_base = sA.n

                @block.scalar
                def _(sc, batch=list(batch), vprev=cmb_done[0]):
                    sc.wait_ge(sG.h, wb_loaded[1])
                    sc.wait_ge(sV.h, vprev)
                    for sl_i, (ot, tcn, ps, mm_i, _) in enumerate(batch):
                        sc.wait_ge(sMM.h, mm_i)
                        sA.inc(nc.scalar.activation(G2B[:, sl_i * 512:(sl_i + 1) * 512],
                                                    ps[:], AF.Gelu,
                                                    bias=BG[:, CT + ot:CT + ot + 1]))

                @block.vector
                def _(ve, batch=list(batch), a_base=a_base):
                    for sl_i, (ot, tcn, ps, mm_i, _) in enumerate(batch):
                        ve.wait_ge(sA.h, a_base + sl_i + 1)
                        sl = slice(ot * T + tcn * 512, ot * T + tcn * 512 + 512)
                        nc.vector.scalar_tensor_tensor(X[:, sl], G1B[:, sl], SCB[:, 4:5],
                                                       X[:, sl], ALU.mult, ALU.add)
                        sV.inc(nc.vector.scalar_tensor_tensor(X[:, sl], G2B[:, sl_i * 512:(sl_i + 1) * 512],
                                                              SCB[:, 5:6],
                                                              X[:, sl], ALU.mult, ALU.add))
                cmb_done[0] = sV.n

                if out_dma:
                    @block.sync
                    def _(sy, batch=list(batch), vmark=cmb_done[0]):
                        sy.wait_ge(sV.h, vmark)
                        for (ot, tcn, ps, mm_i, _) in batch:
                            sS.dma(sy.dma_start(
                                out=d_out.rearrange("(ct p) t -> p ct t", p=128)
                                [:, ot, tcn * 512:(tcn + 1) * 512],
                                in_=X[:, ot * T + tcn * 512: ot * T + tcn * 512 + 512]))

            cmb_done = [sV.n]
            emit_proj(1, lambda c, tcn: XLN[:, c * T + tcn * 512: c * T + tcn * 512 + 512],
                      list(range(CT)), epi2, sV, epi_batch=8)

    def emit_attn_b(XLN, xs, ln_done, Q13, K13):
        s_stage_g = []
        pqz = [0]
        with ExitStack() as ph:
            BNC = [ph.enter_context(nc.sbuf_tensor(f"bBN{i}", [128, 512], BF16)) for i in range(4)]
            bnc_free = {}
            bnc_use = [0]
            for g in range(G4):
                load_weight(g % 2, d_wqkvw[:, g * D:(g + 1) * D])
                if g == 0:
                    @block.gpsimd
                    def _(gp, w=[(s, s.n) for s in ALL_SEMS]):
                        for s, n in w:
                            if n:
                                gp.wait_ge(s.h, n)
                        gp.memset(Q13[:], 0.0)
                        sP.inc(gp.memset(K13[:], 0.0))
                    pqz[0] = sP.n

                def epi(ot, tcn, ps, mm_i, g=g):
                    u = bnc_use[0]
                    bnc_use[0] += 1
                    bn = BNC[u % 4]

                    @block.vector
                    def _(ve, ot=ot, tcn=tcn, ps=ps, mm_i=mm_i, bn=bn, u=u):
                        ve.wait_ge(sMM.h, mm_i)
                        if u >= 4:
                            ve.wait_ge(sS.h, bnc_free[u - 4])
                        sV.inc(nc.vector.tensor_copy(bn[:], ps[:]))
                    v_i = sV.n

                    @block.sync
                    def _(sy, ot=ot, tcn=tcn, bn=bn, v_i=v_i, g=g):
                        sy.wait_ge(sV.h, v_i)
                        sS.dma(sy.dma_start(
                            out=d_stage[g, ot * 128:(ot + 1) * 128, tcn * 512:(tcn + 1) * 512],
                            in_=bn[:]))
                    bnc_free[bnc_use[0] - 1] = sS.n

                def twait(te, g=g):
                    te.wait_ge(sV.h, ln_done)
                    if g == 0:
                        te.wait_ge(sA.h, sA.n)
                emit_proj(g % 2, lambda c, tcn: XLN[:, c * T + tcn * 512: c * T + tcn * 512 + 512],
                          list(range(CT)), epi, sV, tensor_wait=twait)
                s_stage_g.append(sS.n)
            s_stage_all = sS.n
        xs.close()
        b1_end = sMM.n
        b1_v_end = sV.n
        p_qk_zero = pqz[0]

        load_weight(0, d_wout)

        with ExitStack() as ph:
            V17 = ph.enter_context(nc.sbuf_tensor("bV17", [128, 8192], BF16))
            W18 = ph.enter_context(nc.sbuf_tensor("bW18", [128, 8192], BF16))
            ESB = ph.enter_context(nc.sbuf_tensor("bESB", [128, 2048], BF16))
            ZS = ph.enter_context(nc.sbuf_tensor("bZS", [128, 128], F32))
            RZS = ph.enter_context(nc.sbuf_tensor("bRZS", [128, 128], F32))
            PACK = ph.enter_context(nc.sbuf_tensor("bPACK", [128, 4096], BF16))
            PTB = ph.enter_context(nc.sbuf_tensor("bPTB", [128, 4096], BF16))
            ZG = ph.enter_context(nc.sbuf_tensor("bZG", [128, 8192], BF16))
            PCS = [ph.enter_context(nc.sbuf_tensor(f"bPC{i}", [128, 2048], BF16))
                   for i in range(2)]

            st4s = [d_stage[g].rearrange("(cc hd) (h p) -> cc hd h p", hd=16, p=128)
                    for g in range(G4)]
            r_qk = {}
            r_vw = {}
            mm_p1_done = {}
            a_exp_last = {}
            rdv_last = {}
            ev_gp_last = {}
            zdma_at = []
            g_mm_done = {}
            g_evac_done = {}
            pvz = [0]

            def emit_wave_a(g):
                @block.scalar
                def _(sc, g=g):
                    sc.wait_ge(sP.h, p_qk_zero)
                    sc.wait_ge(sS.h, s_stage_g[g])
                    if g >= 1:
                        sc.wait_ge(sMM.h, mm_p1_done[g - 1])
                    st4 = st4s[g]
                    for j in range(4):
                        sR.dma(sc.dma_start(
                            out=Q13[32 * j:32 * j + 16, :].rearrange("hd (m h p) -> hd m h p", m=4, h=16),
                            in_=st4[j:16:4].transpose([1, 0, 2, 3])))
                        sR.dma(sc.dma_start(
                            out=K13[32 * j:32 * j + 16, :].rearrange("hd (m h p) -> hd m h p", m=4, h=16),
                            in_=st4[16 + j:32:4].transpose([1, 0, 2, 3])))
                r_qk[g] = sR.n

            def emit_wave_b(g):
                @block.gpsimd
                def _(gp, g=g):
                    gp.wait_ge(sP.h, pvz[0])
                    gp.wait_ge(sS.h, s_stage_g[g])
                    if g >= 1:
                        gp.wait_ge(sMM.h, g_mm_done[g - 1])
                        gp.wait_ge(sP.h, g_evac_done[g - 1])
                    st4 = st4s[g]
                    for j in range(4):
                        sRb.dma(gp.dma_start(
                            out=W18[32 * j:32 * j + 16, :].rearrange("hd (m h p) -> hd m h p", m=4, h=16),
                            in_=st4[48 + j:64:4].transpose([1, 0, 2, 3])))
                    for m in range(4):
                        sRb.dma(gp.dma_start(
                            out=V17[32 * m:32 * m + 16, :].rearrange("h (j hd p) -> h j hd p", j=4, hd=16),
                            in_=st4[32 + 4 * m:32 + 4 * m + 4].transpose([2, 0, 1, 3])))
                r_vw[g] = sRb.n

            emit_wave_a(0)

            @block.gpsimd
            def _(gp, w=[(s, s.n) for s in ALL_SEMS]):
                for s, n in w:
                    if n:
                        gp.wait_ge(s.h, n)
                gp.memset(V17[:], 0.0)
                gp.memset(W18[:], 0.0)
                sP.inc(gp.memset(PACK[:], 0.0))
            pvz[0] = sP.n
            p_vw_zero = pvz[0]

            emit_wave_b(0)

            for g in range(G4):

                with ExitStack() as pp:
                    P1 = [pp.enter_context(nc.psum_tensor(f"bP1{g}_{j}", [128, 512], F32))
                          for j in range(4)]
                    P2 = [pp.enter_context(nc.psum_tensor(f"bP2{g}_{m}", [128, 512], F32))
                          for m in range(4)]
                    r_done_v = {}
                    cp_at = {}
                    ev_gp = {}
                    r_exp = {}
                    r_ev = {}
                    mm_p2_at = {}
                    for r in range(4):
                        @block.tensor
                        def _(te, g=g, r=r):
                            if r == 0:
                                te.wait_ge(sR.h, r_qk[g])
                                if g == 0:
                                    te.wait_ge(sMM.h, b1_end)
                                    te.wait_ge(sV.h, b1_v_end)
                                else:
                                    te.wait_ge(sA.h, a_exp_last[g - 1])
                            if r >= 1:
                                te.wait_ge(sA.h, r_exp[r - 1])
                            for beta in range(32):
                                p = 32 * r + beta
                                for cc in range(16):
                                    j, m = cc % 4, cc // 4
                                    tp = (32 * j, 32 * m) if (j == 3 or m == 3) else None
                                    i = nc.tensor.matmul(
                                        P1[j][32 * m:32 * m + 16, 16 * beta:16 * beta + 16],
                                        Q13[32 * j:32 * j + 32,
                                            m * 2048 + p: m * 2048 + p + 15 * 128 + 1: 128],
                                        K13[32 * j:32 * j + 32,
                                            m * 2048 + p: m * 2048 + p + 15 * 128 + 1: 128],
                                        start=True, stop=True, tile_position=tp,
                                        skip_group_check=True)
                            sMM.inc(i)
                        mm_p1 = sMM.n
                        if r == 3:
                            mm_p1_done[g] = sMM.n

                        @block.scalar
                        def _(sc, g=g, r=r, mm_p1=mm_p1):
                            sc.wait_ge(sMM.h, mm_p1)
                            if r >= 1:
                                sc.wait_ge(sV.h, r_done_v[r - 1])
                            if r == 0 and g >= 1:
                                sc.wait_ge(sV.h, rdv_last[g - 1])
                            if r == 0 and g == 0:
                                sc.wait_ge(sS.h, s_stage_all)
                            for j in range(4):
                                i = nc.scalar.activation(ESB[:, j * 512:(j + 1) * 512],
                                                         P1[j][:], AF.Exp, scale=0.25)
                            sA.inc(i)
                        r_exp[r] = sA.n

                        @block.vector
                        def _(ve, g=g, r=r, a_e=sA.n):
                            ve.wait_ge(sA.h, a_e)
                            if r >= 1:
                                ve.wait_ge(sMM.h, mm_p2_at[r - 1])
                            if r == 0 and g >= 1:
                                ve.wait_ge(sMM.h, g_mm_done[g - 1])
                            if r == 0 and g == 0:
                                ve.wait_ge(sP.h, p_vw_zero)
                                ve.wait_ge(sS.h, s_stage_all)
                            for j in range(4):
                                i = nc.vector.tensor_reduce(
                                    ZS[:, j * 32:(j + 1) * 32],
                                    ESB[:, j * 512:(j + 1) * 512].rearrange("q (b h) -> q b h", h=16),
                                    AX.X, ALU.add)
                            vsync(ve, i)
                            vsync(ve, nc.vector.reciprocal(RZS[:], ZS[:]))
                            for j in range(4):
                                i = nc.vector.tensor_tensor(
                                    PACK.rearrange("q (bq bl x) -> q bq bl x", bq=8, bl=4)[:, :, :, 32 * j:32 * j + 16],
                                    ESB[:, j * 512:(j + 1) * 512].rearrange("q (bq bl h) -> q bq bl h", bq=8, bl=4),
                                    RZS[:, j * 32:(j + 1) * 32].rearrange("q (bq bl) -> q bq bl", bq=8)
                                    .unsqueeze(3).to_broadcast([128, 8, 4, 16]),
                                    ALU.mult)
                            vsync(ve, i)
                            for pk in range(8):
                                i = nc.vector.transpose(PTB[:, pk * 512:(pk + 1) * 512],
                                                        PACK[:, pk * 512:(pk + 1) * 512])
                            sV.inc(i)
                        r_done_v[r] = sV.n
                        v_pt = sV.n

                        @block.tensor
                        def _(te, g=g, r=r, v_pt=v_pt):
                            te.wait_ge(sV.h, v_pt)
                            if r == 0:
                                te.wait_ge(sRb.h, r_vw[g])
                            if r >= 1:
                                te.wait_ge(sA.h, cp_at[r - 1])
                            for beta in range(32):
                                p = 32 * r + beta
                                pk, bl = beta // 4, beta % 4
                                for cc in range(16):
                                    j, m = cc % 4, cc // 4
                                    tp = (32 * m, 32 * j) if (j == 3 or m == 3) else None
                                    i = nc.tensor.matmul(
                                        P2[m][32 * j:32 * j + 16, 16 * beta:16 * beta + 16],
                                        V17[32 * m:32 * m + 32,
                                            j * 2048 + p: j * 2048 + p + 15 * 128 + 1: 128],
                                        PTB[32 * m:32 * m + 32,
                                            pk * 512 + 128 * bl + 32 * j: pk * 512 + 128 * bl + 32 * j + 16],
                                        start=True, stop=True, tile_position=tp,
                                        skip_group_check=True)
                            sMM.inc(i)
                        mm_p2_at[r] = sMM.n
                        mm_p2 = sMM.n

                        @block.scalar
                        def _(sc, g=g, r=r, mm_p2=mm_p2):
                            sc.wait_ge(sMM.h, mm_p2)
                            if r >= 2:
                                sc.wait_ge(sP.h, ev_gp[r - 2])
                            if r < 2 and g >= 1:
                                sc.wait_ge(sP.h, ev_gp_last[g - 1])
                            for m in range(4):
                                i = nc.scalar.activation(PCS[r % 2][:, m * 512:(m + 1) * 512],
                                                         P2[m][:], AF.Copy)
                            sA.inc(i)
                        cp_at[r] = sA.n

                        @block.gpsimd
                        def _(gp, g=g, r=r, cpa=sA.n):
                            gp.wait_ge(sA.h, cpa)
                            if r == 0 and g >= 1:
                                gp.wait_ge(sS.h, zdma_at[g - 1])
                            for m in range(4):
                                outv = (ZG.rearrange("q (bl sm h w) -> q bl sm h w",
                                                     bl=4, sm=4, h=16)[:, :, m, :, 8 * r:8 * r + 8]
                                        .rearrange("q bl h bq -> q bq bl h"))
                                in0 = PCS[r % 2][:, m * 512:(m + 1) * 512].rearrange(
                                    "q (bq bl h) -> q bq bl h", bq=8, bl=4)
                                in1 = (W18.rearrange("q (mm h p) -> q mm h p", mm=4, h=16)
                                       [:, m, :, 32 * r:32 * r + 32]
                                       .rearrange("q h (bq bl) -> q bq bl h", bq=8))
                                i = nc.gpsimd.tensor_tensor(outv, in0, in1, ALU.mult)
                            sP.inc(i)
                        ev_gp[r] = sP.n
                    g_mm_done[g] = sMM.n
                    g_evac_done[g] = sP.n
                    a_exp_last[g] = r_exp[3]
                    rdv_last[g] = r_done_v[3]
                    ev_gp_last[g] = ev_gp[3]
                if g + 1 < G4:
                    emit_wave_a(g + 1)
                    emit_wave_b(g + 1)
                p_ev = sP.n
                zhalf = []

                @block.sync
                def _(sy, g=g, p_ev=p_ev):
                    sy.wait_ge(sP.h, p_ev)
                    for sl in range(16):
                        for j in range(4):
                            sS.dma(sy.dma_start(
                                out=d_cin[g * D + sl * 64 + j * 16: g * D + sl * 64 + j * 16 + 16, :],
                                in_=ZG[32 * j:32 * j + 16, sl * 512:(sl + 1) * 512]))
                        if sl == 7:
                            zhalf.append(sS.n)
                zdma_at.append(sS.n)

                for hf in range(2):
                    @block.gpsimd
                    def _(gp, g=g, hf=hf, zat=(zhalf[0] if hf == 0 else zdma_at[g])):
                        gp.wait_ge(sS.h, zat)
                        sC.inc(nc.gpsimd.collective_compute(
                            "AllGather", ALU.bypass,
                            replica_groups=[list(range(8))],
                            ins=[d_cin[g * D + hf * 512: g * D + hf * 512 + 512, :].opt()],
                            outs=[d_cout[g * 8 * D + hf * 4096: g * 8 * D + hf * 4096 + 4096, :].opt()]))
        c_done = sC.n

        with ExitStack() as ph:
            BIX = ph.enter_context(nc.sbuf_tensor("bBIX", [128, 32], I32))
            BNDs = [ph.enter_context(nc.sbuf_tensor(f"bBND{i}", [128, 8 * 512], BF16))
                    for i in range(2)]

            @block.sync
            def _(sy):
                sS.dma(sy.dma_start(out=BIX[:], in_=d_bidx[:]))
            s_bix = sS.n

            band_loaded = {}
            band_mm = {}
            wo_epi = {}
            with ExitStack() as pp:
                pws = [pp.enter_context(nc.psum_tensor(f"bwo{i}", [128, 512], F32))
                       for i in range(8)]
                for jj in range(4):
                    @block.gpsimd
                    def _(gp, jj=jj, drain=sG.n):
                        gp.wait_ge(sC.h, c_done - 1)
                        gp.wait_ge(sS.h, s_bix)
                        gp.wait_ge(sG.h, drain)
                        if jj >= 2:
                            gp.wait_ge(sMM.h, band_mm[jj - 2])
                        for kk in range(4):
                            sG.dma(nc.gpsimd.indirect_dma_start(
                                out=BNDs[jj % 2][:, kk * 512:(kk + 1) * 512],
                                out_offset=None,
                                in_=d_cout[:],
                                in_offset=IndirectOffsetOnAxis(BIX[:, jj * 8 + kk:jj * 8 + kk + 1], 0)))
                    band_a = sG.n

                    @block.gpsimd
                    def _(gp, jj=jj, ba=band_a):
                        gp.wait_ge(sC.h, c_done)
                        gp.wait_ge(sG.h, ba)
                        for kk in range(4, 8):
                            sG.dma(nc.gpsimd.indirect_dma_start(
                                out=BNDs[jj % 2][:, kk * 512:(kk + 1) * 512],
                                out_offset=None,
                                in_=d_cout[:],
                                in_offset=IndirectOffsetOnAxis(BIX[:, jj * 8 + kk:jj * 8 + kk + 1], 0)))
                    band_loaded[jj] = sG.n

                    for dt in range(CT):
                        ps = pws[dt]

                        @block.tensor
                        def _(te, jj=jj, dt=dt, ps=ps, ba=band_a):
                            if dt == 0:
                                te.wait_ge(sG.h, ba)
                                te.wait_ge(sG.h, wb_loaded[0])
                            if jj >= 1 and dt == 0:
                                te.wait_ge(sV.h, wo_epi[jj - 1])
                            for kk in range(8):
                                if dt == 0 and kk == 4:
                                    te.wait_ge(sG.h, band_loaded[jj])
                                i = nc.tensor.matmul(
                                    ps[:],
                                    WB[0][:, kk * D + dt * 128: kk * D + dt * 128 + 128],
                                    BNDs[jj % 2][:, kk * 512:(kk + 1) * 512],
                                    start=(kk == 0), stop=(kk == 7))
                            sMM.inc(i)
                        mm_i = sMM.n

                        @block.vector
                        def _(ve, jj=jj, dt=dt, ps=ps, mm_i=mm_i):
                            ve.wait_ge(sMM.h, mm_i)
                            sl = slice(dt * T + jj * 512, dt * T + jj * 512 + 512)
                            sV.inc(nc.vector.tensor_tensor(X[:, sl], X[:, sl], ps[:], ALU.add))
                    wo_epi[jj] = sV.n
                    band_mm[jj] = sMM.n
            wb_free[0] = sMM.n

    with nc.sbuf_tensor("XLN1", [128, CT * T], BF16) as XLN:
        norm1 = emit_ln(XLN)
        emit_attn_a(XLN, norm1)
    if upto >= 2:
        with nc.sbuf_tensor("XLN2", [128, CT * T], BF16) as XLN:
            emit_ln(XLN)
            emit_ffn(XLN)
    if upto >= 3:
        qk = ExitStack()
        Q13 = qk.enter_context(nc.sbuf_tensor("bQ13", [128, 8192], BF16))
        K13 = qk.enter_context(nc.sbuf_tensor("bK13", [128, 8192], BF16))
        xs = ExitStack()
        XLN = xs.enter_context(nc.sbuf_tensor("XLN3", [128, CT * T], BF16))
        emit_ln(XLN)
        emit_attn_b(XLN, xs, sV.n, Q13, K13)
        qk.close()
    if upto >= 4:
        with nc.sbuf_tensor("XLN4", [128, CT * T], BF16) as XLN:
            emit_ln(XLN)
            emit_ffn(XLN, out_dma=True)

    @block.sync
    def _(sy):
        for s in ALL_SEMS:
            if s.n:
                sy.wait_ge(s.h, s.n)
        if upto < 4:
            sS.dma(sy.dma_start(out=d_out.rearrange("(ct p) t -> p ct t", p=128),
                                in_=X.rearrange("p (ct t) -> p ct t", ct=CT)))
        sy.wait_ge(sS.h, sS.n)

    es.close()
    return nc


def _mask_tiles():
    tiles = np.zeros((max(1, NPART), 128, 512), dtype=np.float32)
    for i, (jt, tc) in enumerate(PARTIALS):
        j = 128 * jt + np.arange(128)[:, None]
        t = 512 * tc + np.arange(512)[None, :]
        tiles[i] = (3.0 * j <= t).astype(np.float32)
    return tiles.reshape(-1, 512).astype(ml_dtypes.bfloat16)


def _wout_perm():
    perm = np.zeros(D, dtype=np.int64)
    for pm4 in range(4):
        for m in range(4):
            for j in range(4):
                for hd in range(16):
                    row = (4 * pm4 + m) * 64 + j * 16 + hd
                    cc = 4 * m + j
                    perm[row] = 256 * pm4 + 16 * cc + hd
    return perm


def _band_idx(core):
    idx = np.zeros((32, 128), dtype=np.int32)
    for jj in range(4):
        s_ = 4 * (core % 2) + jj
        g = core // 2
        for kk in range(8):
            row = kk * 128 + np.arange(128)
            idx[jj * 8 + kk] = g * (8 * D) + (row // 512) * 4096 + s_ * 512 + row % 512
    return np.ascontiguousarray(idx.T)


def kernel(L0_attn_a_x, L0_attn_a_Wq, L0_attn_a_Wk, L0_attn_a_Wv, L0_attn_a_Wo,
           L0_attn_a_bo, L0_attn_a_Wconv, L0_attn_b_w_qkvw, L0_attn_b_w_out,
           L0_ffn_gate_w, L0_ffn_gate_b, L0_ffn_expert_w, L0_ffn_expert_b):
    f32 = np.float32
    x = np.asarray(L0_attn_a_x, f32)
    com = {
        "wqT": np.ascontiguousarray(np.asarray(L0_attn_a_Wq, f32).T),
        "wkT": np.ascontiguousarray(np.asarray(L0_attn_a_Wk, f32).T),
        "wvT": np.ascontiguousarray(np.asarray(L0_attn_a_Wv, f32).T),
        "woT": np.ascontiguousarray(np.asarray(L0_attn_a_Wo, f32).T),
        "bo128": np.ascontiguousarray(np.asarray(L0_attn_a_bo, f32).reshape(CT, 128).T),
        "wconvT": np.ascontiguousarray(np.asarray(L0_attn_a_Wconv, f32).transpose(2, 1, 0)),
        "wqkvw": np.ascontiguousarray(np.asarray(L0_attn_b_w_qkvw, f32)),
        "woutS": np.ascontiguousarray(np.asarray(L0_attn_b_w_out, f32)[_wout_perm()]),
        "gw128": np.ascontiguousarray(
            np.asarray(L0_ffn_gate_w, f32).T.reshape(CT, 128, 8).transpose(1, 0, 2).reshape(128, 8 * CT)),
        "gb": np.asarray(L0_ffn_gate_b, f32).reshape(1, 8),
        "ewT": np.ascontiguousarray(
            np.asarray(L0_ffn_expert_w, f32).transpose(0, 2, 1).reshape(8 * D, D)),
        "ebP": np.ascontiguousarray(
            np.asarray(L0_ffn_expert_b, f32).reshape(8, CT, 128).transpose(0, 2, 1).reshape(8 * 128, CT)),
        "maskP": _mask_tiles(),
    }
    nc = bass.Bass("TRN2", target_bir_lowering=False, debug=False)
    build_program(nc, upto=UPTO)
    in_maps = []
    for c in range(8):
        m = dict(com)
        m["xT"] = np.ascontiguousarray(x[c].T)
        m["bandidx"] = _band_idx(c)
        in_maps.append(m)
    res = run_bass_kernel_spmd(nc, in_maps, core_ids=list(range(8)))
    out = np.stack([r["outT"] for r in res.results])
    return np.ascontiguousarray(out.transpose(0, 2, 1))


# version 11
# speedup vs baseline: 1.0157x; 1.0062x over previous
import os
import numpy as np
import ml_dtypes
from contextlib import ExitStack

import concourse.bass as bass
import concourse.mybir as mybir
from concourse.bass_utils import run_bass_kernel_spmd
from concourse.bass_core_rust import IndirectOffsetOnAxis

F32 = mybir.dt.float32
BF16 = mybir.dt.bfloat16
I32 = mybir.dt.int32
AF = mybir.ActivationFunctionType
ALU = mybir.AluOpType
AX = mybir.AxisListType

T = 2048
D = 1024
CT = 8
TCn = 4
H = 16
TK = 683
TKP = 768
JTn = 6
G4 = 4

UPTO = int(os.environ.get("MK_UPTO", "4"))
FFNSTAGE = int(os.environ.get("MK_FFNSTAGE", "3"))


class Sem:
    def __init__(self, h):
        self.h = h
        self.n = 0

    def inc(self, instr, k=1):
        instr.then_inc(self.h, k)
        self.n += k
        return instr

    def dma(self, instr):
        return self.inc(instr, 16)


def blk_state(jt, tc):
    jmin, jmax = 128 * jt, min(128 * jt + 127, TK - 1)
    tmin, tmax = 512 * tc, 512 * tc + 511
    if 3 * jmin > tmax:
        return "masked"
    if 3 * jmax <= tmin:
        return "full"
    return "partial"


PARTIALS = [(jt, tc) for jt in range(JTn) for tc in range(TCn)
            if blk_state(jt, tc) == "partial"]
NPART = len(PARTIALS)


def build_program(nc, upto=4):
    es = ExitStack()

    d_x = nc.dram_tensor("xT", [D, T], F32, kind="ExternalInput").ap()
    d_out = nc.dram_tensor("outT", [D, T], F32, kind="ExternalOutput").ap()
    d_wq = nc.dram_tensor("wqT", [D, D], F32, kind="ExternalInput").ap()
    d_wk = nc.dram_tensor("wkT", [D, D], F32, kind="ExternalInput").ap()
    d_wv = nc.dram_tensor("wvT", [D, D], F32, kind="ExternalInput").ap()
    d_wo = nc.dram_tensor("woT", [D, D], F32, kind="ExternalInput").ap()
    d_bo = nc.dram_tensor("bo128", [128, CT], F32, kind="ExternalInput").ap()
    d_wconv = nc.dram_tensor("wconvT", [3, D, D], F32, kind="ExternalInput").ap()
    d_wqkvw = nc.dram_tensor("wqkvw", [D, 4 * D], F32, kind="ExternalInput").ap()
    d_wout = nc.dram_tensor("woutS", [D, D], F32, kind="ExternalInput").ap()
    d_gw = nc.dram_tensor("gw128", [128, 8 * CT], F32, kind="ExternalInput").ap()
    d_gb = nc.dram_tensor("gb", [1, 8], F32, kind="ExternalInput").ap()
    d_ew = nc.dram_tensor("ewT", [8 * D, D], F32, kind="ExternalInput").ap()
    d_eb = nc.dram_tensor("ebP", [8 * 128, CT], F32, kind="ExternalInput").ap()
    d_mask = nc.dram_tensor("maskP", [max(1, NPART) * 128, 512], BF16, kind="ExternalInput").ap()
    d_bidx = nc.dram_tensor("bandidx", [128, 32], I32, kind="ExternalInput").ap()

    d_odr = nc.dram_tensor("o_bounce", [D, T], BF16).ap()
    d_stage = nc.dram_tensor("qkvw_stage", [G4, 8 * 128, T], BF16).ap()
    d_cin = nc.dram_tensor("cc_in", [G4 * D, 512], BF16).ap()
    d_cout = nc.dram_tensor("cc_out", [8 * G4 * D, 512], BF16).ap()

    X = es.enter_context(nc.sbuf_tensor("X", [128, CT * T], F32))
    WB = [es.enter_context(nc.sbuf_tensor(f"WB{i}", [128, CT * D], BF16)) for i in range(2)]
    ONESB = es.enter_context(nc.sbuf_tensor("ONESB", [128, 16], BF16))
    ONE128 = es.enter_context(nc.sbuf_tensor("ONE128", [128, 128], BF16))
    ONESR = es.enter_context(nc.sbuf_tensor("ONESR", [1, 128], F32))
    IOTA8 = es.enter_context(nc.sbuf_tensor("IOTA8", [1, 8], F32))
    IOTAP = es.enter_context(nc.sbuf_tensor("IOTAP", [128, 1], F32))
    GW = es.enter_context(nc.sbuf_tensor("GW", [128, 8 * CT], F32))
    GBr = es.enter_context(nc.sbuf_tensor("GBr", [1, 8], F32))
    SCR = es.enter_context(nc.sbuf_tensor("SCR", [1, 64], F32))
    EPS1 = es.enter_context(nc.sbuf_tensor("EPS1", [128, 1], F32))
    SCB = es.enter_context(nc.sbuf_tensor("SCB", [128, 16], F32))
    IDXS = es.enter_context(nc.sbuf_tensor("IDXS", [128, 18], I32))
    BG = es.enter_context(nc.sbuf_tensor("BGg", [128, 2 * CT], F32))
    CIOTA = es.enter_context(nc.sbuf_tensor("CIOTA", [128, 8], F32))
    IXS = es.enter_context(nc.sbuf_tensor("IXS", [128, 16], F32))

    sS = Sem(es.enter_context(nc.semaphore("sS")))
    sG = Sem(es.enter_context(nc.semaphore("sG")))
    sMM = Sem(es.enter_context(nc.semaphore("sMM")))
    sV = Sem(es.enter_context(nc.semaphore("sV")))
    sA = Sem(es.enter_context(nc.semaphore("sA")))
    sP = Sem(es.enter_context(nc.semaphore("sP")))
    sC = Sem(es.enter_context(nc.semaphore("sC")))
    sX = Sem(es.enter_context(nc.semaphore("sX")))
    sXg = Sem(es.enter_context(nc.semaphore("sXg")))
    sR = Sem(es.enter_context(nc.semaphore("sR")))
    sRb = Sem(es.enter_context(nc.semaphore("sRb")))
    ALL_SEMS = [sS, sG, sMM, sV, sA, sP, sC, sX, sXg, sR, sRb]

    block = es.enter_context(nc.Block())

    def vsync(ve, instr):
        sV.inc(instr)
        ve.wait_ge(sV.h, sV.n)

    def asylc(sc, instr):
        sA.inc(instr)
        sc.wait_ge(sA.h, sA.n)

    wb_free = [0, 0]
    wb_loaded = [0, 0]

    def load_weight(slot, dram_ap, cols=D, nct=CT):
        @block.gpsimd
        def _(gp, slot=slot, dram_ap=dram_ap, cols=cols, nct=nct):
            gp.wait_ge(sMM.h, wb_free[slot])
            sG.dma(gp.dma_start(
                out=WB[slot].rearrange("p (ct i) -> p ct i", ct=CT)[:, 0:nct, 0:cols],
                in_=dram_ap.rearrange("(ct p) i -> p ct i", p=128)))
        wb_loaded[slot] = sG.n

    XT = X.rearrange("p (ct t) -> p ct t", ct=CT)
    DXT = d_x.rearrange("(ct p) t -> p ct t", p=128)

    @block.sync
    def _(sy):
        sX.dma(sy.dma_start(out=XT[:, 0:3], in_=DXT[:, 0:3]))
        sS.dma(sy.dma_start(out=GW[:], in_=d_gw[:]))
        sS.dma(sy.dma_start(out=GBr[:], in_=d_gb[:]))

    s_init = sS.n

    @block.scalar
    def _(sc):
        sX.dma(sc.dma_start(out=XT[:, 3:6], in_=DXT[:, 3:6]))

    @block.gpsimd
    def _(gp):
        sXg.dma(gp.dma_start(out=XT[:, 6:8], in_=DXT[:, 6:8]))
        gp.memset(ONESB[:], 1.0)
        gp.memset(ONE128[:], 1.0)
        gp.memset(ONESR[:], 1.0)
        gp.memset(EPS1[:], 1e-5)
        nc.gpsimd.iota(IOTA8[:], pattern=[[1, 8]], base=0, channel_multiplier=0,
                       allow_small_or_imprecise_dtypes=True)
        nc.gpsimd.iota(CIOTA[:], pattern=[[128, 8]], base=0, channel_multiplier=0,
                       allow_small_or_imprecise_dtypes=True)
        sP.inc(nc.gpsimd.iota(IOTAP[:], pattern=[[0, 1]], base=0, channel_multiplier=1,
                              allow_small_or_imprecise_dtypes=True))
    p_init = sP.n

    ln_ctr = [0]

    def emit_ln(XLN):
        k = ln_ctr[0]
        ln_ctr[0] += 1
        with ExitStack() as ph:
            SQ = ph.enter_context(nc.sbuf_tensor(f"lnSQ{k}", [128, CT * T], BF16))
            VAR = SQ[:].bitcast(F32)[:, 0:T]
            MB = SQ[:, 2 * T:3 * T]
            RB = SQ[:, 3 * T:4 * T]
            M2 = RB

            a_sq = []
            v_cp = []

            @block.scalar
            def _(sc):
                for s in ALL_SEMS:
                    if s.n:
                        sc.wait_ge(s.h, s.n)
                for ct in range(CT):
                    sA.inc(nc.scalar.activation(SQ[:, ct * T:(ct + 1) * T],
                                                X[:, ct * T:(ct + 1) * T], AF.Square))
                    a_sq.append(sA.n)
            a_prep = sA.n

            @block.vector
            def _(ve):
                for s in ALL_SEMS:
                    if s.n:
                        ve.wait_ge(s.h, s.n)
                for ct in range(CT):
                    sV.inc(nc.vector.tensor_copy(XLN[:, ct * T:(ct + 1) * T],
                                                 X[:, ct * T:(ct + 1) * T]))
                    v_cp.append(sV.n)
            v_prep = sV.n

            with ExitStack() as pp:
                p_s = [pp.enter_context(nc.psum_tensor(f"lnp{k}_{i}", [128, 512], F32))
                       for i in range(8)]

                @block.tensor
                def _(te):
                    for ct in range(CT):
                        te.wait_ge(sV.h, v_cp[ct])
                        te.wait_ge(sA.h, a_sq[ct])
                        for tcn in range(TCn):
                            sl = slice(ct * T + tcn * 512, ct * T + tcn * 512 + 512)
                            nc.tensor.matmul(p_s[tcn][:], ONE128[:], XLN[:, sl],
                                             start=(ct == 0), stop=(ct == CT - 1))
                            i = nc.tensor.matmul(p_s[4 + tcn][:], ONE128[:], SQ[:, sl],
                                                 start=(ct == 0), stop=(ct == CT - 1))
                    sMM.inc(i)
                mm_sums = sMM.n

                @block.vector
                def _(ve):
                    ve.wait_ge(sMM.h, mm_sums)
                    for tcn in range(TCn):
                        sl = slice(tcn * 512, tcn * 512 + 512)
                        nc.vector.tensor_scalar_mul(MB[:, sl], p_s[tcn][:], 1.0 / D)
                        nc.vector.tensor_scalar_mul(VAR[:, sl], p_s[4 + tcn][:], 1.0 / D)
                    for tcn in range(TCn):
                        sl = slice(tcn * 512, tcn * 512 + 512)
                        nc.vector.tensor_mul(M2[:, sl], MB[:, sl], MB[:, sl])
                    for tcn in range(TCn):
                        sl = slice(tcn * 512, tcn * 512 + 512)
                        i = nc.vector.tensor_sub(VAR[:, sl], VAR[:, sl], M2[:, sl])
                    sV.inc(i)
                v_var = sV.n

            @block.scalar
            def _(sc):
                sc.wait_ge(sV.h, v_var)
                sc.wait_ge(sP.h, p_init)
                asylc(sc, nc.scalar.activation(VAR[:], VAR[:], AF.Ln, bias=EPS1[:]))
                sA.inc(nc.scalar.activation(RB[:], VAR[:], AF.Exp, scale=-0.5))
            a_rstd = sA.n

            norm = []

            @block.vector
            def _(ve):
                ve.wait_ge(sA.h, a_rstd)
                for ct in range(CT):
                    sl = slice(ct * T, (ct + 1) * T)
                    nc.vector.tensor_sub(XLN[:, sl], XLN[:, sl], MB[:])
                    sV.inc(nc.vector.tensor_mul(XLN[:, sl], XLN[:, sl], RB[:]))
                    norm.append(sV.n)
            return norm

    proj_ctr = [0]

    def emit_proj(slot, rhs_fn, o_tiles, epilogue, ep_sem, tensor_wait=None, n_k=CT,
                  epi_batch=None, k_waits=None):
        k = proj_ctr[0]
        proj_ctr[0] += 1
        items = [(ot, tcn) for ot in o_tiles for tcn in range(TCn)]
        with ExitStack() as pp:
            pss = [pp.enter_context(nc.psum_tensor(f"pj{k}_{i}", [128, 512], F32))
                   for i in range(min(8, len(items)))]
            np_ = len(pss)
            epi_at = {}
            pending = []
            for i, (ot, tcn) in enumerate(items):
                ps = pss[i % np_]

                @block.tensor
                def _(te, i=i, ot=ot, tcn=tcn, ps=ps):
                    if i == 0:
                        if k_waits is None:
                            te.wait_ge(sG.h, wb_loaded[slot])
                        if tensor_wait is not None:
                            tensor_wait(te)
                    if i >= np_:
                        te.wait_ge(ep_sem.h, epi_at[i - np_])
                    for c in range(n_k):
                        if i == 0 and k_waits is not None:
                            te.wait_ge(sG.h, k_waits[c])
                        j = nc.tensor.matmul(
                            ps[:],
                            WB[slot][:, c * D + ot * 128: c * D + ot * 128 + 128],
                            rhs_fn(c, tcn),
                            start=(c == 0), stop=(c == n_k - 1))
                    sMM.inc(j)
                mm_i = sMM.n

                if epi_batch is None:
                    epilogue(ot, tcn, ps, mm_i)
                    epi_at[i] = ep_sem.n
                else:
                    pending.append((ot, tcn, ps, mm_i, i))
                    if len(pending) == epi_batch or i == len(items) - 1:
                        base = ep_sem.n
                        epilogue(pending)
                        assert ep_sem.n == base + len(pending)
                        for kk2, (_, _, _, _, i2) in enumerate(pending):
                            epi_at[i2] = base + kk2 + 1
                        pending = []
            wb_free[slot] = sMM.n

    def emit_attn_a(XLN, norm):
        with ExitStack() as ph:
            KT = ph.enter_context(nc.sbuf_tensor("aKT", [128, CT * TKP], BF16))
            VA = ph.enter_context(nc.sbuf_tensor("aVA", [128, JTn * 1040], BF16))
            QHP = [ph.enter_context(nc.sbuf_tensor(f"aQ{i}", [128, T], BF16)) for i in range(2)]
            PTs = [ph.enter_context(nc.sbuf_tensor(f"aPT{i}", [128, JTn * 512], BF16))
                   for i in range(2)]
            RZ2 = [ph.enter_context(nc.sbuf_tensor(f"aRZ{i}", [1, 512], BF16))
                   for i in range(2)]
            RZB = ph.enter_context(nc.sbuf_tensor("aRZB", [64, 512], F32))
            RZROW = RZB[0:1, :]
            OB = [ph.enter_context(nc.sbuf_tensor(f"aOB{i}", [128, T], BF16)) for i in range(2)]
            MASKS = ph.enter_context(nc.sbuf_tensor("aMSK", [128, max(1, NPART) * 512], BF16))
            ph_k = ExitStack()
            KTMP = ph_k.enter_context(nc.sbuf_tensor("aKTMP", [128, CT * TKP], BF16))

            if NPART:
                @block.sync
                def _(sy):
                    sS.dma(sy.dma_start(out=MASKS.rearrange("p (n w) -> p n w", w=512),
                                        in_=d_mask.rearrange("(n p) w -> p n w", p=128)))
            s_mask = sS.n

            load_weight(0, d_wconv[0])

            @block.gpsimd
            def _(gp, w=[(s, s.n) for s in ALL_SEMS]):
                for s, n in w:
                    if n:
                        gp.wait_ge(s.h, n)
                sP.inc(gp.memset(
                    KTMP.rearrange("p (ct c) -> p ct c", ct=CT)[:, :, 684:TKP], 0.0))
            p_zero = sP.n

            @block.vector
            def _(ve, w=[(s, s.n) for s in ALL_SEMS]):
                for s, n in w:
                    if n:
                        ve.wait_ge(s.h, n)
                sV.inc(nc.vector.memset(
                    KT.rearrange("p (ct c) -> p ct c", ct=CT)[:, :, TK:TKP], 0.0))

            ln_done = sV.n
            with ExitStack() as pp:
                pcs = [pp.enter_context(nc.psum_tensor(f"cv{i}", [128, 341], F32))
                       for i in range(8)]
                cv_evac = [0]
                for og in range(2):
                    ots = [og * 4 + i for i in range(4)]
                    for tau in range(3):
                        slot = (og * 3 + tau) % 2
                        if not (og == 0 and tau == 0):
                            load_weight(slot, d_wconv[tau])
                        g_w = sG.n

                        @block.tensor
                        def _(te, og=og, tau=tau, slot=slot, g_w=g_w):
                            te.wait_ge(sG.h, g_w)
                            if og == 0 and tau == 0:
                                te.wait_ge(sP.h, p_zero)
                            if og == 1 and tau == 0:
                                te.wait_ge(sV.h, cv_evac[0])
                            last = None
                            for oi, ot in enumerate(ots):
                                for jc in range(2):
                                    ps = pcs[oi * 2 + jc]
                                    j0 = jc * 341
                                    for c in range(CT):
                                        if og == 0 and tau == 0 and oi == 0 and jc == 0:
                                            te.wait_ge(sV.h, norm[c])
                                        last = nc.tensor.matmul(
                                            ps[:],
                                            WB[slot][:, c * D + ot * 128: c * D + ot * 128 + 128],
                                            XLN[:, c * T + 3 * j0 + tau: c * T + 3 * j0 + tau + 3 * 341: 3],
                                            start=(tau == 0 and c == 0),
                                            stop=(tau == 2 and c == CT - 1))
                            sMM.inc(last)
                        wb_free[slot] = sMM.n
                    mm_cv = sMM.n

                    @block.vector
                    def _(ve, ots=ots, mm_cv=mm_cv):
                        ve.wait_ge(sMM.h, mm_cv)
                        for oi, ot in enumerate(ots):
                            for jc in range(2):
                                nc.vector.tensor_copy(
                                    KTMP[:, ot * TKP + 1 + jc * 341: ot * TKP + 1 + jc * 341 + 341],
                                    pcs[oi * 2 + jc][:])
                            i = nc.vector.tensor_copy(KTMP[:, ot * TKP: ot * TKP + 1],
                                                      XLN[:, ot * T: ot * T + 1])
                        sV.inc(i)
                        cv_evac[0] = sV.n
            v_ktmp = sV.n

            load_weight(0, d_wk)
            g_wk = sG.n
            with ExitStack() as pp:
                pks = [pp.enter_context(nc.psum_tensor(f"kp{i}", [128, 512], F32))
                       for i in range(8)]
                jchunks = [(0, 512), (512, 171)]
                items = [(ot, jci) for ot in range(CT) for jci in range(2)]
                epi_at = {}
                for i, (ot, jci) in enumerate(items):
                    j0, n = jchunks[jci]
                    ps = pks[i % 8]

                    @block.tensor
                    def _(te, i=i, ot=ot, j0=j0, n=n, ps=ps):
                        if i == 0:
                            te.wait_ge(sG.h, g_wk)
                            te.wait_ge(sV.h, v_ktmp)
                        if i >= 8:
                            te.wait_ge(sV.h, epi_at[i - 8])
                        for c in range(CT):
                            j = nc.tensor.matmul(
                                ps[:, 0:n],
                                WB[0][:, c * D + ot * 128: c * D + ot * 128 + 128],
                                KTMP[:, c * TKP + j0: c * TKP + j0 + n],
                                start=(c == 0), stop=(c == CT - 1))
                        sMM.inc(j)
                    mm_i = sMM.n

                    @block.vector
                    def _(ve, ot=ot, j0=j0, n=n, ps=ps, mm_i=mm_i):
                        ve.wait_ge(sMM.h, mm_i)
                        sV.inc(nc.vector.tensor_copy(KT[:, ot * TKP + j0: ot * TKP + j0 + n],
                                                     ps[:, 0:n]))
                    epi_at[i] = sV.n
                wb_free[0] = sMM.n

            load_weight(1, d_wv)
            g_wv = sG.n
            v_kdone = sV.n
            with ExitStack() as pp:
                pvs = [pp.enter_context(nc.psum_tensor(f"vp{i}", [128, 512], F32))
                       for i in range(8)]
                items = [(jt, ic) for jt in range(JTn) for ic in range(2)]
                epi_at = {}
                for i, (jt, ic) in enumerate(items):
                    ps = pvs[i % 8]

                    @block.tensor
                    def _(te, i=i, jt=jt, ic=ic, ps=ps):
                        if i == 0:
                            te.wait_ge(sG.h, g_wv)
                            te.wait_ge(sV.h, v_kdone)
                        if i >= 8:
                            te.wait_ge(sV.h, epi_at[i - 8])
                        for c in range(CT):
                            j = nc.tensor.matmul(
                                ps[:],
                                KTMP[:, c * TKP + jt * 128: c * TKP + jt * 128 + 128],
                                WB[1][:, c * D + ic * 512: c * D + ic * 512 + 512],
                                start=(c == 0), stop=(c == CT - 1))
                        sMM.inc(j)
                    mm_i = sMM.n

                    @block.vector
                    def _(ve, jt=jt, ic=ic, ps=ps, mm_i=mm_i):
                        ve.wait_ge(sMM.h, mm_i)
                        sV.inc(nc.vector.tensor_copy(
                            VA[:, jt * 1040 + (8 * ic) * 65: jt * 1040 + (8 * ic + 8) * 65]
                            .rearrange("p (h c) -> p h c", h=8)[:, :, 0:64],
                            ps[:].rearrange("p (h c) -> p h c", h=8)))
                    epi_at[i] = sV.n
                wb_free[1] = sMM.n

                @block.vector
                def _(ve):
                    sV.inc(nc.vector.tensor_copy(
                        VA.rearrange("p (jt x) -> p jt x", jt=JTn)[:, :, 64:1040:65],
                        ONESB[:, 0:1].to_broadcast([128, JTn, 16])))
            v_va = sV.n

            ph_k.close()
            load_weight(0, d_wq)
            g_wq = sG.n
            with ExitStack() as pp:
                psc = [pp.enter_context(nc.psum_tensor(f"sc{i}", [128, 512], F32))
                       for i in range(4)]
                pso = [pp.enter_context(nc.psum_tensor(f"so{i}", [65, 512], F32))
                       for i in range(2)]
                psz = [pp.enter_context(nc.psum_tensor(f"szz{i}", [64, 512], F32))
                       for i in range(2)]
                sc_use = [0]
                sc_free = {}
                q_epi = {}
                ob_store = {}
                o_free = {}
                z_free = {}
                v_q = {}
                mm_o = {}
                mm_zd = {}
                it_state = {}
                iters = [(hp, hh, tcn) for hp in range(8) for hh in range(2)
                         for tcn in range(TCn)]
                n_it = len(iters)

                def emit_qproj(hp):
                    qb = QHP[hp % 2]
                    for tcn in range(TCn):
                        u = sc_use[0]
                        ps = psc[u % 4]
                        sc_use[0] += 1

                        @block.tensor
                        def _(te, hp=hp, tcn=tcn, ps=ps, u=u):
                            if hp == 0 and tcn == 0:
                                te.wait_ge(sG.h, g_wq)
                                te.wait_ge(sV.h, v_va)
                            if u >= 4:
                                fr = sc_free[u - 4]
                                te.wait_ge(fr[0].h, fr[1])
                            if hp >= 2 and tcn == 0:
                                te.wait_ge(sA.h, q_epi[hp - 2])
                            for c in range(CT):
                                j = nc.tensor.matmul(
                                    ps[:],
                                    WB[0][:, c * D + hp * 128: c * D + hp * 128 + 128],
                                    XLN[:, c * T + tcn * 512: c * T + tcn * 512 + 512],
                                    start=(c == 0), stop=(c == CT - 1))
                            sMM.inc(j)
                        mm_i = sMM.n

                        @block.vector
                        def _(ve, qb=qb, tcn=tcn, ps=ps, mm_i=mm_i):
                            ve.wait_ge(sMM.h, mm_i)
                            sV.inc(nc.vector.tensor_copy(qb[:, tcn * 512:(tcn + 1) * 512], ps[:]))
                        sc_free[u] = (sV, sV.n)
                    v_q[hp] = sV.n

                def emit_lagged(n):
                    po, pz, hp, h, tcn, a_rz_n = it_state[n]

                    @block.tensor
                    def _(te, n=n, pz=pz, a_rz_n=a_rz_n):
                        if n >= 2:
                            fr = z_free[n - 2]
                            te.wait_ge(fr[0].h, fr[1])
                        te.wait_ge(sA.h, a_rz_n)
                        sMM.inc(nc.tensor.matmul(pz[:], ONE128[0:1, 0:64], RZ2[n % 2][:],
                                                 start=True, stop=True))
                    mm_zd[n] = sMM.n

                    @block.vector
                    def _(ve, n=n, po=po, pz=pz, hp=hp, h=h, tcn=tcn):
                        ve.wait_ge(sMM.h, mm_zd[n])
                        if h % 2 == 0 and tcn == 0 and hp >= 2:
                            ve.wait_ge(sS.h, ob_store[hp - 2])
                        nc.vector.tensor_copy(RZB[:], pz[:])
                        sV.inc(nc.vector.tensor_mul(
                            OB[hp % 2][(h % 2) * 64:(h % 2) * 64 + 64, tcn * 512:(tcn + 1) * 512],
                            po[0:64, :], RZB[:]))
                    o_free[n] = (sV, sV.n)
                    z_free[n] = (sV, sV.n)
                    if h % 2 == 1 and tcn == TCn - 1:
                        v_ob = sV.n

                        @block.sync
                        def _(sy, hp=hp, v_ob=v_ob):
                            if hp >= 2:
                                sy.wait_ge(sS.h, ob_store[hp - 2])
                            sy.wait_ge(sV.h, v_ob)
                            sS.dma(sy.dma_start(out=d_odr[hp * 128:(hp + 1) * 128, :],
                                                in_=OB[hp % 2][:]))
                        ob_store[hp] = sS.n

                def emit_sc_z(n):
                    po = it_state[n][0]

                    @block.scalar
                    def _(sc, n=n, po=po):
                        sc.wait_ge(sMM.h, mm_o[n])
                        if n >= 2:
                            sc.wait_ge(sMM.h, mm_zd[n - 2])
                        nc.scalar.activation(RZROW[:], po[64:65, :], AF.Ln)
                        sA.inc(nc.scalar.activation(RZ2[n % 2][:], RZROW[:], AF.Exp,
                                                    scale=-1.0))

                for n, (hp, hh, tcn) in enumerate(iters):
                    if hh == 0 and tcn == 0:
                        emit_qproj(hp)
                    qb = QHP[hp % 2]
                    h = 2 * hp + hh
                    kbase = (h // 2) * TKP
                    krow = (h % 2) * 64
                    PTb = PTs[n % 2]
                    live = [jt for jt in range(JTn) if blk_state(jt, tcn) != "masked"]
                    ntcn = iters[n + 1][2] if n + 1 < n_it else None
                    nlive = (len([jt for jt in range(JTn) if blk_state(jt, ntcn) != "masked"])
                             if ntcn is not None else 0)
                    S_n = sA.n
                    a_exp_at = {jt: S_n + ji + 1 for ji, jt in enumerate(live)}
                    a_last = S_n + len(live)
                    a_rz_n = S_n + len(live) + (1 if n >= 1 else 0) + nlive + 1
                    mm_sc_at = {}
                    psc_map = []
                    for ji, jt in enumerate(live):
                        u = sc_use[0]
                        ps = psc[u % 4]
                        sc_use[0] += 1
                        psc_map.append(ps)

                        @block.tensor
                        def _(te, jt=jt, ps=ps, u=u, vq=v_q[hp], kbase=kbase,
                              krow=krow, qb=qb, tcn=tcn):
                            if u >= 4:
                                fr = sc_free[u - 4]
                                te.wait_ge(fr[0].h, fr[1])
                            te.wait_ge(sV.h, vq)
                            sMM.inc(nc.tensor.matmul(
                                ps[:],
                                KT[krow:krow + 64, kbase + jt * 128: kbase + jt * 128 + 128],
                                qb[krow:krow + 64, tcn * 512:(tcn + 1) * 512],
                                start=True, stop=True))
                        mm_sc_at[jt] = sMM.n
                        sc_free[u] = (sA, a_exp_at[jt])
                    v_mask = sV.n
                    parts = [jt for jt in live if blk_state(jt, tcn) == "partial"]
                    if parts:
                        @block.vector
                        def _(ve, parts=parts, tcn=tcn, a_exp_at=dict(a_exp_at), PTb=PTb):
                            ve.wait_ge(sS.h, s_mask)
                            for jt in parts:
                                pi = PARTIALS.index((jt, tcn))
                                ve.wait_ge(sA.h, a_exp_at[jt])
                                i = nc.vector.tensor_mul(
                                    PTb[:, jt * 512:(jt + 1) * 512],
                                    PTb[:, jt * 512:(jt + 1) * 512],
                                    MASKS[:, pi * 512:(pi + 1) * 512])
                            sV.inc(i)
                        v_mask = sV.n
                    po = pso[n % 2]
                    pz = psz[n % 2]

                    @block.tensor
                    def _(te, n=n, h=h, live=live, po=po, v_mask=v_mask,
                          a_last=a_last, PTb=PTb):
                        if n >= 2:
                            fr = o_free[n - 2]
                            te.wait_ge(fr[0].h, fr[1])
                        te.wait_ge(sA.h, a_last)
                        te.wait_ge(sV.h, v_mask)
                        for ci, jt in enumerate(live):
                            j = nc.tensor.matmul(
                                po[:],
                                VA[:, jt * 1040 + 65 * h: jt * 1040 + 65 * h + 65],
                                PTb[:, jt * 512:(jt + 1) * 512],
                                start=(ci == 0), stop=(ci == len(live) - 1))
                        sMM.inc(j)
                    mm_o[n] = sMM.n
                    it_state[n] = (po, pz, hp, h, tcn, a_rz_n)
                    if n >= 1:
                        emit_lagged(n - 1)

                    @block.scalar
                    def _(sc, n=n, live=live, mm_sc_at=dict(mm_sc_at),
                          psc_map=list(psc_map), PTb=PTb):
                        if n >= 2:
                            sc.wait_ge(sMM.h, mm_o[n - 2])
                        for ji, jt in enumerate(live):
                            sc.wait_ge(sMM.h, mm_sc_at[jt])
                            sA.inc(nc.scalar.activation(
                                PTb[:, jt * 512:(jt + 1) * 512], psc_map[ji][:],
                                AF.Exp, scale=0.125))
                    assert sA.n == a_last, (sA.n, a_last)
                    if n >= 1:
                        emit_sc_z(n - 1)
                        assert sA.n == it_state[n - 1][5], (sA.n, it_state[n - 1][5])
                    if h % 2 == 1 and tcn == TCn - 1:
                        q_epi[hp] = sA.n
                emit_lagged(n_it - 1)
                emit_sc_z(n_it - 1)
                assert sA.n == it_state[n_it - 1][5], (sA.n, it_state[n_it - 1][5])
                wb_free[0] = sMM.n
            s_oall = sS.n

        if True:
            load_weight(1, d_wo)
            with ExitStack() as op:
                OCH = op.enter_context(nc.sbuf_tensor("aOCH", [128, CT * T], BF16))
                BO = op.enter_context(nc.sbuf_tensor("aBO2", [128, CT], F32))

                @block.sync
                def _(sy):
                    sS.dma(sy.dma_start(out=BO[:], in_=d_bo[:]))
                s_bo = sS.n

                @block.sync
                def _(sy):
                    sy.wait_ge(sS.h, s_oall)
                    sS.dma(sy.dma_start(out=OCH.rearrange("p (ct t) -> p ct t", ct=CT),
                                        in_=d_odr.rearrange("(ct p) t -> p ct t", p=128)))
                s_och = sS.n

                def epi(ot, tcn, ps, mm_i):
                    @block.vector
                    def _(ve, ot=ot, tcn=tcn, ps=ps, mm_i=mm_i):
                        ve.wait_ge(sMM.h, mm_i)
                        ve.wait_ge(sS.h, s_bo)
                        sV.inc(nc.vector.scalar_tensor_tensor(
                            X[:, ot * T + tcn * 512: ot * T + tcn * 512 + 512],
                            ps[:], BO[:, ot:ot + 1],
                            X[:, ot * T + tcn * 512: ot * T + tcn * 512 + 512],
                            ALU.add, ALU.add))

                emit_proj(1, lambda c, tcn: OCH[:, c * T + tcn * 512: c * T + tcn * 512 + 512],
                          list(range(CT)), epi, sV,
                          tensor_wait=lambda te: te.wait_ge(sS.h, s_och))

    ffn_ctr = [0]

    def emit_ffn(XLN, out_dma=False):
        k = ffn_ctr[0]
        ffn_ctr[0] += 1
        with ExitStack() as ph:
            POOL = ph.enter_context(nc.sbuf_tensor(f"fPO{k}", [128, CT], F32))
            LOGI = ph.enter_context(nc.sbuf_tensor(f"fLG{k}", [1, 8], F32))
            TOP8 = ph.enter_context(nc.sbuf_tensor(f"fT8{k}", [1, 8], F32))
            TMP8 = ph.enter_context(nc.sbuf_tensor(f"fTM{k}", [1, 24], F32))
            G1B = ph.enter_context(nc.sbuf_tensor(f"fG1{k}", [128, CT * T], BF16))
            G2B = ph.enter_context(nc.sbuf_tensor(f"fG2{k}", [128, 8 * 512], BF16))

            @block.vector
            def _(ve):
                for ct in range(CT):
                    i = nc.vector.tensor_reduce(POOL[:, ct:ct + 1], XLN[:, ct * T:(ct + 1) * T],
                                                AX.X, ALU.add)
                vsync(ve, i)
                sV.inc(nc.vector.tensor_scalar_mul(POOL[:], POOL[:], 1.0 / T))
            v_pool = sV.n

            with ExitStack() as pp:
                pl = pp.enter_context(nc.psum_tensor(f"fpl{k}", [1, 8], F32))
                pb = pp.enter_context(nc.psum_tensor(f"fpb{k}", [128, 8], F32))

                @block.tensor
                def _(te):
                    te.wait_ge(sV.h, v_pool)
                    te.wait_ge(sS.h, s_init)
                    for c in range(CT):
                        j = nc.tensor.matmul(pl[:], POOL[:, c:c + 1], GW[:, c * 8:(c + 1) * 8],
                                             start=(c == 0), stop=(c == CT - 1))
                    sMM.inc(j)
                mm_lg = sMM.n

                @block.vector
                def _(ve):
                    ve.wait_ge(sMM.h, mm_lg)
                    vsync(ve, nc.vector.tensor_add(LOGI[:], pl[:], GBr[:]))
                    vsync(ve, nc.vector.max(out=TOP8[:], in_=LOGI[:]))
                    vsync(ve, nc.vector.tensor_single_scalar(TMP8[:, 0:8], LOGI[:], TOP8[:, 1:2], ALU.is_ge))
                    vsync(ve, nc.vector.tensor_single_scalar(TMP8[:, 8:16], LOGI[:], TOP8[:, 0:1], ALU.is_ge))
                v_keep = sV.n

                @block.scalar
                def _(sc):
                    sc.wait_ge(sV.h, v_keep)
                    sA.inc(nc.scalar.activation(TMP8[:, 16:24], LOGI[:], AF.Exp))
                a_exp = sA.n

                @block.vector
                def _(ve):
                    ve.wait_ge(sA.h, a_exp)
                    vsync(ve, nc.vector.tensor_mul(TMP8[:, 16:24], TMP8[:, 16:24], TMP8[:, 0:8]))
                    vsync(ve, nc.vector.tensor_reduce(SCR[:, 0:1], TMP8[:, 16:24], AX.X, ALU.add))
                    vsync(ve, nc.vector.reciprocal(SCR[:, 1:2], SCR[:, 0:1]))
                    vsync(ve, nc.vector.tensor_single_scalar(TMP8[:, 16:24], TMP8[:, 16:24], SCR[:, 1:2], ALU.mult))
                    vsync(ve, nc.vector.tensor_mul(SCR[:, 8:16], IOTA8[:], TMP8[:, 8:16]))
                    vsync(ve, nc.vector.tensor_reduce(SCR[:, 2:3], SCR[:, 8:16], AX.X, ALU.add))
                    vsync(ve, nc.vector.tensor_mul(SCR[:, 8:16], IOTA8[:], TMP8[:, 0:8]))
                    vsync(ve, nc.vector.tensor_reduce(SCR[:, 3:4], SCR[:, 8:16], AX.X, ALU.add))
                    vsync(ve, nc.vector.tensor_sub(SCR[:, 3:4], SCR[:, 3:4], SCR[:, 2:3]))
                    vsync(ve, nc.vector.tensor_mul(SCR[:, 8:16], TMP8[:, 16:24], TMP8[:, 8:16]))
                    vsync(ve, nc.vector.tensor_reduce(SCR[:, 4:5], SCR[:, 8:16], AX.X, ALU.add))
                    vsync(ve, nc.vector.tensor_scalar(SCR[:, 5:6], SCR[:, 4:5], -1.0, 1.0, ALU.mult, ALU.add))
                    nc.vector.tensor_scalar_mul(SCR[:, 16:17], SCR[:, 2:3], 1024.0)
                    nc.vector.tensor_scalar_mul(SCR[:, 17:18], SCR[:, 3:4], 1024.0)
                    nc.vector.tensor_scalar_mul(SCR[:, 18:19], SCR[:, 2:3], 128.0)
                    nc.vector.tensor_scalar_mul(SCR[:, 19:20], SCR[:, 3:4], 128.0)
                    nc.vector.tensor_copy(SCR[:, 20:21], SCR[:, 4:5])
                    vsync(ve, nc.vector.tensor_copy(SCR[:, 21:22], SCR[:, 5:6]))
                v_gate = sV.n

                @block.tensor
                def _(te):
                    te.wait_ge(sV.h, v_gate)
                    sMM.inc(nc.tensor.matmul(pb[:, 0:6], ONESR[:], SCR[0:1, 16:22],
                                             start=True, stop=True))
                mm_bc = sMM.n

                @block.vector
                def _(ve):
                    ve.wait_ge(sMM.h, mm_bc)
                    ve.wait_ge(sP.h, p_init)
                    vsync(ve, nc.vector.tensor_copy(SCB[:, 0:6], pb[:, 0:6]))
                    nc.vector.tensor_add(SCB[:, 6:7], SCB[:, 0:1], IOTAP[:])
                    nc.vector.tensor_add(SCB[:, 7:8], SCB[:, 1:2], IOTAP[:])
                    nc.vector.tensor_add(SCB[:, 8:9], SCB[:, 2:3], IOTAP[:])
                    vsync(ve, nc.vector.tensor_add(SCB[:, 9:10], SCB[:, 3:4], IOTAP[:]))
                    nc.vector.tensor_tensor(IXS[:, 0:8], SCB[:, 6:7].to_broadcast([128, 8]),
                                            CIOTA[:], ALU.add)
                    vsync(ve, nc.vector.tensor_tensor(IXS[:, 8:16], SCB[:, 7:8].to_broadcast([128, 8]),
                                                      CIOTA[:], ALU.add))
                    vsync(ve, nc.vector.tensor_copy(IDXS[:, 0:16], IXS[:]))
                    nc.vector.tensor_copy(IDXS[:, 16:17], SCB[:, 8:9])
                    sV.inc(nc.vector.tensor_copy(IDXS[:, 17:18], SCB[:, 9:10]))
                v_idx = sV.n

            if FFNSTAGE < 2:
                return
            gmarks = {0: [], 1: []}
            for s_ in range(2):
                @block.gpsimd
                def _(gp, s_=s_):
                    gp.wait_ge(sV.h, v_idx)
                    gp.wait_ge(sMM.h, wb_free[s_])
                    for c in range(CT):
                        sG.dma(nc.gpsimd.indirect_dma_start(
                            out=WB[s_][:, c * D:(c + 1) * D],
                            out_offset=None,
                            in_=d_ew[:],
                            in_offset=IndirectOffsetOnAxis(IDXS[:, s_ * 8 + c:s_ * 8 + c + 1], 0)))
                        gmarks[s_].append(sG.n)
                    sG.dma(nc.gpsimd.indirect_dma_start(
                        out=BG[:, s_ * CT:(s_ + 1) * CT],
                        out_offset=None,
                        in_=d_eb[:],
                        in_offset=IndirectOffsetOnAxis(IDXS[:, 16 + s_:17 + s_], 0)))
                wb_loaded[s_] = sG.n

            if FFNSTAGE < 3:
                return
            def epi1(batch):
                @block.scalar
                def _(sc, batch=list(batch)):
                    sc.wait_ge(sG.h, wb_loaded[0])
                    for (ot, tcn, ps, mm_i, _) in batch:
                        sc.wait_ge(sMM.h, mm_i)
                        sA.inc(nc.scalar.activation(
                            G1B[:, ot * T + tcn * 512: ot * T + tcn * 512 + 512],
                            ps[:], AF.Gelu, bias=BG[:, ot:ot + 1]))

            emit_proj(0, lambda c, tcn: XLN[:, c * T + tcn * 512: c * T + tcn * 512 + 512],
                      list(range(CT)), epi1, sA, epi_batch=8)

            def epi2(batch):
                a_base = sA.n

                @block.scalar
                def _(sc, batch=list(batch), vprev=cmb_done[0]):
                    sc.wait_ge(sG.h, wb_loaded[1])
                    sc.wait_ge(sV.h, vprev)
                    for sl_i, (ot, tcn, ps, mm_i, _) in enumerate(batch):
                        sc.wait_ge(sMM.h, mm_i)
                        sA.inc(nc.scalar.activation(G2B[:, sl_i * 512:(sl_i + 1) * 512],
                                                    ps[:], AF.Gelu,
                                                    bias=BG[:, CT + ot:CT + ot + 1]))

                @block.vector
                def _(ve, batch=list(batch), a_base=a_base):
                    for sl_i, (ot, tcn, ps, mm_i, _) in enumerate(batch):
                        ve.wait_ge(sA.h, a_base + sl_i + 1)
                        sl = slice(ot * T + tcn * 512, ot * T + tcn * 512 + 512)
                        nc.vector.scalar_tensor_tensor(X[:, sl], G1B[:, sl], SCB[:, 4:5],
                                                       X[:, sl], ALU.mult, ALU.add)
                        sV.inc(nc.vector.scalar_tensor_tensor(X[:, sl], G2B[:, sl_i * 512:(sl_i + 1) * 512],
                                                              SCB[:, 5:6],
                                                              X[:, sl], ALU.mult, ALU.add))
                cmb_done[0] = sV.n

                if out_dma:
                    @block.sync
                    def _(sy, batch=list(batch), vmark=cmb_done[0]):
                        sy.wait_ge(sV.h, vmark)
                        for ot in sorted(set(b[0] for b in batch)):
                            sS.dma(sy.dma_start(
                                out=d_out.rearrange("(ct p) t -> p ct t", p=128)[:, ot],
                                in_=X[:, ot * T:(ot + 1) * T]))

            cmb_done = [sV.n]
            emit_proj(1, lambda c, tcn: XLN[:, c * T + tcn * 512: c * T + tcn * 512 + 512],
                      list(range(CT)), epi2, sV, epi_batch=8)

    def emit_attn_b(XLN, xs, ln_done, Q13, K13):
        s_stage_g = []
        pqz = [0]
        with ExitStack() as ph:
            BNC = [ph.enter_context(nc.sbuf_tensor(f"bBN{i}", [128, 512], BF16)) for i in range(4)]
            bnc_free = {}
            bnc_use = [0]
            for g in range(G4):
                load_weight(g % 2, d_wqkvw[:, g * D:(g + 1) * D])
                if g == 0:
                    @block.gpsimd
                    def _(gp, w=[(s, s.n) for s in ALL_SEMS]):
                        for s, n in w:
                            if n:
                                gp.wait_ge(s.h, n)
                        gp.memset(Q13[:], 0.0)
                        sP.inc(gp.memset(K13[:], 0.0))
                    pqz[0] = sP.n

                def epi(ot, tcn, ps, mm_i, g=g):
                    u = bnc_use[0]
                    bnc_use[0] += 1
                    bn = BNC[u % 4]

                    @block.vector
                    def _(ve, ot=ot, tcn=tcn, ps=ps, mm_i=mm_i, bn=bn, u=u):
                        ve.wait_ge(sMM.h, mm_i)
                        if u >= 4:
                            ve.wait_ge(sS.h, bnc_free[u - 4])
                        sV.inc(nc.vector.tensor_copy(bn[:], ps[:]))
                    v_i = sV.n

                    @block.sync
                    def _(sy, ot=ot, tcn=tcn, bn=bn, v_i=v_i, g=g):
                        sy.wait_ge(sV.h, v_i)
                        sS.dma(sy.dma_start(
                            out=d_stage[g, ot * 128:(ot + 1) * 128, tcn * 512:(tcn + 1) * 512],
                            in_=bn[:]))
                    bnc_free[bnc_use[0] - 1] = sS.n

                def twait(te, g=g):
                    te.wait_ge(sV.h, ln_done)
                    if g == 0:
                        te.wait_ge(sA.h, sA.n)
                emit_proj(g % 2, lambda c, tcn: XLN[:, c * T + tcn * 512: c * T + tcn * 512 + 512],
                          list(range(CT)), epi, sV, tensor_wait=twait)
                s_stage_g.append(sS.n)
            s_stage_all = sS.n
        xs.close()
        b1_end = sMM.n
        b1_v_end = sV.n
        p_qk_zero = pqz[0]

        load_weight(0, d_wout)

        with ExitStack() as ph:
            V17 = ph.enter_context(nc.sbuf_tensor("bV17", [128, 8192], BF16))
            W18 = ph.enter_context(nc.sbuf_tensor("bW18", [128, 8192], BF16))
            ESB = ph.enter_context(nc.sbuf_tensor("bESB", [128, 2048], BF16))
            ZS = ph.enter_context(nc.sbuf_tensor("bZS", [128, 128], F32))
            RZS = ph.enter_context(nc.sbuf_tensor("bRZS", [128, 128], F32))
            PACK = ph.enter_context(nc.sbuf_tensor("bPACK", [128, 4096], BF16))
            PTB = ph.enter_context(nc.sbuf_tensor("bPTB", [128, 4096], BF16))
            ZG = ph.enter_context(nc.sbuf_tensor("bZG", [128, 8192], BF16))
            PCS = [ph.enter_context(nc.sbuf_tensor(f"bPC{i}", [128, 2048], BF16))
                   for i in range(2)]

            st4s = [d_stage[g].rearrange("(cc hd) (h p) -> cc hd h p", hd=16, p=128)
                    for g in range(G4)]
            r_qk = {}
            r_vw = {}
            mm_p1_done = {}
            a_exp_last = {}
            rdv_last = {}
            ev_gp_last = {}
            zdma_at = []
            g_mm_done = {}
            g_evac_done = {}
            pvz = [0]

            def emit_wave_a(g):
                @block.scalar
                def _(sc, g=g):
                    sc.wait_ge(sP.h, p_qk_zero)
                    sc.wait_ge(sS.h, s_stage_g[g])
                    if g >= 1:
                        sc.wait_ge(sMM.h, mm_p1_done[g - 1])
                    st4 = st4s[g]
                    for j in range(4):
                        sR.dma(sc.dma_start(
                            out=Q13[32 * j:32 * j + 16, :].rearrange("hd (m h p) -> hd m h p", m=4, h=16),
                            in_=st4[j:16:4].transpose([1, 0, 2, 3])))
                        sR.dma(sc.dma_start(
                            out=K13[32 * j:32 * j + 16, :].rearrange("hd (m h p) -> hd m h p", m=4, h=16),
                            in_=st4[16 + j:32:4].transpose([1, 0, 2, 3])))
                r_qk[g] = sR.n

            def emit_wave_b(g):
                @block.gpsimd
                def _(gp, g=g):
                    gp.wait_ge(sP.h, pvz[0])
                    gp.wait_ge(sS.h, s_stage_g[g])
                    if g >= 1:
                        gp.wait_ge(sMM.h, g_mm_done[g - 1])
                        gp.wait_ge(sP.h, g_evac_done[g - 1])
                    st4 = st4s[g]
                    for j in range(4):
                        sRb.dma(gp.dma_start(
                            out=W18[32 * j:32 * j + 16, :].rearrange("hd (m h p) -> hd m h p", m=4, h=16),
                            in_=st4[48 + j:64:4].transpose([1, 0, 2, 3])))
                    for m in range(4):
                        sRb.dma(gp.dma_start(
                            out=V17[32 * m:32 * m + 16, :].rearrange("h (j hd p) -> h j hd p", j=4, hd=16),
                            in_=st4[32 + 4 * m:32 + 4 * m + 4].transpose([2, 0, 1, 3])))
                r_vw[g] = sRb.n

            emit_wave_a(0)

            @block.gpsimd
            def _(gp, w=[(s, s.n) for s in ALL_SEMS]):
                for s, n in w:
                    if n:
                        gp.wait_ge(s.h, n)
                gp.memset(V17[:], 0.0)
                gp.memset(W18[:], 0.0)
                sP.inc(gp.memset(PACK[:], 0.0))
            pvz[0] = sP.n
            p_vw_zero = pvz[0]

            emit_wave_b(0)

            for g in range(G4):

                with ExitStack() as pp:
                    P1 = [pp.enter_context(nc.psum_tensor(f"bP1{g}_{j}", [128, 512], F32))
                          for j in range(4)]
                    P2 = [pp.enter_context(nc.psum_tensor(f"bP2{g}_{m}", [128, 512], F32))
                          for m in range(4)]
                    r_done_v = {}
                    cp_at = {}
                    ev_gp = {}
                    r_exp = {}
                    r_ev = {}
                    mm_p2_at = {}
                    for r in range(4):
                        @block.tensor
                        def _(te, g=g, r=r):
                            if r == 0:
                                te.wait_ge(sR.h, r_qk[g])
                                if g == 0:
                                    te.wait_ge(sMM.h, b1_end)
                                    te.wait_ge(sV.h, b1_v_end)
                                else:
                                    te.wait_ge(sA.h, a_exp_last[g - 1])
                            if r >= 1:
                                te.wait_ge(sA.h, r_exp[r - 1])
                            for beta in range(32):
                                p = 32 * r + beta
                                for cc in range(16):
                                    j, m = cc % 4, cc // 4
                                    tp = (32 * j, 32 * m) if (j == 3 or m == 3) else None
                                    i = nc.tensor.matmul(
                                        P1[j][32 * m:32 * m + 16, 16 * beta:16 * beta + 16],
                                        Q13[32 * j:32 * j + 32,
                                            m * 2048 + p: m * 2048 + p + 15 * 128 + 1: 128],
                                        K13[32 * j:32 * j + 32,
                                            m * 2048 + p: m * 2048 + p + 15 * 128 + 1: 128],
                                        start=True, stop=True, tile_position=tp,
                                        skip_group_check=True)
                            sMM.inc(i)
                        mm_p1 = sMM.n
                        if r == 3:
                            mm_p1_done[g] = sMM.n

                        @block.scalar
                        def _(sc, g=g, r=r, mm_p1=mm_p1):
                            sc.wait_ge(sMM.h, mm_p1)
                            if r >= 1:
                                sc.wait_ge(sV.h, r_done_v[r - 1])
                            if r == 0 and g >= 1:
                                sc.wait_ge(sV.h, rdv_last[g - 1])
                            if r == 0 and g == 0:
                                sc.wait_ge(sS.h, s_stage_all)
                            for j in range(4):
                                i = nc.scalar.activation(ESB[:, j * 512:(j + 1) * 512],
                                                         P1[j][:], AF.Exp, scale=0.25)
                            sA.inc(i)
                        r_exp[r] = sA.n

                        @block.vector
                        def _(ve, g=g, r=r, a_e=sA.n):
                            ve.wait_ge(sA.h, a_e)
                            if r >= 1:
                                ve.wait_ge(sMM.h, mm_p2_at[r - 1])
                            if r == 0 and g >= 1:
                                ve.wait_ge(sMM.h, g_mm_done[g - 1])
                            if r == 0 and g == 0:
                                ve.wait_ge(sP.h, p_vw_zero)
                                ve.wait_ge(sS.h, s_stage_all)
                            for j in range(4):
                                i = nc.vector.tensor_reduce(
                                    ZS[:, j * 32:(j + 1) * 32],
                                    ESB[:, j * 512:(j + 1) * 512].rearrange("q (b h) -> q b h", h=16),
                                    AX.X, ALU.add)
                            vsync(ve, i)
                            vsync(ve, nc.vector.reciprocal(RZS[:], ZS[:]))
                            for j in range(4):
                                i = nc.vector.tensor_tensor(
                                    PACK.rearrange("q (bq bl x) -> q bq bl x", bq=8, bl=4)[:, :, :, 32 * j:32 * j + 16],
                                    ESB[:, j * 512:(j + 1) * 512].rearrange("q (bq bl h) -> q bq bl h", bq=8, bl=4),
                                    RZS[:, j * 32:(j + 1) * 32].rearrange("q (bq bl) -> q bq bl", bq=8)
                                    .unsqueeze(3).to_broadcast([128, 8, 4, 16]),
                                    ALU.mult)
                            vsync(ve, i)
                            for pk in range(8):
                                i = nc.vector.transpose(PTB[:, pk * 512:(pk + 1) * 512],
                                                        PACK[:, pk * 512:(pk + 1) * 512])
                            sV.inc(i)
                        r_done_v[r] = sV.n
                        v_pt = sV.n

                        @block.tensor
                        def _(te, g=g, r=r, v_pt=v_pt):
                            te.wait_ge(sV.h, v_pt)
                            if r == 0:
                                te.wait_ge(sRb.h, r_vw[g])
                            if r >= 1:
                                te.wait_ge(sA.h, cp_at[r - 1])
                            for beta in range(32):
                                p = 32 * r + beta
                                pk, bl = beta // 4, beta % 4
                                for cc in range(16):
                                    j, m = cc % 4, cc // 4
                                    tp = (32 * m, 32 * j) if (j == 3 or m == 3) else None
                                    i = nc.tensor.matmul(
                                        P2[m][32 * j:32 * j + 16, 16 * beta:16 * beta + 16],
                                        V17[32 * m:32 * m + 32,
                                            j * 2048 + p: j * 2048 + p + 15 * 128 + 1: 128],
                                        PTB[32 * m:32 * m + 32,
                                            pk * 512 + 128 * bl + 32 * j: pk * 512 + 128 * bl + 32 * j + 16],
                                        start=True, stop=True, tile_position=tp,
                                        skip_group_check=True)
                            sMM.inc(i)
                        mm_p2_at[r] = sMM.n
                        mm_p2 = sMM.n

                        @block.scalar
                        def _(sc, g=g, r=r, mm_p2=mm_p2):
                            sc.wait_ge(sMM.h, mm_p2)
                            if r >= 2:
                                sc.wait_ge(sP.h, ev_gp[r - 2])
                            if r < 2 and g >= 1:
                                sc.wait_ge(sP.h, ev_gp_last[g - 1])
                            for m in range(4):
                                i = nc.scalar.activation(PCS[r % 2][:, m * 512:(m + 1) * 512],
                                                         P2[m][:], AF.Copy)
                            sA.inc(i)
                        cp_at[r] = sA.n

                        @block.gpsimd
                        def _(gp, g=g, r=r, cpa=sA.n):
                            gp.wait_ge(sA.h, cpa)
                            if r == 0 and g >= 1:
                                gp.wait_ge(sS.h, zdma_at[g - 1])
                            for m in range(4):
                                outv = (ZG.rearrange("q (bl sm h w) -> q bl sm h w",
                                                     bl=4, sm=4, h=16)[:, :, m, :, 8 * r:8 * r + 8]
                                        .rearrange("q bl h bq -> q bq bl h"))
                                in0 = PCS[r % 2][:, m * 512:(m + 1) * 512].rearrange(
                                    "q (bq bl h) -> q bq bl h", bq=8, bl=4)
                                in1 = (W18.rearrange("q (mm h p) -> q mm h p", mm=4, h=16)
                                       [:, m, :, 32 * r:32 * r + 32]
                                       .rearrange("q h (bq bl) -> q bq bl h", bq=8))
                                i = nc.gpsimd.tensor_tensor(outv, in0, in1, ALU.mult)
                            sP.inc(i)
                        ev_gp[r] = sP.n
                    g_mm_done[g] = sMM.n
                    g_evac_done[g] = sP.n
                    a_exp_last[g] = r_exp[3]
                    rdv_last[g] = r_done_v[3]
                    ev_gp_last[g] = ev_gp[3]
                if g + 1 < G4:
                    emit_wave_a(g + 1)
                    emit_wave_b(g + 1)
                p_ev = sP.n
                zhalf = []

                d_cin4 = d_cin.rearrange("(gg sl j p) c -> gg sl j p c", gg=G4, sl=16, j=4)

                @block.sync
                def _(sy, g=g, p_ev=p_ev):
                    sy.wait_ge(sP.h, p_ev)
                    for hf in range(2):
                        for j in range(4):
                            sS.dma(sy.dma_start(
                                out=d_cin4[g, hf * 8:(hf + 1) * 8, j].transpose([1, 0, 2]),
                                in_=ZG[32 * j:32 * j + 16, hf * 4096:(hf + 1) * 4096]
                                .rearrange("p (sl c) -> p sl c", c=512)))
                        if hf == 0:
                            zhalf.append(sS.n)
                zdma_at.append(sS.n)

                for hf in range(2):
                    @block.gpsimd
                    def _(gp, g=g, hf=hf, zat=(zhalf[0] if hf == 0 else zdma_at[g])):
                        gp.wait_ge(sS.h, zat)
                        sC.inc(nc.gpsimd.collective_compute(
                            "AllGather", ALU.bypass,
                            replica_groups=[list(range(8))],
                            ins=[d_cin[g * D + hf * 512: g * D + hf * 512 + 512, :].opt()],
                            outs=[d_cout[g * 8 * D + hf * 4096: g * 8 * D + hf * 4096 + 4096, :].opt()]))
        c_done = sC.n

        with ExitStack() as ph:
            BIX = ph.enter_context(nc.sbuf_tensor("bBIX", [128, 32], I32))
            BNDs = [ph.enter_context(nc.sbuf_tensor(f"bBND{i}", [128, 8 * 512], BF16))
                    for i in range(2)]

            @block.sync
            def _(sy):
                sS.dma(sy.dma_start(out=BIX[:], in_=d_bidx[:]))
            s_bix = sS.n

            band_loaded = {}
            band_mm = {}
            wo_epi = {}
            with ExitStack() as pp:
                pws = [pp.enter_context(nc.psum_tensor(f"bwo{i}", [128, 512], F32))
                       for i in range(8)]
                for jj in range(4):
                    @block.gpsimd
                    def _(gp, jj=jj, drain=sG.n):
                        gp.wait_ge(sC.h, c_done - 1)
                        gp.wait_ge(sS.h, s_bix)
                        gp.wait_ge(sG.h, drain)
                        if jj >= 2:
                            gp.wait_ge(sMM.h, band_mm[jj - 2])
                        for kk in range(4):
                            sG.dma(nc.gpsimd.indirect_dma_start(
                                out=BNDs[jj % 2][:, kk * 512:(kk + 1) * 512],
                                out_offset=None,
                                in_=d_cout[:],
                                in_offset=IndirectOffsetOnAxis(BIX[:, jj * 8 + kk:jj * 8 + kk + 1], 0)))
                    band_a = sG.n

                    @block.gpsimd
                    def _(gp, jj=jj, ba=band_a):
                        gp.wait_ge(sC.h, c_done)
                        gp.wait_ge(sG.h, ba)
                        for kk in range(4, 8):
                            sG.dma(nc.gpsimd.indirect_dma_start(
                                out=BNDs[jj % 2][:, kk * 512:(kk + 1) * 512],
                                out_offset=None,
                                in_=d_cout[:],
                                in_offset=IndirectOffsetOnAxis(BIX[:, jj * 8 + kk:jj * 8 + kk + 1], 0)))
                    band_loaded[jj] = sG.n

                    for dt in range(CT):
                        ps = pws[dt]

                        @block.tensor
                        def _(te, jj=jj, dt=dt, ps=ps, ba=band_a):
                            if dt == 0:
                                te.wait_ge(sG.h, ba)
                                te.wait_ge(sG.h, wb_loaded[0])
                            if jj >= 1 and dt == 0:
                                te.wait_ge(sV.h, wo_epi[jj - 1])
                            for kk in range(8):
                                if dt == 0 and kk == 4:
                                    te.wait_ge(sG.h, band_loaded[jj])
                                i = nc.tensor.matmul(
                                    ps[:],
                                    WB[0][:, kk * D + dt * 128: kk * D + dt * 128 + 128],
                                    BNDs[jj % 2][:, kk * 512:(kk + 1) * 512],
                                    start=(kk == 0), stop=(kk == 7))
                            sMM.inc(i)
                        mm_i = sMM.n

                        @block.vector
                        def _(ve, jj=jj, dt=dt, ps=ps, mm_i=mm_i):
                            ve.wait_ge(sMM.h, mm_i)
                            sl = slice(dt * T + jj * 512, dt * T + jj * 512 + 512)
                            sV.inc(nc.vector.tensor_tensor(X[:, sl], X[:, sl], ps[:], ALU.add))
                    wo_epi[jj] = sV.n
                    band_mm[jj] = sMM.n
            wb_free[0] = sMM.n

    with nc.sbuf_tensor("XLN1", [128, CT * T], BF16) as XLN:
        norm1 = emit_ln(XLN)
        emit_attn_a(XLN, norm1)
    if upto >= 2:
        with nc.sbuf_tensor("XLN2", [128, CT * T], BF16) as XLN:
            emit_ln(XLN)
            emit_ffn(XLN)
    if upto >= 3:
        qk = ExitStack()
        Q13 = qk.enter_context(nc.sbuf_tensor("bQ13", [128, 8192], BF16))
        K13 = qk.enter_context(nc.sbuf_tensor("bK13", [128, 8192], BF16))
        xs = ExitStack()
        XLN = xs.enter_context(nc.sbuf_tensor("XLN3", [128, CT * T], BF16))
        emit_ln(XLN)
        emit_attn_b(XLN, xs, sV.n, Q13, K13)
        qk.close()
    if upto >= 4:
        with nc.sbuf_tensor("XLN4", [128, CT * T], BF16) as XLN:
            emit_ln(XLN)
            emit_ffn(XLN, out_dma=True)

    @block.sync
    def _(sy):
        for s in ALL_SEMS:
            if s.n:
                sy.wait_ge(s.h, s.n)
        if upto < 4:
            sS.dma(sy.dma_start(out=d_out.rearrange("(ct p) t -> p ct t", p=128),
                                in_=X.rearrange("p (ct t) -> p ct t", ct=CT)))
        sy.wait_ge(sS.h, sS.n)

    es.close()
    return nc


def _mask_tiles():
    tiles = np.zeros((max(1, NPART), 128, 512), dtype=np.float32)
    for i, (jt, tc) in enumerate(PARTIALS):
        j = 128 * jt + np.arange(128)[:, None]
        t = 512 * tc + np.arange(512)[None, :]
        tiles[i] = (3.0 * j <= t).astype(np.float32)
    return tiles.reshape(-1, 512).astype(ml_dtypes.bfloat16)


def _wout_perm():
    perm = np.zeros(D, dtype=np.int64)
    for pm4 in range(4):
        for m in range(4):
            for j in range(4):
                for hd in range(16):
                    row = (4 * pm4 + m) * 64 + j * 16 + hd
                    cc = 4 * m + j
                    perm[row] = 256 * pm4 + 16 * cc + hd
    return perm


def _band_idx(core):
    idx = np.zeros((32, 128), dtype=np.int32)
    for jj in range(4):
        s_ = 4 * (core % 2) + jj
        g = core // 2
        for kk in range(8):
            row = kk * 128 + np.arange(128)
            idx[jj * 8 + kk] = g * (8 * D) + (row // 512) * 4096 + s_ * 512 + row % 512
    return np.ascontiguousarray(idx.T)


def kernel(L0_attn_a_x, L0_attn_a_Wq, L0_attn_a_Wk, L0_attn_a_Wv, L0_attn_a_Wo,
           L0_attn_a_bo, L0_attn_a_Wconv, L0_attn_b_w_qkvw, L0_attn_b_w_out,
           L0_ffn_gate_w, L0_ffn_gate_b, L0_ffn_expert_w, L0_ffn_expert_b):
    f32 = np.float32
    x = np.asarray(L0_attn_a_x, f32)
    com = {
        "wqT": np.ascontiguousarray(np.asarray(L0_attn_a_Wq, f32).T),
        "wkT": np.ascontiguousarray(np.asarray(L0_attn_a_Wk, f32).T),
        "wvT": np.ascontiguousarray(np.asarray(L0_attn_a_Wv, f32).T),
        "woT": np.ascontiguousarray(np.asarray(L0_attn_a_Wo, f32).T),
        "bo128": np.ascontiguousarray(np.asarray(L0_attn_a_bo, f32).reshape(CT, 128).T),
        "wconvT": np.ascontiguousarray(np.asarray(L0_attn_a_Wconv, f32).transpose(2, 1, 0)),
        "wqkvw": np.ascontiguousarray(np.asarray(L0_attn_b_w_qkvw, f32)),
        "woutS": np.ascontiguousarray(np.asarray(L0_attn_b_w_out, f32)[_wout_perm()]),
        "gw128": np.ascontiguousarray(
            np.asarray(L0_ffn_gate_w, f32).T.reshape(CT, 128, 8).transpose(1, 0, 2).reshape(128, 8 * CT)),
        "gb": np.asarray(L0_ffn_gate_b, f32).reshape(1, 8),
        "ewT": np.ascontiguousarray(
            np.asarray(L0_ffn_expert_w, f32).transpose(0, 2, 1).reshape(8 * D, D)),
        "ebP": np.ascontiguousarray(
            np.asarray(L0_ffn_expert_b, f32).reshape(8, CT, 128).transpose(0, 2, 1).reshape(8 * 128, CT)),
        "maskP": _mask_tiles(),
    }
    nc = bass.Bass("TRN2", target_bir_lowering=False, debug=False)
    build_program(nc, upto=UPTO)
    in_maps = []
    for c in range(8):
        m = dict(com)
        m["xT"] = np.ascontiguousarray(x[c].T)
        m["bandidx"] = _band_idx(c)
        in_maps.append(m)
    res = run_bass_kernel_spmd(nc, in_maps, core_ids=list(range(8)))
    out = np.stack([r["outT"] for r in res.results])
    return np.ascontiguousarray(out.transpose(0, 2, 1))


# version 13 (final)
# speedup vs baseline: 1.0305x; 1.0145x over previous
import os
import numpy as np
import ml_dtypes
from contextlib import ExitStack

import concourse.bass as bass
import concourse.mybir as mybir
from concourse.bass_utils import run_bass_kernel_spmd
from concourse.bass_core_rust import IndirectOffsetOnAxis

F32 = mybir.dt.float32
BF16 = mybir.dt.bfloat16
I32 = mybir.dt.int32
AF = mybir.ActivationFunctionType
ALU = mybir.AluOpType
AX = mybir.AxisListType

T = 2048
D = 1024
CT = 8
TCn = 4
H = 16
TK = 683
TKP = 768
JTn = 6
G4 = 4

UPTO = int(os.environ.get("MK_UPTO", "4"))
FFNSTAGE = int(os.environ.get("MK_FFNSTAGE", "3"))


class Sem:
    def __init__(self, h):
        self.h = h
        self.n = 0

    def inc(self, instr, k=1):
        instr.then_inc(self.h, k)
        self.n += k
        return instr

    def dma(self, instr):
        return self.inc(instr, 16)


def blk_state(jt, tc):
    jmin, jmax = 128 * jt, min(128 * jt + 127, TK - 1)
    tmin, tmax = 512 * tc, 512 * tc + 511
    if 3 * jmin > tmax:
        return "masked"
    if 3 * jmax <= tmin:
        return "full"
    return "partial"


PARTIALS = [(jt, tc) for jt in range(JTn) for tc in range(TCn)
            if blk_state(jt, tc) == "partial"]
NPART = len(PARTIALS)


def build_program(nc, upto=4):
    es = ExitStack()

    d_x = nc.dram_tensor("xT", [D, T], F32, kind="ExternalInput").ap()
    d_out = nc.dram_tensor("outT", [D, T], F32, kind="ExternalOutput").ap()
    d_wq = nc.dram_tensor("wqT", [D, D], F32, kind="ExternalInput").ap()
    d_wk = nc.dram_tensor("wkT", [D, D], F32, kind="ExternalInput").ap()
    d_wv = nc.dram_tensor("wvT", [D, D], F32, kind="ExternalInput").ap()
    d_wo = nc.dram_tensor("woT", [D, D], F32, kind="ExternalInput").ap()
    d_bo = nc.dram_tensor("bo128", [128, CT], F32, kind="ExternalInput").ap()
    d_wconv = nc.dram_tensor("wconvT", [3, D, D], F32, kind="ExternalInput").ap()
    d_wqkvw = nc.dram_tensor("wqkvw", [D, 4 * D], F32, kind="ExternalInput").ap()
    d_wout = nc.dram_tensor("woutS", [D, D], F32, kind="ExternalInput").ap()
    d_gw = nc.dram_tensor("gw128", [128, 8 * CT], F32, kind="ExternalInput").ap()
    d_gb = nc.dram_tensor("gb", [1, 8], F32, kind="ExternalInput").ap()
    d_ew = nc.dram_tensor("ewT", [8 * D, D], F32, kind="ExternalInput").ap()
    d_eb = nc.dram_tensor("ebP", [8 * 128, CT], F32, kind="ExternalInput").ap()
    d_mask = nc.dram_tensor("maskP", [max(1, NPART) * 128, 512], BF16, kind="ExternalInput").ap()
    d_bidx = nc.dram_tensor("bandidx", [128, 32], I32, kind="ExternalInput").ap()

    d_odr = nc.dram_tensor("o_bounce", [D, T], BF16).ap()
    d_stage = nc.dram_tensor("qkvw_stage", [G4, 8 * 128, T], BF16).ap()
    d_cin = nc.dram_tensor("cc_in", [G4 * D, 512], BF16).ap()
    d_cout = nc.dram_tensor("cc_out", [8 * G4 * D, 512], BF16).ap()

    X = es.enter_context(nc.sbuf_tensor("X", [128, CT * T], F32))
    WB = [es.enter_context(nc.sbuf_tensor(f"WB{i}", [128, CT * D], BF16)) for i in range(2)]
    ONESB = es.enter_context(nc.sbuf_tensor("ONESB", [128, 16], BF16))
    ONE128 = es.enter_context(nc.sbuf_tensor("ONE128", [128, 128], BF16))
    ONESR = es.enter_context(nc.sbuf_tensor("ONESR", [1, 128], F32))
    IOTA8 = es.enter_context(nc.sbuf_tensor("IOTA8", [1, 8], F32))
    IOTAP = es.enter_context(nc.sbuf_tensor("IOTAP", [128, 1], F32))
    GW = es.enter_context(nc.sbuf_tensor("GW", [128, 8 * CT], F32))
    GBr = es.enter_context(nc.sbuf_tensor("GBr", [1, 8], F32))
    SCR = es.enter_context(nc.sbuf_tensor("SCR", [1, 64], F32))
    EPS1 = es.enter_context(nc.sbuf_tensor("EPS1", [128, 1], F32))
    SCB = es.enter_context(nc.sbuf_tensor("SCB", [128, 16], F32))
    IDXS = es.enter_context(nc.sbuf_tensor("IDXS", [128, 18], I32))
    BG = es.enter_context(nc.sbuf_tensor("BGg", [128, 2 * CT], F32))
    CIOTA = es.enter_context(nc.sbuf_tensor("CIOTA", [128, 8], F32))
    IXS = es.enter_context(nc.sbuf_tensor("IXS", [128, 16], F32))

    sS = Sem(es.enter_context(nc.semaphore("sS")))
    sG = Sem(es.enter_context(nc.semaphore("sG")))
    sMM = Sem(es.enter_context(nc.semaphore("sMM")))
    sV = Sem(es.enter_context(nc.semaphore("sV")))
    sA = Sem(es.enter_context(nc.semaphore("sA")))
    sP = Sem(es.enter_context(nc.semaphore("sP")))
    sC = Sem(es.enter_context(nc.semaphore("sC")))
    sX = Sem(es.enter_context(nc.semaphore("sX")))
    sXg = Sem(es.enter_context(nc.semaphore("sXg")))
    sR = Sem(es.enter_context(nc.semaphore("sR")))
    sRb = Sem(es.enter_context(nc.semaphore("sRb")))
    sRc = Sem(es.enter_context(nc.semaphore("sRc")))
    ALL_SEMS = [sS, sG, sMM, sV, sA, sP, sC, sX, sXg, sR, sRb, sRc]

    block = es.enter_context(nc.Block())

    def vsync(ve, instr):
        sV.inc(instr)
        ve.wait_ge(sV.h, sV.n)

    def asylc(sc, instr):
        sA.inc(instr)
        sc.wait_ge(sA.h, sA.n)

    wb_free = [0, 0]
    wb_loaded = [0, 0]

    def load_weight(slot, dram_ap, cols=D, nct=CT):
        @block.gpsimd
        def _(gp, slot=slot, dram_ap=dram_ap, cols=cols, nct=nct):
            gp.wait_ge(sMM.h, wb_free[slot])
            sG.dma(gp.dma_start(
                out=WB[slot].rearrange("p (ct i) -> p ct i", ct=CT)[:, 0:nct, 0:cols],
                in_=dram_ap.rearrange("(ct p) i -> p ct i", p=128)))
        wb_loaded[slot] = sG.n

    XT = X.rearrange("p (ct t) -> p ct t", ct=CT)
    DXT = d_x.rearrange("(ct p) t -> p ct t", p=128)

    @block.sync
    def _(sy):
        sX.dma(sy.dma_start(out=XT[:, 0:3], in_=DXT[:, 0:3]))
        sS.dma(sy.dma_start(out=GW[:], in_=d_gw[:]))
        sS.dma(sy.dma_start(out=GBr[:], in_=d_gb[:]))

    s_init = sS.n

    @block.scalar
    def _(sc):
        sX.dma(sc.dma_start(out=XT[:, 3:6], in_=DXT[:, 3:6]))

    @block.gpsimd
    def _(gp):
        sXg.dma(gp.dma_start(out=XT[:, 6:8], in_=DXT[:, 6:8]))
        gp.memset(ONESB[:], 1.0)
        gp.memset(ONE128[:], 1.0)
        gp.memset(ONESR[:], 1.0)
        gp.memset(EPS1[:], 1e-5)
        nc.gpsimd.iota(IOTA8[:], pattern=[[1, 8]], base=0, channel_multiplier=0,
                       allow_small_or_imprecise_dtypes=True)
        nc.gpsimd.iota(CIOTA[:], pattern=[[128, 8]], base=0, channel_multiplier=0,
                       allow_small_or_imprecise_dtypes=True)
        sP.inc(nc.gpsimd.iota(IOTAP[:], pattern=[[0, 1]], base=0, channel_multiplier=1,
                              allow_small_or_imprecise_dtypes=True))
    p_init = sP.n

    ln_ctr = [0]

    def emit_ln(XLN):
        k = ln_ctr[0]
        ln_ctr[0] += 1
        with ExitStack() as ph:
            SQ = ph.enter_context(nc.sbuf_tensor(f"lnSQ{k}", [128, CT * T], BF16))
            VAR = SQ[:].bitcast(F32)[:, 0:T]
            MB = SQ[:, 2 * T:3 * T]
            RB = SQ[:, 3 * T:4 * T]
            M2 = RB

            a_sq = []
            v_cp = []

            @block.scalar
            def _(sc):
                for s in ALL_SEMS:
                    if s.n:
                        sc.wait_ge(s.h, s.n)
                for ct in range(CT):
                    sA.inc(nc.scalar.activation(SQ[:, ct * T:(ct + 1) * T],
                                                X[:, ct * T:(ct + 1) * T], AF.Square))
                    a_sq.append(sA.n)
            a_prep = sA.n

            @block.vector
            def _(ve):
                for s in ALL_SEMS:
                    if s.n:
                        ve.wait_ge(s.h, s.n)
                for ct in range(CT):
                    sV.inc(nc.vector.tensor_copy(XLN[:, ct * T:(ct + 1) * T],
                                                 X[:, ct * T:(ct + 1) * T]))
                    v_cp.append(sV.n)
            v_prep = sV.n

            with ExitStack() as pp:
                p_s = [pp.enter_context(nc.psum_tensor(f"lnp{k}_{i}", [128, 512], F32))
                       for i in range(8)]

                @block.tensor
                def _(te):
                    for ct in range(CT):
                        te.wait_ge(sV.h, v_cp[ct])
                        te.wait_ge(sA.h, a_sq[ct])
                        for tcn in range(TCn):
                            sl = slice(ct * T + tcn * 512, ct * T + tcn * 512 + 512)
                            nc.tensor.matmul(p_s[tcn][:], ONE128[:], XLN[:, sl],
                                             start=(ct == 0), stop=(ct == CT - 1))
                            i = nc.tensor.matmul(p_s[4 + tcn][:], ONE128[:], SQ[:, sl],
                                                 start=(ct == 0), stop=(ct == CT - 1))
                    sMM.inc(i)
                mm_sums = sMM.n

                @block.vector
                def _(ve):
                    ve.wait_ge(sMM.h, mm_sums)
                    for tcn in range(TCn):
                        sl = slice(tcn * 512, tcn * 512 + 512)
                        nc.vector.tensor_scalar_mul(MB[:, sl], p_s[tcn][:], 1.0 / D)
                        nc.vector.tensor_scalar_mul(VAR[:, sl], p_s[4 + tcn][:], 1.0 / D)
                    for tcn in range(TCn):
                        sl = slice(tcn * 512, tcn * 512 + 512)
                        nc.vector.tensor_mul(M2[:, sl], MB[:, sl], MB[:, sl])
                    for tcn in range(TCn):
                        sl = slice(tcn * 512, tcn * 512 + 512)
                        i = nc.vector.tensor_sub(VAR[:, sl], VAR[:, sl], M2[:, sl])
                    sV.inc(i)
                v_var = sV.n

            @block.scalar
            def _(sc):
                sc.wait_ge(sV.h, v_var)
                sc.wait_ge(sP.h, p_init)
                asylc(sc, nc.scalar.activation(VAR[:], VAR[:], AF.Ln, bias=EPS1[:]))
                sA.inc(nc.scalar.activation(RB[:], VAR[:], AF.Exp, scale=-0.5))
            a_rstd = sA.n

            norm = []

            @block.vector
            def _(ve):
                ve.wait_ge(sA.h, a_rstd)
                for ct in range(CT):
                    sl = slice(ct * T, (ct + 1) * T)
                    nc.vector.tensor_sub(XLN[:, sl], XLN[:, sl], MB[:])
                    sV.inc(nc.vector.tensor_mul(XLN[:, sl], XLN[:, sl], RB[:]))
                    norm.append(sV.n)
            return norm

    proj_ctr = [0]

    def emit_proj(slot, rhs_fn, o_tiles, epilogue, ep_sem, tensor_wait=None, n_k=CT,
                  epi_batch=None, k_waits=None):
        k = proj_ctr[0]
        proj_ctr[0] += 1
        items = [(ot, tcn) for ot in o_tiles for tcn in range(TCn)]
        with ExitStack() as pp:
            pss = [pp.enter_context(nc.psum_tensor(f"pj{k}_{i}", [128, 512], F32))
                   for i in range(min(8, len(items)))]
            np_ = len(pss)
            epi_at = {}
            pending = []
            for i, (ot, tcn) in enumerate(items):
                ps = pss[i % np_]

                @block.tensor
                def _(te, i=i, ot=ot, tcn=tcn, ps=ps):
                    if i == 0:
                        if k_waits is None:
                            te.wait_ge(sG.h, wb_loaded[slot])
                        if tensor_wait is not None:
                            tensor_wait(te)
                    if i >= np_:
                        te.wait_ge(ep_sem.h, epi_at[i - np_])
                    for c in range(n_k):
                        if i == 0 and k_waits is not None:
                            te.wait_ge(k_waits[0].h, k_waits[1][c])
                        j = nc.tensor.matmul(
                            ps[:],
                            WB[slot][:, c * D + ot * 128: c * D + ot * 128 + 128],
                            rhs_fn(c, tcn),
                            start=(c == 0), stop=(c == n_k - 1))
                    sMM.inc(j)
                mm_i = sMM.n

                if epi_batch is None:
                    epilogue(ot, tcn, ps, mm_i)
                    epi_at[i] = ep_sem.n
                else:
                    pending.append((ot, tcn, ps, mm_i, i))
                    if len(pending) == epi_batch or i == len(items) - 1:
                        base = ep_sem.n
                        epilogue(pending)
                        assert ep_sem.n == base + len(pending)
                        for kk2, (_, _, _, _, i2) in enumerate(pending):
                            epi_at[i2] = base + kk2 + 1
                        pending = []
            wb_free[slot] = sMM.n

    def emit_attn_a(XLN, norm):
        with ExitStack() as ph:
            KT = ph.enter_context(nc.sbuf_tensor("aKT", [128, CT * TKP], BF16))
            VA = ph.enter_context(nc.sbuf_tensor("aVA", [128, JTn * 1040], BF16))
            QHP = [ph.enter_context(nc.sbuf_tensor(f"aQ{i}", [128, T], BF16)) for i in range(2)]
            PTs = [ph.enter_context(nc.sbuf_tensor(f"aPT{i}", [128, JTn * 512], BF16))
                   for i in range(2)]
            RZ2 = [ph.enter_context(nc.sbuf_tensor(f"aRZ{i}", [1, 512], BF16))
                   for i in range(2)]
            RZB = ph.enter_context(nc.sbuf_tensor("aRZB", [64, 512], F32))
            RZROW = RZB[0:1, :]
            OB = [ph.enter_context(nc.sbuf_tensor(f"aOB{i}", [128, T], BF16)) for i in range(2)]
            MASKS = ph.enter_context(nc.sbuf_tensor("aMSK", [128, max(1, NPART) * 512], BF16))
            ph_k = ExitStack()
            KTMP = ph_k.enter_context(nc.sbuf_tensor("aKTMP", [128, CT * TKP], BF16))

            if NPART:
                @block.sync
                def _(sy):
                    sS.dma(sy.dma_start(out=MASKS.rearrange("p (n w) -> p n w", w=512),
                                        in_=d_mask.rearrange("(n p) w -> p n w", p=128)))
            s_mask = sS.n

            load_weight(0, d_wconv[0])

            @block.gpsimd
            def _(gp, w=[(s, s.n) for s in ALL_SEMS]):
                for s, n in w:
                    if n:
                        gp.wait_ge(s.h, n)
                sP.inc(gp.memset(
                    KTMP.rearrange("p (ct c) -> p ct c", ct=CT)[:, :, 684:TKP], 0.0))
            p_zero = sP.n

            @block.vector
            def _(ve, w=[(s, s.n) for s in ALL_SEMS]):
                for s, n in w:
                    if n:
                        ve.wait_ge(s.h, n)
                sV.inc(nc.vector.memset(
                    KT.rearrange("p (ct c) -> p ct c", ct=CT)[:, :, TK:TKP], 0.0))

            ln_done = sV.n
            with ExitStack() as pp:
                pcs = [pp.enter_context(nc.psum_tensor(f"cv{i}", [128, 341], F32))
                       for i in range(8)]
                cv_evac = [0]
                for og in range(2):
                    ots = [og * 4 + i for i in range(4)]
                    for tau in range(3):
                        slot = (og * 3 + tau) % 2
                        if not (og == 0 and tau == 0):
                            load_weight(slot, d_wconv[tau])
                        g_w = sG.n

                        @block.tensor
                        def _(te, og=og, tau=tau, slot=slot, g_w=g_w):
                            te.wait_ge(sG.h, g_w)
                            if og == 0 and tau == 0:
                                te.wait_ge(sP.h, p_zero)
                            if og == 1 and tau == 0:
                                te.wait_ge(sV.h, cv_evac[0])
                            last = None
                            for oi, ot in enumerate(ots):
                                for jc in range(2):
                                    ps = pcs[oi * 2 + jc]
                                    j0 = jc * 341
                                    for c in range(CT):
                                        if og == 0 and tau == 0 and oi == 0 and jc == 0:
                                            te.wait_ge(sV.h, norm[c])
                                        last = nc.tensor.matmul(
                                            ps[:],
                                            WB[slot][:, c * D + ot * 128: c * D + ot * 128 + 128],
                                            XLN[:, c * T + 3 * j0 + tau: c * T + 3 * j0 + tau + 3 * 341: 3],
                                            start=(tau == 0 and c == 0),
                                            stop=(tau == 2 and c == CT - 1))
                            sMM.inc(last)
                        wb_free[slot] = sMM.n
                    mm_cv = sMM.n

                    @block.vector
                    def _(ve, ots=ots, mm_cv=mm_cv):
                        ve.wait_ge(sMM.h, mm_cv)
                        for oi, ot in enumerate(ots):
                            for jc in range(2):
                                nc.vector.tensor_copy(
                                    KTMP[:, ot * TKP + 1 + jc * 341: ot * TKP + 1 + jc * 341 + 341],
                                    pcs[oi * 2 + jc][:])
                            i = nc.vector.tensor_copy(KTMP[:, ot * TKP: ot * TKP + 1],
                                                      XLN[:, ot * T: ot * T + 1])
                        sV.inc(i)
                        cv_evac[0] = sV.n
            v_ktmp = sV.n

            load_weight(0, d_wk)
            g_wk = sG.n
            with ExitStack() as pp:
                pks = [pp.enter_context(nc.psum_tensor(f"kp{i}", [128, 512], F32))
                       for i in range(8)]
                jchunks = [(0, 512), (512, 171)]
                items = [(ot, jci) for ot in range(CT) for jci in range(2)]
                epi_at = {}
                for i, (ot, jci) in enumerate(items):
                    j0, n = jchunks[jci]
                    ps = pks[i % 8]

                    @block.tensor
                    def _(te, i=i, ot=ot, j0=j0, n=n, ps=ps):
                        if i == 0:
                            te.wait_ge(sG.h, g_wk)
                            te.wait_ge(sV.h, v_ktmp)
                        if i >= 8:
                            te.wait_ge(sV.h, epi_at[i - 8])
                        for c in range(CT):
                            j = nc.tensor.matmul(
                                ps[:, 0:n],
                                WB[0][:, c * D + ot * 128: c * D + ot * 128 + 128],
                                KTMP[:, c * TKP + j0: c * TKP + j0 + n],
                                start=(c == 0), stop=(c == CT - 1))
                        sMM.inc(j)
                    mm_i = sMM.n

                    @block.vector
                    def _(ve, ot=ot, j0=j0, n=n, ps=ps, mm_i=mm_i):
                        ve.wait_ge(sMM.h, mm_i)
                        sV.inc(nc.vector.tensor_copy(KT[:, ot * TKP + j0: ot * TKP + j0 + n],
                                                     ps[:, 0:n]))
                    epi_at[i] = sV.n
                wb_free[0] = sMM.n

            load_weight(1, d_wv)
            g_wv = sG.n
            v_kdone = sV.n
            with ExitStack() as pp:
                pvs = [pp.enter_context(nc.psum_tensor(f"vp{i}", [128, 512], F32))
                       for i in range(8)]
                items = [(jt, ic) for jt in range(JTn) for ic in range(2)]
                epi_at = {}
                for i, (jt, ic) in enumerate(items):
                    ps = pvs[i % 8]

                    @block.tensor
                    def _(te, i=i, jt=jt, ic=ic, ps=ps):
                        if i == 0:
                            te.wait_ge(sG.h, g_wv)
                            te.wait_ge(sV.h, v_kdone)
                        if i >= 8:
                            te.wait_ge(sV.h, epi_at[i - 8])
                        for c in range(CT):
                            j = nc.tensor.matmul(
                                ps[:],
                                KTMP[:, c * TKP + jt * 128: c * TKP + jt * 128 + 128],
                                WB[1][:, c * D + ic * 512: c * D + ic * 512 + 512],
                                start=(c == 0), stop=(c == CT - 1))
                        sMM.inc(j)
                    mm_i = sMM.n

                    @block.vector
                    def _(ve, jt=jt, ic=ic, ps=ps, mm_i=mm_i):
                        ve.wait_ge(sMM.h, mm_i)
                        sV.inc(nc.vector.tensor_copy(
                            VA[:, jt * 1040 + (8 * ic) * 65: jt * 1040 + (8 * ic + 8) * 65]
                            .rearrange("p (h c) -> p h c", h=8)[:, :, 0:64],
                            ps[:].rearrange("p (h c) -> p h c", h=8)))
                    epi_at[i] = sV.n
                wb_free[1] = sMM.n

                @block.vector
                def _(ve):
                    sV.inc(nc.vector.tensor_copy(
                        VA.rearrange("p (jt x) -> p jt x", jt=JTn)[:, :, 64:1040:65],
                        ONESB[:, 0:1].to_broadcast([128, JTn, 16])))
            v_va = sV.n

            ph_k.close()
            load_weight(0, d_wq)
            g_wq = sG.n
            with ExitStack() as pp:
                psc = [pp.enter_context(nc.psum_tensor(f"sc{i}", [128, 512], F32))
                       for i in range(4)]
                pso = [pp.enter_context(nc.psum_tensor(f"so{i}", [65, 512], F32))
                       for i in range(2)]
                psz = [pp.enter_context(nc.psum_tensor(f"szz{i}", [64, 512], F32))
                       for i in range(2)]
                sc_use = [0]
                sc_free = {}
                q_epi = {}
                ob_store = {}
                o_free = {}
                z_free = {}
                v_q = {}
                mm_o = {}
                mm_zd = {}
                it_state = {}
                iters = [(hp, hh, tcn) for hp in range(8) for hh in range(2)
                         for tcn in range(TCn)]
                n_it = len(iters)

                def emit_qproj(hp):
                    qb = QHP[hp % 2]
                    for tcn in range(TCn):
                        u = sc_use[0]
                        ps = psc[u % 4]
                        sc_use[0] += 1

                        @block.tensor
                        def _(te, hp=hp, tcn=tcn, ps=ps, u=u):
                            if hp == 0 and tcn == 0:
                                te.wait_ge(sG.h, g_wq)
                                te.wait_ge(sV.h, v_va)
                            if u >= 4:
                                fr = sc_free[u - 4]
                                te.wait_ge(fr[0].h, fr[1])
                            if hp >= 2 and tcn == 0:
                                te.wait_ge(sA.h, q_epi[hp - 2])
                            for c in range(CT):
                                j = nc.tensor.matmul(
                                    ps[:],
                                    WB[0][:, c * D + hp * 128: c * D + hp * 128 + 128],
                                    XLN[:, c * T + tcn * 512: c * T + tcn * 512 + 512],
                                    start=(c == 0), stop=(c == CT - 1))
                            sMM.inc(j)
                        mm_i = sMM.n

                        @block.vector
                        def _(ve, qb=qb, tcn=tcn, ps=ps, mm_i=mm_i):
                            ve.wait_ge(sMM.h, mm_i)
                            sV.inc(nc.vector.tensor_copy(qb[:, tcn * 512:(tcn + 1) * 512], ps[:]))
                        sc_free[u] = (sV, sV.n)
                    v_q[hp] = sV.n

                def emit_lagged(n):
                    po, pz, hp, h, tcn, a_rz_n = it_state[n]

                    @block.tensor
                    def _(te, n=n, pz=pz, a_rz_n=a_rz_n):
                        if n >= 2:
                            fr = z_free[n - 2]
                            te.wait_ge(fr[0].h, fr[1])
                        te.wait_ge(sA.h, a_rz_n)
                        sMM.inc(nc.tensor.matmul(pz[:], ONE128[0:1, 0:64], RZ2[n % 2][:],
                                                 start=True, stop=True))
                    mm_zd[n] = sMM.n

                    @block.vector
                    def _(ve, n=n, po=po, pz=pz, hp=hp, h=h, tcn=tcn):
                        ve.wait_ge(sMM.h, mm_zd[n])
                        if h % 2 == 0 and tcn == 0 and hp >= 2:
                            ve.wait_ge(sS.h, ob_store[hp - 2])
                        nc.vector.tensor_copy(RZB[:], pz[:])
                        sV.inc(nc.vector.tensor_mul(
                            OB[hp % 2][(h % 2) * 64:(h % 2) * 64 + 64, tcn * 512:(tcn + 1) * 512],
                            po[0:64, :], RZB[:]))
                    o_free[n] = (sV, sV.n)
                    z_free[n] = (sV, sV.n)
                    if h % 2 == 1 and tcn == TCn - 1:
                        v_ob = sV.n

                        @block.sync
                        def _(sy, hp=hp, v_ob=v_ob):
                            if hp >= 2:
                                sy.wait_ge(sS.h, ob_store[hp - 2])
                            sy.wait_ge(sV.h, v_ob)
                            sS.dma(sy.dma_start(out=d_odr[hp * 128:(hp + 1) * 128, :],
                                                in_=OB[hp % 2][:]))
                        ob_store[hp] = sS.n

                def emit_sc_z(n):
                    po = it_state[n][0]

                    @block.scalar
                    def _(sc, n=n, po=po):
                        sc.wait_ge(sMM.h, mm_o[n])
                        if n >= 2:
                            sc.wait_ge(sMM.h, mm_zd[n - 2])
                        nc.scalar.activation(RZROW[:], po[64:65, :], AF.Ln)
                        sA.inc(nc.scalar.activation(RZ2[n % 2][:], RZROW[:], AF.Exp,
                                                    scale=-1.0))

                for n, (hp, hh, tcn) in enumerate(iters):
                    if hh == 0 and tcn == 0:
                        emit_qproj(hp)
                    qb = QHP[hp % 2]
                    h = 2 * hp + hh
                    kbase = (h // 2) * TKP
                    krow = (h % 2) * 64
                    PTb = PTs[n % 2]
                    live = [jt for jt in range(JTn) if blk_state(jt, tcn) != "masked"]
                    ntcn = iters[n + 1][2] if n + 1 < n_it else None
                    nlive = (len([jt for jt in range(JTn) if blk_state(jt, ntcn) != "masked"])
                             if ntcn is not None else 0)
                    S_n = sA.n
                    a_exp_at = {jt: S_n + ji + 1 for ji, jt in enumerate(live)}
                    a_last = S_n + len(live)
                    a_rz_n = S_n + len(live) + (1 if n >= 1 else 0) + nlive + 1
                    mm_sc_at = {}
                    psc_map = []
                    for ji, jt in enumerate(live):
                        u = sc_use[0]
                        ps = psc[u % 4]
                        sc_use[0] += 1
                        psc_map.append(ps)

                        @block.tensor
                        def _(te, jt=jt, ps=ps, u=u, vq=v_q[hp], kbase=kbase,
                              krow=krow, qb=qb, tcn=tcn):
                            if u >= 4:
                                fr = sc_free[u - 4]
                                te.wait_ge(fr[0].h, fr[1])
                            te.wait_ge(sV.h, vq)
                            sMM.inc(nc.tensor.matmul(
                                ps[:],
                                KT[krow:krow + 64, kbase + jt * 128: kbase + jt * 128 + 128],
                                qb[krow:krow + 64, tcn * 512:(tcn + 1) * 512],
                                start=True, stop=True))
                        mm_sc_at[jt] = sMM.n
                        sc_free[u] = (sA, a_exp_at[jt])
                    v_mask = sV.n
                    parts = [jt for jt in live if blk_state(jt, tcn) == "partial"]
                    if parts:
                        @block.vector
                        def _(ve, parts=parts, tcn=tcn, a_exp_at=dict(a_exp_at), PTb=PTb):
                            ve.wait_ge(sS.h, s_mask)
                            for jt in parts:
                                pi = PARTIALS.index((jt, tcn))
                                ve.wait_ge(sA.h, a_exp_at[jt])
                                i = nc.vector.tensor_mul(
                                    PTb[:, jt * 512:(jt + 1) * 512],
                                    PTb[:, jt * 512:(jt + 1) * 512],
                                    MASKS[:, pi * 512:(pi + 1) * 512])
                            sV.inc(i)
                        v_mask = sV.n
                    po = pso[n % 2]
                    pz = psz[n % 2]

                    @block.tensor
                    def _(te, n=n, h=h, live=live, po=po, v_mask=v_mask,
                          a_last=a_last, PTb=PTb):
                        if n >= 2:
                            fr = o_free[n - 2]
                            te.wait_ge(fr[0].h, fr[1])
                        te.wait_ge(sA.h, a_last)
                        te.wait_ge(sV.h, v_mask)
                        for ci, jt in enumerate(live):
                            j = nc.tensor.matmul(
                                po[:],
                                VA[:, jt * 1040 + 65 * h: jt * 1040 + 65 * h + 65],
                                PTb[:, jt * 512:(jt + 1) * 512],
                                start=(ci == 0), stop=(ci == len(live) - 1))
                        sMM.inc(j)
                    mm_o[n] = sMM.n
                    it_state[n] = (po, pz, hp, h, tcn, a_rz_n)
                    if n >= 1:
                        emit_lagged(n - 1)

                    @block.scalar
                    def _(sc, n=n, live=live, mm_sc_at=dict(mm_sc_at),
                          psc_map=list(psc_map), PTb=PTb):
                        if n >= 2:
                            sc.wait_ge(sMM.h, mm_o[n - 2])
                        for ji, jt in enumerate(live):
                            sc.wait_ge(sMM.h, mm_sc_at[jt])
                            sA.inc(nc.scalar.activation(
                                PTb[:, jt * 512:(jt + 1) * 512], psc_map[ji][:],
                                AF.Exp, scale=0.125))
                    assert sA.n == a_last, (sA.n, a_last)
                    if n >= 1:
                        emit_sc_z(n - 1)
                        assert sA.n == it_state[n - 1][5], (sA.n, it_state[n - 1][5])
                    if h % 2 == 1 and tcn == TCn - 1:
                        q_epi[hp] = sA.n
                emit_lagged(n_it - 1)
                emit_sc_z(n_it - 1)
                assert sA.n == it_state[n_it - 1][5], (sA.n, it_state[n_it - 1][5])
                wb_free[0] = sMM.n
            s_oall = sS.n

        if True:
            load_weight(1, d_wo)
            with ExitStack() as op:
                OCH = op.enter_context(nc.sbuf_tensor("aOCH", [128, CT * T], BF16))
                BO = op.enter_context(nc.sbuf_tensor("aBO2", [128, CT], F32))

                @block.sync
                def _(sy):
                    sS.dma(sy.dma_start(out=BO[:], in_=d_bo[:]))
                s_bo = sS.n

                och_c = []

                @block.sync
                def _(sy):
                    sy.wait_ge(sS.h, s_oall)
                    for ct in range(CT):
                        sS.dma(sy.dma_start(
                            out=OCH.rearrange("p (ct t) -> p ct t", ct=CT)[:, ct],
                            in_=d_odr[ct * 128:(ct + 1) * 128, :]))
                        och_c.append(sS.n)

                def epi(ot, tcn, ps, mm_i):
                    @block.vector
                    def _(ve, ot=ot, tcn=tcn, ps=ps, mm_i=mm_i):
                        ve.wait_ge(sMM.h, mm_i)
                        ve.wait_ge(sS.h, s_bo)
                        sV.inc(nc.vector.scalar_tensor_tensor(
                            X[:, ot * T + tcn * 512: ot * T + tcn * 512 + 512],
                            ps[:], BO[:, ot:ot + 1],
                            X[:, ot * T + tcn * 512: ot * T + tcn * 512 + 512],
                            ALU.add, ALU.add))

                emit_proj(1, lambda c, tcn: OCH[:, c * T + tcn * 512: c * T + tcn * 512 + 512],
                          list(range(CT)), epi, sV, k_waits=(sS, och_c))

    ffn_ctr = [0]

    def emit_ffn(XLN, out_dma=False):
        k = ffn_ctr[0]
        ffn_ctr[0] += 1
        with ExitStack() as ph:
            POOL = ph.enter_context(nc.sbuf_tensor(f"fPO{k}", [128, CT], F32))
            LOGI = ph.enter_context(nc.sbuf_tensor(f"fLG{k}", [1, 8], F32))
            TOP8 = ph.enter_context(nc.sbuf_tensor(f"fT8{k}", [1, 8], F32))
            TMP8 = ph.enter_context(nc.sbuf_tensor(f"fTM{k}", [1, 24], F32))
            G1B = ph.enter_context(nc.sbuf_tensor(f"fG1{k}", [128, CT * T], BF16))
            G2B = ph.enter_context(nc.sbuf_tensor(f"fG2{k}", [128, 8 * 512], BF16))

            @block.vector
            def _(ve):
                for ct in range(CT):
                    i = nc.vector.tensor_reduce(POOL[:, ct:ct + 1], XLN[:, ct * T:(ct + 1) * T],
                                                AX.X, ALU.add)
                vsync(ve, i)
                sV.inc(nc.vector.tensor_scalar_mul(POOL[:], POOL[:], 1.0 / T))
            v_pool = sV.n

            with ExitStack() as pp:
                pl = pp.enter_context(nc.psum_tensor(f"fpl{k}", [1, 8], F32))
                pb = pp.enter_context(nc.psum_tensor(f"fpb{k}", [128, 8], F32))

                @block.tensor
                def _(te):
                    te.wait_ge(sV.h, v_pool)
                    te.wait_ge(sS.h, s_init)
                    for c in range(CT):
                        j = nc.tensor.matmul(pl[:], POOL[:, c:c + 1], GW[:, c * 8:(c + 1) * 8],
                                             start=(c == 0), stop=(c == CT - 1))
                    sMM.inc(j)
                mm_lg = sMM.n

                @block.vector
                def _(ve):
                    ve.wait_ge(sMM.h, mm_lg)
                    vsync(ve, nc.vector.tensor_add(LOGI[:], pl[:], GBr[:]))
                    vsync(ve, nc.vector.max(out=TOP8[:], in_=LOGI[:]))
                    vsync(ve, nc.vector.tensor_single_scalar(TMP8[:, 0:8], LOGI[:], TOP8[:, 1:2], ALU.is_ge))
                    vsync(ve, nc.vector.tensor_single_scalar(TMP8[:, 8:16], LOGI[:], TOP8[:, 0:1], ALU.is_ge))
                v_keep = sV.n

                @block.scalar
                def _(sc):
                    sc.wait_ge(sV.h, v_keep)
                    sA.inc(nc.scalar.activation(TMP8[:, 16:24], LOGI[:], AF.Exp))
                a_exp = sA.n

                @block.vector
                def _(ve):
                    ve.wait_ge(sA.h, a_exp)
                    vsync(ve, nc.vector.tensor_mul(TMP8[:, 16:24], TMP8[:, 16:24], TMP8[:, 0:8]))
                    vsync(ve, nc.vector.tensor_reduce(SCR[:, 0:1], TMP8[:, 16:24], AX.X, ALU.add))
                    vsync(ve, nc.vector.reciprocal(SCR[:, 1:2], SCR[:, 0:1]))
                    vsync(ve, nc.vector.tensor_single_scalar(TMP8[:, 16:24], TMP8[:, 16:24], SCR[:, 1:2], ALU.mult))
                    vsync(ve, nc.vector.tensor_mul(SCR[:, 8:16], IOTA8[:], TMP8[:, 8:16]))
                    vsync(ve, nc.vector.tensor_reduce(SCR[:, 2:3], SCR[:, 8:16], AX.X, ALU.add))
                    vsync(ve, nc.vector.tensor_mul(SCR[:, 8:16], IOTA8[:], TMP8[:, 0:8]))
                    vsync(ve, nc.vector.tensor_reduce(SCR[:, 3:4], SCR[:, 8:16], AX.X, ALU.add))
                    vsync(ve, nc.vector.tensor_sub(SCR[:, 3:4], SCR[:, 3:4], SCR[:, 2:3]))
                    vsync(ve, nc.vector.tensor_mul(SCR[:, 8:16], TMP8[:, 16:24], TMP8[:, 8:16]))
                    vsync(ve, nc.vector.tensor_reduce(SCR[:, 4:5], SCR[:, 8:16], AX.X, ALU.add))
                    vsync(ve, nc.vector.tensor_scalar(SCR[:, 5:6], SCR[:, 4:5], -1.0, 1.0, ALU.mult, ALU.add))
                    nc.vector.tensor_scalar_mul(SCR[:, 16:17], SCR[:, 2:3], 1024.0)
                    nc.vector.tensor_scalar_mul(SCR[:, 17:18], SCR[:, 3:4], 1024.0)
                    nc.vector.tensor_scalar_mul(SCR[:, 18:19], SCR[:, 2:3], 128.0)
                    nc.vector.tensor_scalar_mul(SCR[:, 19:20], SCR[:, 3:4], 128.0)
                    nc.vector.tensor_copy(SCR[:, 20:21], SCR[:, 4:5])
                    vsync(ve, nc.vector.tensor_copy(SCR[:, 21:22], SCR[:, 5:6]))
                v_gate = sV.n

                @block.tensor
                def _(te):
                    te.wait_ge(sV.h, v_gate)
                    sMM.inc(nc.tensor.matmul(pb[:, 0:6], ONESR[:], SCR[0:1, 16:22],
                                             start=True, stop=True))
                mm_bc = sMM.n

                @block.vector
                def _(ve):
                    ve.wait_ge(sMM.h, mm_bc)
                    ve.wait_ge(sP.h, p_init)
                    vsync(ve, nc.vector.tensor_copy(SCB[:, 0:6], pb[:, 0:6]))
                    nc.vector.tensor_add(SCB[:, 6:7], SCB[:, 0:1], IOTAP[:])
                    nc.vector.tensor_add(SCB[:, 7:8], SCB[:, 1:2], IOTAP[:])
                    nc.vector.tensor_add(SCB[:, 8:9], SCB[:, 2:3], IOTAP[:])
                    vsync(ve, nc.vector.tensor_add(SCB[:, 9:10], SCB[:, 3:4], IOTAP[:]))
                    nc.vector.tensor_tensor(IXS[:, 0:8], SCB[:, 6:7].to_broadcast([128, 8]),
                                            CIOTA[:], ALU.add)
                    vsync(ve, nc.vector.tensor_tensor(IXS[:, 8:16], SCB[:, 7:8].to_broadcast([128, 8]),
                                                      CIOTA[:], ALU.add))
                    vsync(ve, nc.vector.tensor_copy(IDXS[:, 0:16], IXS[:]))
                    nc.vector.tensor_copy(IDXS[:, 16:17], SCB[:, 8:9])
                    sV.inc(nc.vector.tensor_copy(IDXS[:, 17:18], SCB[:, 9:10]))
                v_idx = sV.n

            if FFNSTAGE < 2:
                return
            gmarks = {0: [], 1: []}
            for s_ in range(2):
                @block.gpsimd
                def _(gp, s_=s_):
                    gp.wait_ge(sV.h, v_idx)
                    gp.wait_ge(sMM.h, wb_free[s_])
                    for c in range(CT):
                        sG.dma(nc.gpsimd.indirect_dma_start(
                            out=WB[s_][:, c * D:(c + 1) * D],
                            out_offset=None,
                            in_=d_ew[:],
                            in_offset=IndirectOffsetOnAxis(IDXS[:, s_ * 8 + c:s_ * 8 + c + 1], 0)))
                        gmarks[s_].append(sG.n)
                    sG.dma(nc.gpsimd.indirect_dma_start(
                        out=BG[:, s_ * CT:(s_ + 1) * CT],
                        out_offset=None,
                        in_=d_eb[:],
                        in_offset=IndirectOffsetOnAxis(IDXS[:, 16 + s_:17 + s_], 0)))
                wb_loaded[s_] = sG.n

            if FFNSTAGE < 3:
                return
            def epi1(batch):
                @block.scalar
                def _(sc, batch=list(batch)):
                    sc.wait_ge(sG.h, wb_loaded[0])
                    for (ot, tcn, ps, mm_i, _) in batch:
                        sc.wait_ge(sMM.h, mm_i)
                        sA.inc(nc.scalar.activation(
                            G1B[:, ot * T + tcn * 512: ot * T + tcn * 512 + 512],
                            ps[:], AF.Gelu, bias=BG[:, ot:ot + 1]))

            emit_proj(0, lambda c, tcn: XLN[:, c * T + tcn * 512: c * T + tcn * 512 + 512],
                      list(range(CT)), epi1, sA, epi_batch=8)

            def epi2(batch):
                a_base = sA.n

                @block.scalar
                def _(sc, batch=list(batch), vprev=cmb_done[0]):
                    sc.wait_ge(sG.h, wb_loaded[1])
                    sc.wait_ge(sV.h, vprev)
                    for sl_i, (ot, tcn, ps, mm_i, _) in enumerate(batch):
                        sc.wait_ge(sMM.h, mm_i)
                        sA.inc(nc.scalar.activation(G2B[:, sl_i * 512:(sl_i + 1) * 512],
                                                    ps[:], AF.Gelu,
                                                    bias=BG[:, CT + ot:CT + ot + 1]))

                @block.vector
                def _(ve, batch=list(batch), a_base=a_base):
                    for sl_i, (ot, tcn, ps, mm_i, _) in enumerate(batch):
                        ve.wait_ge(sA.h, a_base + sl_i + 1)
                        sl = slice(ot * T + tcn * 512, ot * T + tcn * 512 + 512)
                        nc.vector.scalar_tensor_tensor(X[:, sl], G1B[:, sl], SCB[:, 4:5],
                                                       X[:, sl], ALU.mult, ALU.add)
                        sV.inc(nc.vector.scalar_tensor_tensor(X[:, sl], G2B[:, sl_i * 512:(sl_i + 1) * 512],
                                                              SCB[:, 5:6],
                                                              X[:, sl], ALU.mult, ALU.add))
                cmb_done[0] = sV.n

                if out_dma:
                    @block.sync
                    def _(sy, batch=list(batch), vmark=cmb_done[0]):
                        sy.wait_ge(sV.h, vmark)
                        for ot in sorted(set(b[0] for b in batch)):
                            sS.dma(sy.dma_start(
                                out=d_out.rearrange("(ct p) t -> p ct t", p=128)[:, ot],
                                in_=X[:, ot * T:(ot + 1) * T]))

            cmb_done = [sV.n]
            emit_proj(1, lambda c, tcn: XLN[:, c * T + tcn * 512: c * T + tcn * 512 + 512],
                      list(range(CT)), epi2, sV, epi_batch=8)

    def emit_attn_b(XLN, xs, ln_done, Q13, K13):
        s_stage_g = []
        pqz = [0]
        with ExitStack() as ph:
            BNC = [ph.enter_context(nc.sbuf_tensor(f"bBN{i}", [128, 512], BF16)) for i in range(4)]
            bnc_free = {}
            bnc_use = [0]
            for g in range(G4):
                load_weight(g % 2, d_wqkvw[:, g * D:(g + 1) * D])
                if g == 0:
                    @block.gpsimd
                    def _(gp, w=[(s, s.n) for s in ALL_SEMS]):
                        for s, n in w:
                            if n:
                                gp.wait_ge(s.h, n)
                        gp.memset(Q13[:], 0.0)
                        sP.inc(gp.memset(K13[:], 0.0))
                    pqz[0] = sP.n

                def epi(ot, tcn, ps, mm_i, g=g):
                    u = bnc_use[0]
                    bnc_use[0] += 1
                    bn = BNC[u % 4]

                    @block.vector
                    def _(ve, ot=ot, tcn=tcn, ps=ps, mm_i=mm_i, bn=bn, u=u):
                        ve.wait_ge(sMM.h, mm_i)
                        if u >= 4:
                            ve.wait_ge(sS.h, bnc_free[u - 4])
                        sV.inc(nc.vector.tensor_copy(bn[:], ps[:]))
                    v_i = sV.n

                    @block.sync
                    def _(sy, ot=ot, tcn=tcn, bn=bn, v_i=v_i, g=g):
                        sy.wait_ge(sV.h, v_i)
                        sS.dma(sy.dma_start(
                            out=d_stage[g, ot * 128:(ot + 1) * 128, tcn * 512:(tcn + 1) * 512],
                            in_=bn[:]))
                    bnc_free[bnc_use[0] - 1] = sS.n

                def twait(te, g=g):
                    te.wait_ge(sV.h, ln_done)
                    if g == 0:
                        te.wait_ge(sA.h, sA.n)
                emit_proj(g % 2, lambda c, tcn: XLN[:, c * T + tcn * 512: c * T + tcn * 512 + 512],
                          list(range(CT)), epi, sV, tensor_wait=twait)
                s_stage_g.append(sS.n)
            s_stage_all = sS.n
        xs.close()
        b1_end = sMM.n
        b1_v_end = sV.n
        p_qk_zero = pqz[0]

        load_weight(0, d_wout)

        with ExitStack() as ph:
            V17 = ph.enter_context(nc.sbuf_tensor("bV17", [128, 8192], BF16))
            W18 = ph.enter_context(nc.sbuf_tensor("bW18", [128, 8192], BF16))
            ESB = ph.enter_context(nc.sbuf_tensor("bESB", [128, 2048], BF16))
            ZS = ph.enter_context(nc.sbuf_tensor("bZS", [128, 128], F32))
            RZS = ph.enter_context(nc.sbuf_tensor("bRZS", [128, 128], F32))
            PACK = ph.enter_context(nc.sbuf_tensor("bPACK", [128, 4096], BF16))
            PTB = ph.enter_context(nc.sbuf_tensor("bPTB", [128, 4096], BF16))
            ZG = ph.enter_context(nc.sbuf_tensor("bZG", [128, 8192], BF16))
            PCS = [ph.enter_context(nc.sbuf_tensor(f"bPC{i}", [128, 2048], BF16))
                   for i in range(2)]

            st4s = [d_stage[g].rearrange("(cc hd) (h p) -> cc hd h p", hd=16, p=128)
                    for g in range(G4)]
            r_qk = {}
            r_vw = {}
            r_v17 = {}
            mm_p1_done = {}
            a_exp_last = {}
            rdv_last = {}
            ev_gp_last = {}
            zdma_at = []
            g_mm_done = {}
            g_evac_done = {}
            pvz = [0]

            def emit_wave_a(g):
                @block.scalar
                def _(sc, g=g):
                    sc.wait_ge(sP.h, p_qk_zero)
                    sc.wait_ge(sS.h, s_stage_g[g])
                    if g >= 1:
                        sc.wait_ge(sMM.h, mm_p1_done[g - 1])
                    st4 = st4s[g]
                    for j in range(4):
                        sR.dma(sc.dma_start(
                            out=Q13[32 * j:32 * j + 16, :].rearrange("hd (m h p) -> hd m h p", m=4, h=16),
                            in_=st4[j:16:4].transpose([1, 0, 2, 3])))
                        sR.dma(sc.dma_start(
                            out=K13[32 * j:32 * j + 16, :].rearrange("hd (m h p) -> hd m h p", m=4, h=16),
                            in_=st4[16 + j:32:4].transpose([1, 0, 2, 3])))
                r_qk[g] = sR.n

            def emit_wave_b(g):
                @block.scalar
                def _(sc, g=g):
                    sc.wait_ge(sP.h, pvz[0])
                    sc.wait_ge(sS.h, s_stage_g[g])
                    if g >= 1:
                        sc.wait_ge(sMM.h, g_mm_done[g - 1])
                    st4 = st4s[g]
                    for m in range(4):
                        sRc.dma(sc.dma_start(
                            out=V17[32 * m:32 * m + 16, :].rearrange("h (j hd p) -> h j hd p", j=4, hd=16),
                            in_=st4[32 + 4 * m:32 + 4 * m + 4].transpose([2, 0, 1, 3])))
                r_v17[g] = sRc.n

                @block.gpsimd
                def _(gp, g=g):
                    gp.wait_ge(sP.h, pvz[0])
                    gp.wait_ge(sS.h, s_stage_g[g])
                    if g >= 1:
                        gp.wait_ge(sMM.h, g_mm_done[g - 1])
                        gp.wait_ge(sP.h, g_evac_done[g - 1])
                    st4 = st4s[g]
                    for j in range(4):
                        sRb.dma(gp.dma_start(
                            out=W18[32 * j:32 * j + 16, :].rearrange("hd (m h p) -> hd m h p", m=4, h=16),
                            in_=st4[48 + j:64:4].transpose([1, 0, 2, 3])))
                r_vw[g] = sRb.n

            emit_wave_a(0)

            @block.gpsimd
            def _(gp, w=[(s, s.n) for s in ALL_SEMS]):
                for s, n in w:
                    if n:
                        gp.wait_ge(s.h, n)
                gp.memset(V17[:], 0.0)
                gp.memset(W18[:], 0.0)
                sP.inc(gp.memset(PACK[:], 0.0))
            pvz[0] = sP.n
            p_vw_zero = pvz[0]

            emit_wave_b(0)

            for g in range(G4):

                with ExitStack() as pp:
                    P1 = [pp.enter_context(nc.psum_tensor(f"bP1{g}_{j}", [128, 512], F32))
                          for j in range(4)]
                    P2 = [pp.enter_context(nc.psum_tensor(f"bP2{g}_{m}", [128, 512], F32))
                          for m in range(4)]
                    r_done_v = {}
                    cp_at = {}
                    ev_gp = {}
                    r_exp = {}
                    r_ev = {}
                    mm_p2_at = {}
                    for r in range(4):
                        @block.tensor
                        def _(te, g=g, r=r):
                            if r == 0:
                                te.wait_ge(sR.h, r_qk[g])
                                if g == 0:
                                    te.wait_ge(sMM.h, b1_end)
                                    te.wait_ge(sV.h, b1_v_end)
                                else:
                                    te.wait_ge(sA.h, a_exp_last[g - 1])
                            if r >= 1:
                                te.wait_ge(sA.h, r_exp[r - 1])
                            for beta in range(32):
                                p = 32 * r + beta
                                for cc in range(16):
                                    j, m = cc % 4, cc // 4
                                    tp = (32 * j, 32 * m) if (j == 3 or m == 3) else None
                                    i = nc.tensor.matmul(
                                        P1[j][32 * m:32 * m + 16, 16 * beta:16 * beta + 16],
                                        Q13[32 * j:32 * j + 32,
                                            m * 2048 + p: m * 2048 + p + 15 * 128 + 1: 128],
                                        K13[32 * j:32 * j + 32,
                                            m * 2048 + p: m * 2048 + p + 15 * 128 + 1: 128],
                                        start=True, stop=True, tile_position=tp,
                                        skip_group_check=True)
                            sMM.inc(i)
                        mm_p1 = sMM.n
                        if r == 3:
                            mm_p1_done[g] = sMM.n

                        @block.scalar
                        def _(sc, g=g, r=r, mm_p1=mm_p1):
                            sc.wait_ge(sMM.h, mm_p1)
                            if r >= 1:
                                sc.wait_ge(sV.h, r_done_v[r - 1])
                            if r == 0 and g >= 1:
                                sc.wait_ge(sV.h, rdv_last[g - 1])
                            if r == 0 and g == 0:
                                sc.wait_ge(sS.h, s_stage_all)
                            for j in range(4):
                                i = nc.scalar.activation(ESB[:, j * 512:(j + 1) * 512],
                                                         P1[j][:], AF.Exp, scale=0.25)
                            sA.inc(i)
                        r_exp[r] = sA.n

                        @block.vector
                        def _(ve, g=g, r=r, a_e=sA.n):
                            ve.wait_ge(sA.h, a_e)
                            if r >= 1:
                                ve.wait_ge(sMM.h, mm_p2_at[r - 1])
                            if r == 0 and g >= 1:
                                ve.wait_ge(sMM.h, g_mm_done[g - 1])
                            if r == 0 and g == 0:
                                ve.wait_ge(sP.h, p_vw_zero)
                                ve.wait_ge(sS.h, s_stage_all)
                            for j in range(4):
                                i = nc.vector.tensor_reduce(
                                    ZS[:, j * 32:(j + 1) * 32],
                                    ESB[:, j * 512:(j + 1) * 512].rearrange("q (b h) -> q b h", h=16),
                                    AX.X, ALU.add)
                            vsync(ve, i)
                            vsync(ve, nc.vector.reciprocal(RZS[:], ZS[:]))
                            for j in range(4):
                                i = nc.vector.tensor_tensor(
                                    PACK.rearrange("q (bq bl x) -> q bq bl x", bq=8, bl=4)[:, :, :, 32 * j:32 * j + 16],
                                    ESB[:, j * 512:(j + 1) * 512].rearrange("q (bq bl h) -> q bq bl h", bq=8, bl=4),
                                    RZS[:, j * 32:(j + 1) * 32].rearrange("q (bq bl) -> q bq bl", bq=8)
                                    .unsqueeze(3).to_broadcast([128, 8, 4, 16]),
                                    ALU.mult)
                            vsync(ve, i)
                            for pk in range(8):
                                i = nc.vector.transpose(PTB[:, pk * 512:(pk + 1) * 512],
                                                        PACK[:, pk * 512:(pk + 1) * 512])
                            sV.inc(i)
                        r_done_v[r] = sV.n
                        v_pt = sV.n

                        @block.tensor
                        def _(te, g=g, r=r, v_pt=v_pt):
                            te.wait_ge(sV.h, v_pt)
                            if r == 0:
                                te.wait_ge(sRb.h, r_vw[g])
                                te.wait_ge(sRc.h, r_v17[g])
                            if r >= 1:
                                te.wait_ge(sA.h, cp_at[r - 1])
                            for beta in range(32):
                                p = 32 * r + beta
                                pk, bl = beta // 4, beta % 4
                                for cc in range(16):
                                    j, m = cc % 4, cc // 4
                                    tp = (32 * m, 32 * j) if (j == 3 or m == 3) else None
                                    i = nc.tensor.matmul(
                                        P2[m][32 * j:32 * j + 16, 16 * beta:16 * beta + 16],
                                        V17[32 * m:32 * m + 32,
                                            j * 2048 + p: j * 2048 + p + 15 * 128 + 1: 128],
                                        PTB[32 * m:32 * m + 32,
                                            pk * 512 + 128 * bl + 32 * j: pk * 512 + 128 * bl + 32 * j + 16],
                                        start=True, stop=True, tile_position=tp,
                                        skip_group_check=True)
                            sMM.inc(i)
                        mm_p2_at[r] = sMM.n
                        mm_p2 = sMM.n

                        @block.scalar
                        def _(sc, g=g, r=r, mm_p2=mm_p2):
                            sc.wait_ge(sMM.h, mm_p2)
                            if r >= 2:
                                sc.wait_ge(sP.h, ev_gp[r - 2])
                            if r < 2 and g >= 1:
                                sc.wait_ge(sP.h, ev_gp_last[g - 1])
                            for m in range(4):
                                i = nc.scalar.activation(PCS[r % 2][:, m * 512:(m + 1) * 512],
                                                         P2[m][:], AF.Copy)
                            sA.inc(i)
                        cp_at[r] = sA.n

                        @block.gpsimd
                        def _(gp, g=g, r=r, cpa=sA.n):
                            gp.wait_ge(sA.h, cpa)
                            if r == 0 and g >= 1:
                                gp.wait_ge(sS.h, zdma_at[g - 1])
                            for m in range(4):
                                outv = (ZG.rearrange("q (bl sm h w) -> q bl sm h w",
                                                     bl=4, sm=4, h=16)[:, :, m, :, 8 * r:8 * r + 8]
                                        .rearrange("q bl h bq -> q bq bl h"))
                                in0 = PCS[r % 2][:, m * 512:(m + 1) * 512].rearrange(
                                    "q (bq bl h) -> q bq bl h", bq=8, bl=4)
                                in1 = (W18.rearrange("q (mm h p) -> q mm h p", mm=4, h=16)
                                       [:, m, :, 32 * r:32 * r + 32]
                                       .rearrange("q h (bq bl) -> q bq bl h", bq=8))
                                i = nc.gpsimd.tensor_tensor(outv, in0, in1, ALU.mult)
                            sP.inc(i)
                        ev_gp[r] = sP.n
                    g_mm_done[g] = sMM.n
                    g_evac_done[g] = sP.n
                    a_exp_last[g] = r_exp[3]
                    rdv_last[g] = r_done_v[3]
                    ev_gp_last[g] = ev_gp[3]
                if g + 1 < G4:
                    emit_wave_a(g + 1)
                    emit_wave_b(g + 1)
                p_ev = sP.n
                zhalf = []

                d_cin4 = d_cin.rearrange("(gg sl j p) c -> gg sl j p c", gg=G4, sl=16, j=4)

                @block.sync
                def _(sy, g=g, p_ev=p_ev):
                    sy.wait_ge(sP.h, p_ev)
                    for hf in range(2):
                        for j in range(4):
                            sS.dma(sy.dma_start(
                                out=d_cin4[g, hf * 8:(hf + 1) * 8, j].transpose([1, 0, 2]),
                                in_=ZG[32 * j:32 * j + 16, hf * 4096:(hf + 1) * 4096]
                                .rearrange("p (sl c) -> p sl c", c=512)))
                        if hf == 0:
                            zhalf.append(sS.n)
                zdma_at.append(sS.n)

                for hf in range(2):
                    @block.gpsimd
                    def _(gp, g=g, hf=hf, zat=(zhalf[0] if hf == 0 else zdma_at[g])):
                        gp.wait_ge(sS.h, zat)
                        sC.inc(nc.gpsimd.collective_compute(
                            "AllGather", ALU.bypass,
                            replica_groups=[list(range(8))],
                            ins=[d_cin[g * D + hf * 512: g * D + hf * 512 + 512, :].opt()],
                            outs=[d_cout[g * 8 * D + hf * 4096: g * 8 * D + hf * 4096 + 4096, :].opt()]))
        c_done = sC.n

        with ExitStack() as ph:
            BIX = ph.enter_context(nc.sbuf_tensor("bBIX", [128, 32], I32))
            BNDs = [ph.enter_context(nc.sbuf_tensor(f"bBND{i}", [128, 8 * 512], BF16))
                    for i in range(2)]

            @block.sync
            def _(sy):
                sS.dma(sy.dma_start(out=BIX[:], in_=d_bidx[:]))
            s_bix = sS.n

            band_loaded = {}
            band_mm = {}
            wo_epi = {}
            with ExitStack() as pp:
                pws = [pp.enter_context(nc.psum_tensor(f"bwo{i}", [128, 512], F32))
                       for i in range(8)]
                for jj in range(4):
                    @block.gpsimd
                    def _(gp, jj=jj, drain=sG.n):
                        gp.wait_ge(sC.h, c_done - 1)
                        gp.wait_ge(sS.h, s_bix)
                        gp.wait_ge(sG.h, drain)
                        if jj >= 2:
                            gp.wait_ge(sMM.h, band_mm[jj - 2])
                        for kk in range(4):
                            sG.dma(nc.gpsimd.indirect_dma_start(
                                out=BNDs[jj % 2][:, kk * 512:(kk + 1) * 512],
                                out_offset=None,
                                in_=d_cout[:],
                                in_offset=IndirectOffsetOnAxis(BIX[:, jj * 8 + kk:jj * 8 + kk + 1], 0)))
                    band_a = sG.n

                    @block.gpsimd
                    def _(gp, jj=jj, ba=band_a):
                        gp.wait_ge(sC.h, c_done)
                        gp.wait_ge(sG.h, ba)
                        for kk in range(4, 8):
                            sG.dma(nc.gpsimd.indirect_dma_start(
                                out=BNDs[jj % 2][:, kk * 512:(kk + 1) * 512],
                                out_offset=None,
                                in_=d_cout[:],
                                in_offset=IndirectOffsetOnAxis(BIX[:, jj * 8 + kk:jj * 8 + kk + 1], 0)))
                    band_loaded[jj] = sG.n

                    for dt in range(CT):
                        ps = pws[dt]

                        @block.tensor
                        def _(te, jj=jj, dt=dt, ps=ps, ba=band_a):
                            if dt == 0:
                                te.wait_ge(sG.h, ba)
                                te.wait_ge(sG.h, wb_loaded[0])
                            if jj >= 1 and dt == 0:
                                te.wait_ge(sV.h, wo_epi[jj - 1])
                            for kk in range(8):
                                if dt == 0 and kk == 4:
                                    te.wait_ge(sG.h, band_loaded[jj])
                                i = nc.tensor.matmul(
                                    ps[:],
                                    WB[0][:, kk * D + dt * 128: kk * D + dt * 128 + 128],
                                    BNDs[jj % 2][:, kk * 512:(kk + 1) * 512],
                                    start=(kk == 0), stop=(kk == 7))
                            sMM.inc(i)
                        mm_i = sMM.n

                        @block.vector
                        def _(ve, jj=jj, dt=dt, ps=ps, mm_i=mm_i):
                            ve.wait_ge(sMM.h, mm_i)
                            sl = slice(dt * T + jj * 512, dt * T + jj * 512 + 512)
                            sV.inc(nc.vector.tensor_tensor(X[:, sl], X[:, sl], ps[:], ALU.add))
                    wo_epi[jj] = sV.n
                    band_mm[jj] = sMM.n
            wb_free[0] = sMM.n

    with nc.sbuf_tensor("XLN1", [128, CT * T], BF16) as XLN:
        norm1 = emit_ln(XLN)
        emit_attn_a(XLN, norm1)
    if upto >= 2:
        with nc.sbuf_tensor("XLN2", [128, CT * T], BF16) as XLN:
            emit_ln(XLN)
            emit_ffn(XLN)
    if upto >= 3:
        qk = ExitStack()
        Q13 = qk.enter_context(nc.sbuf_tensor("bQ13", [128, 8192], BF16))
        K13 = qk.enter_context(nc.sbuf_tensor("bK13", [128, 8192], BF16))
        xs = ExitStack()
        XLN = xs.enter_context(nc.sbuf_tensor("XLN3", [128, CT * T], BF16))
        emit_ln(XLN)
        emit_attn_b(XLN, xs, sV.n, Q13, K13)
        qk.close()
    if upto >= 4:
        with nc.sbuf_tensor("XLN4", [128, CT * T], BF16) as XLN:
            emit_ln(XLN)
            emit_ffn(XLN, out_dma=True)

    @block.sync
    def _(sy):
        for s in ALL_SEMS:
            if s.n:
                sy.wait_ge(s.h, s.n)
        if upto < 4:
            sS.dma(sy.dma_start(out=d_out.rearrange("(ct p) t -> p ct t", p=128),
                                in_=X.rearrange("p (ct t) -> p ct t", ct=CT)))
        sy.wait_ge(sS.h, sS.n)

    es.close()
    return nc


def _mask_tiles():
    tiles = np.zeros((max(1, NPART), 128, 512), dtype=np.float32)
    for i, (jt, tc) in enumerate(PARTIALS):
        j = 128 * jt + np.arange(128)[:, None]
        t = 512 * tc + np.arange(512)[None, :]
        tiles[i] = (3.0 * j <= t).astype(np.float32)
    return tiles.reshape(-1, 512).astype(ml_dtypes.bfloat16)


def _wout_perm():
    perm = np.zeros(D, dtype=np.int64)
    for pm4 in range(4):
        for m in range(4):
            for j in range(4):
                for hd in range(16):
                    row = (4 * pm4 + m) * 64 + j * 16 + hd
                    cc = 4 * m + j
                    perm[row] = 256 * pm4 + 16 * cc + hd
    return perm


def _band_idx(core):
    idx = np.zeros((32, 128), dtype=np.int32)
    for jj in range(4):
        s_ = 4 * (core % 2) + jj
        g = core // 2
        for kk in range(8):
            row = kk * 128 + np.arange(128)
            idx[jj * 8 + kk] = g * (8 * D) + (row // 512) * 4096 + s_ * 512 + row % 512
    return np.ascontiguousarray(idx.T)


def kernel(L0_attn_a_x, L0_attn_a_Wq, L0_attn_a_Wk, L0_attn_a_Wv, L0_attn_a_Wo,
           L0_attn_a_bo, L0_attn_a_Wconv, L0_attn_b_w_qkvw, L0_attn_b_w_out,
           L0_ffn_gate_w, L0_ffn_gate_b, L0_ffn_expert_w, L0_ffn_expert_b):
    f32 = np.float32
    x = np.asarray(L0_attn_a_x, f32)
    com = {
        "wqT": np.ascontiguousarray(np.asarray(L0_attn_a_Wq, f32).T),
        "wkT": np.ascontiguousarray(np.asarray(L0_attn_a_Wk, f32).T),
        "wvT": np.ascontiguousarray(np.asarray(L0_attn_a_Wv, f32).T),
        "woT": np.ascontiguousarray(np.asarray(L0_attn_a_Wo, f32).T),
        "bo128": np.ascontiguousarray(np.asarray(L0_attn_a_bo, f32).reshape(CT, 128).T),
        "wconvT": np.ascontiguousarray(np.asarray(L0_attn_a_Wconv, f32).transpose(2, 1, 0)),
        "wqkvw": np.ascontiguousarray(np.asarray(L0_attn_b_w_qkvw, f32)),
        "woutS": np.ascontiguousarray(np.asarray(L0_attn_b_w_out, f32)[_wout_perm()]),
        "gw128": np.ascontiguousarray(
            np.asarray(L0_ffn_gate_w, f32).T.reshape(CT, 128, 8).transpose(1, 0, 2).reshape(128, 8 * CT)),
        "gb": np.asarray(L0_ffn_gate_b, f32).reshape(1, 8),
        "ewT": np.ascontiguousarray(
            np.asarray(L0_ffn_expert_w, f32).transpose(0, 2, 1).reshape(8 * D, D)),
        "ebP": np.ascontiguousarray(
            np.asarray(L0_ffn_expert_b, f32).reshape(8, CT, 128).transpose(0, 2, 1).reshape(8 * 128, CT)),
        "maskP": _mask_tiles(),
    }
    nc = bass.Bass("TRN2", target_bir_lowering=False, debug=False)
    build_program(nc, upto=UPTO)
    in_maps = []
    for c in range(8):
        m = dict(com)
        m["xT"] = np.ascontiguousarray(x[c].T)
        m["bandidx"] = _band_idx(c)
        in_maps.append(m)
    res = run_bass_kernel_spmd(nc, in_maps, core_ids=list(range(8)))
    out = np.stack([r["outT"] for r in res.results])
    return np.ascontiguousarray(out.transpose(0, 2, 1))
